# version 03
# baseline (speedup 1.0000x reference)
import math
from bisect import bisect_left
from contextlib import ExitStack

import numpy as np
import concourse.bass as bass
import concourse.mybir as mybir
from concourse.bass_utils import run_bass_kernel_spmd

F32 = mybir.dt.float32
BF16 = mybir.dt.bfloat16
I32 = mybir.dt.int32
AF = mybir.ActivationFunctionType
ALU = mybir.AluOpType
AX = mybir.AxisListType

T = 4096
D = 1024
NCORE = 8

ENGINES = ("tensor", "vector", "scalar", "gpsimd", "sync")



class Sem:
    __slots__ = ("h", "name", "is_dma", "inc_idx", "inc_ops", "sid")
    _next = [0]

    def __init__(self, h, name, is_dma):
        self.h = h
        self.name = name
        self.is_dma = is_dma
        self.inc_idx = []
        self.inc_ops = []
        self.sid = Sem._next[0]
        Sem._next[0] += 1


class Op:
    __slots__ = ("eng", "idx", "fn", "waits", "incs", "reads", "writes",
                 "is_dma", "vc", "tag", "sem", "sem_pos")

    def __init__(self, eng, idx, fn, reads, writes, tag):
        self.eng = eng
        self.idx = idx
        self.fn = fn
        self.waits = []
        self.incs = []
        self.reads = reads
        self.writes = writes
        self.is_dma = False
        self.vc = None
        self.tag = tag
        self.sem = None
        self.sem_pos = -1


def _conflict(a, b):
    if a[0] != b[0]:
        return False
    return a[1] is None or b[1] is None or a[1] == b[1]


class Prog:
    def __init__(self, nc, stack, check=True):
        self.nc = nc
        self.stack = stack
        self.check = check
        self.ops = {e: [] for e in ENGINES}
        self.eng_sem = {e: self.new_sem("sem_" + e, is_dma=False) for e in ENGINES}
        self.dma_sems = []
        self.writers = {}
        self.readers = {}
        self.n_ops = 0
        self._dma_pool = []
        self._dma_pool_idx = 0
        self._pending_spoke = {}

    def new_sem(self, name, is_dma=False):
        h = self.stack.enter_context(self.nc.semaphore(name))
        s = Sem(h, name, is_dma)
        if is_dma:
            self.dma_sems.append(s)
        return s

    def dma_sem(self, name):
        if self._dma_pool_idx < len(self._dma_pool):
            s = self._dma_pool[self._dma_pool_idx]
            self._dma_pool_idx += 1
            return s
        s = self.new_sem(name, is_dma=True)
        self._dma_pool.append(s)
        self._dma_pool_idx += 1
        return s

    @staticmethod
    def _join(vc, other):
        for k, v in other.items():
            if vc.get(k, -1) < v:
                vc[k] = v

    @staticmethod
    def _done_key(op):
        if op.is_dma:
            return (op.sem.sid, op.sem_pos)
        return (op.eng, op.idx)

    def _happened(self, vc, op):
        k, v = self._done_key(op)
        return vc.get(k, -1) >= v

    def do(self, eng, fn, reads=(), writes=(), tag="", _dma_sem=None):
        ops = self.ops[eng]
        op = Op(eng, len(ops), fn, list(reads), list(writes), tag)
        vc = {}
        if ops:
            prev = ops[-1]
            vc = dict(prev.vc)
            if not prev.is_dma:
                vc[eng] = prev.idx
        op.vc = vc
        spoke = self._pending_spoke.pop(eng, None)
        if spoke is not None:
            self._add_wait(op, spoke)

        if _dma_sem is not None:
            op.is_dma = True
            op.sem = _dma_sem
            op.sem_pos = len(_dma_sem.inc_ops)

        needed = {}
        for r in op.reads:
            for w in self._find(self.writers, r):
                if not self._happened(vc, w):
                    needed[id(w)] = w
        for wreg in op.writes:
            for w in self._find(self.writers, wreg):
                if not self._happened(vc, w):
                    needed[id(w)] = w
            for rd in self._find(self.readers, wreg):
                if rd is not op and not self._happened(vc, rd):
                    needed[id(rd)] = rd
        for dep in needed.values():
            self._add_wait(op, dep)

        if op.is_dma:
            if self.check:
                for e in _dma_sem.inc_ops:
                    if not self._happened(op.vc, e):
                        raise RuntimeError(
                            f"DMA sem {_dma_sem.name} not serialized: "
                            f"'{tag}' not after '{e.tag}'")
            _dma_sem.inc_ops.append(op)
            op.incs.append(_dma_sem)

        for r in op.reads:
            self._commit(self.readers, r, op)
        for wreg in op.writes:
            self._commit(self.writers, wreg, op)
            rl = self.readers.get(wreg[0])
            if rl:
                rl[:] = [(reg, o) for (reg, o) in rl
                         if not ((reg == wreg or wreg[1] is None)
                                 and self._happened(op.vc, o))]
        ops.append(op)
        self.n_ops += 1
        return op

    def _commit(self, table, region, op):
        lst = table.setdefault(region[0], [])
        lst[:] = [(reg, o) for (reg, o) in lst
                  if not ((reg == region or region[1] is None)
                          and self._happened(op.vc, o))]
        lst.append((region, op))

    @staticmethod
    def _find(table, region):
        out = []
        for reg, o in table.get(region[0], ()):
            if _conflict(reg, region):
                out.append(o)
        return out

    def _add_wait(self, op, dep):
        if dep.is_dma:
            sem = dep.sem
            val = 16 * (dep.sem_pos + 1)
            self._join(op.vc, dep.vc)
            if op.vc.get(sem.sid, -1) < dep.sem_pos:
                op.vc[sem.sid] = dep.sem_pos
        else:
            sem = self.eng_sem[dep.eng]
            pos = bisect_left(sem.inc_idx, dep.idx)
            if pos == len(sem.inc_idx):
                dep.incs.append(sem)
                sem.inc_idx.append(dep.idx)
            val = pos + 1
            self._join(op.vc, dep.vc)
            if op.vc.get(dep.eng, -1) < dep.idx:
                op.vc[dep.eng] = dep.idx
        op.waits.append((sem, val))

    def dma(self, queue, out, in_, sem, reads=(), writes=(), tag=""):
        def fn(eng, out=out, in_=in_):
            return eng.dma_start(out=out, in_=in_)
        return self.do(queue, fn, reads=reads, writes=writes, tag=tag,
                       _dma_sem=sem)

    def claim(self, name, tag="claim"):
        return self.do("gpsimd", lambda e: e.nop(), writes=[(name, None)], tag=tag)

    def barrier(self, tag="barrier"):
        if not hasattr(self, "_bar_counts"):
            self._bar_counts = {}
        toks = {}
        for e in ENGINES:
            if e in ("vector", "sync"):
                continue
            for o in reversed(self.ops[e]):
                if not o.is_dma:
                    toks[e] = o
                    break
        changed = [s for s in self.dma_sems
                   if s.inc_ops and self._bar_counts.get(s.sid) != len(s.inc_ops)]
        for s in changed:
            self._bar_counts[s.sid] = len(s.inc_ops)
        hub = self.do("vector", lambda en: en.nop(), tag=f"{tag}/hub")
        for tk in toks.values():
            self._add_wait(hub, tk)
        for s in changed:
            self._add_wait(hub, s.inc_ops[-1])
        for e in ENGINES:
            if e != "vector":
                self._pending_spoke[e] = hub
        self.writers.clear()
        self.readers.clear()

    def finalize(self):
        blk = self.stack.enter_context(self.nc.Block())

        def run_stream(engine_obj, ops):
            for op in ops:
                tgt = {}
                for sem, v in op.waits:
                    cur = tgt.get(sem.sid)
                    if cur is None or cur[1] < v:
                        tgt[sem.sid] = (sem, v)
                for sem, v in tgt.values():
                    engine_obj.wait_ge(sem.h, v)
                ins = op.fn(engine_obj)
                for sem in op.incs:
                    assert ins is not None, f"op {op.tag} incs but no instr"
                    ins.then_inc(sem.h, 16 if sem.is_dma else 1)

        for eng in ENGINES:
            ops = self.ops[eng]
            getattr(blk, eng)(lambda e, ops=ops: run_stream(e, ops))
        return blk


class TBuf:
    __slots__ = ("ap", "name")

    def __init__(self, ap, name):
        self.ap = ap
        self.name = name

    def __getitem__(self, sl):
        return self.ap[sl]

    def r(self, key=None):
        return (self.name, key)


class Ctx:

    def __init__(self, nc, stack, check=True):
        self.nc = nc
        self.stack = stack
        self.p = Prog(nc, stack, check=check)
        self._consts = {}
        self._const_seq = 0
        self._phase_stack = None
        self._phase_names = []
        self.banks = {}
        self._sb_names = set()
        for i in range(7):
            self.banks[i] = self.psum(f"bank{i}", [128, 512], F32)
        self.bankT = self.psum("bankT", [128, 1024], BF16)
        self._setup = False

    def sb(self, name, shape, dtype, outer=False):
        assert name not in self._sb_names, name
        self._sb_names.add(name)
        st = self.stack if (outer or self._phase_stack is None) else self._phase_stack
        if not outer and self._phase_stack is not None:
            self._phase_names.append(name)
        ap = st.enter_context(self.nc.sbuf_tensor(name, shape, dtype))
        return TBuf(ap, name)

    def push_phase(self):
        self.p._dma_pool_idx = 0
        assert self._phase_stack is None
        self._phase_stack = ExitStack()
        self._phase_stack.__enter__()
        self._phase_names = []
        self._ohots = {}

    def pop_phase(self):
        self._phase_stack.__exit__(None, None, None)
        self._phase_stack = None
        for n in self._phase_names:
            self._sb_names.discard(n)
        self._phase_names = []

    def psum(self, name, shape, dtype):
        ap = self.stack.enter_context(self.nc.psum_tensor(name, shape, dtype))
        return TBuf(ap, name)

    def setup_consts(self):
        p = self.p
        self.ones_col = self.sb("c_ones_col", [128, 1], F32, outer=True)
        self.ones_col_bf = self.sb("c_ones_col_bf", [128, 1], BF16, outer=True)
        self.ones_row = self.sb("c_ones_row", [1, 128], F32, outer=True)
        self.ones_row_bf = self.sb("c_ones_row_bf", [1, 128], BF16, outer=True)
        self.idn = self.sb("c_idn", [128, 128], F32, outer=True)
        self.idn_bf = self.sb("c_idn_bf", [128, 128], BF16, outer=True)
        g = "gpsimd"
        for t in (self.ones_col, self.ones_col_bf, self.ones_row, self.ones_row_bf):
            p.do(g, lambda e, t=t: e.memset(t[:], 1.0), writes=[t.r()], tag="const")
        for t in (self.idn, self.idn_bf):
            p.do(g, lambda e, t=t: e.memset(t[:], 1.0), writes=[t.r()], tag="const")
            p.do(g, lambda e, t=t: e.affine_select(
                out=t[:], in_=t[:], compare_op=ALU.is_equal, fill=0.0,
                base=0, pattern=[[-1, 128]], channel_multiplier=1),
                reads=[t.r()], writes=[t.r()], tag="const")
        self._ohots = {}
        self._setup = True

    def get_ohot(self, dtype):
        if dtype not in self._ohots:
            self._const_seq += 1
            t = self.sb(f"c_ohot_{dtype}_{self._const_seq}", [32, 4096], dtype)
            self.p.do("gpsimd", lambda e: e.memset(t[:], 1.0),
                      writes=[t.r()], tag="const_ohot")
            self.p.do("gpsimd", lambda e: e.affine_select(
                out=t[:], in_=t[:], compare_op=ALU.is_equal,
                fill=0.0, base=0, pattern=[[-1, 32], [0, 128]], channel_multiplier=1),
                reads=[t.r()], writes=[t.r()], tag="const_ohot")
            self._ohots[dtype] = t
        return self._ohots[dtype]

    def constcol(self, value):
        key = float(value)
        if key not in self._consts:
            self._const_seq += 1
            t = self.sb(f"c_col{self._const_seq}", [128, 1], F32, outer=True)
            self.p.do("gpsimd", lambda e, t=t, v=key: e.memset(t[:], v),
                      writes=[t.r()], tag="constcol")
            self._consts[key] = t
        return self._consts[key]



def load_cols(ctx, name, dram_ap, n128, ncols, sem_unused, bank, dtype=F32):
    p = ctx.p
    sem = p.dma_sem(name + "_ld")
    rows = ctx.sb(name + "_rows", [ncols, 128], dtype)
    cols = ctx.sb(name, [128, ncols], dtype)
    p.dma("sync", rows[:], dram_ap, sem, writes=[rows.r()], tag=f"ld_{name}")
    idn = ctx.idn if dtype == F32 else ctx.idn_bf
    p.do("tensor", lambda e: e.transpose(bank[:, 0:ncols], rows[:], idn[0:ncols, 0:ncols]),
         reads=[rows.r(), idn.r()], writes=[bank.r()], tag=f"tr_{name}")
    p.do("scalar", lambda e: e.copy(cols[:], bank[:, 0:ncols]),
         reads=[bank.r()], writes=[cols.r()], tag=f"ev_{name}")
    return cols


def emit_ln(ctx, x_sb, x_key, ntile, tc, a_sb, b_sb, sq, rows, banks, tag, eps=1e-5):
    p = ctx.p
    s1, s2, bc = banks["s1"], banks["s2"], banks["bc"]
    for j in range(ntile):
        p.do("tensor", lambda e, j=j: e.matmul(
            s1[0:1, 0:tc], ctx.ones_col[:], x_sb[:, j * tc:(j + 1) * tc],
            start=(j == 0), stop=(j == ntile - 1)),
            reads=[(x_sb.name, (x_key, j)), ctx.ones_col.r()],
            writes=[s1.r()], tag=f"{tag}_s1_{j}")
        p.do("scalar", lambda e, j=j: e.activation(
            sq[:, (j % 2) * tc:(j % 2 + 1) * tc], x_sb[:, j * tc:(j + 1) * tc],
            AF.Square),
            reads=[(x_sb.name, (x_key, j))], writes=[sq.r(j % 2)],
            tag=f"{tag}_sq_{j}")
        p.do("tensor", lambda e, j=j: e.matmul(
            s2[0:1, 0:tc], ctx.ones_col[:], sq[:, (j % 2) * tc:(j % 2 + 1) * tc],
            start=(j == 0), stop=(j == ntile - 1)),
            reads=[sq.r(j % 2), ctx.ones_col.r()],
            writes=[s2.r()], tag=f"{tag}_s2_{j}")
    invd = 1.0 / (ntile * 128)
    p.do("vector", lambda e: e.tensor_scalar_mul(rows[0:1, 0:tc], s1[0:1, 0:tc], invd),
         reads=[s1.r()], writes=[(rows.name, 0)], tag=f"{tag}_mean")
    p.do("vector", lambda e: e.tensor_scalar_mul(rows[0:1, tc:2 * tc], s2[0:1, 0:tc], invd),
         reads=[s2.r()], writes=[(rows.name, 1)], tag=f"{tag}_ms")
    p.do("vector", lambda e: e.scalar_tensor_tensor(
        rows[0:1, 2 * tc:3 * tc], rows[0:1, 0:tc], -1.0, rows[0:1, 0:tc],
        ALU.mult, ALU.mult),
        reads=[(rows.name, 0)], writes=[(rows.name, 2)], tag=f"{tag}_negm2")
    p.do("vector", lambda e: e.tensor_add(
        rows[0:1, 2 * tc:3 * tc], rows[0:1, 2 * tc:3 * tc], rows[0:1, tc:2 * tc]),
        reads=[(rows.name, 2), (rows.name, 1)],
        writes=[(rows.name, 2)], tag=f"{tag}_var")
    ceps = ctx.constcol(eps)
    p.do("scalar", lambda e: e.activation(
        rows[0:1, 2 * tc:3 * tc], rows[0:1, 2 * tc:3 * tc], AF.Sqrt,
        bias=ceps[0:1, 0:1], scale=1.0),
        reads=[(rows.name, 2), ceps.r()], writes=[(rows.name, 2)],
        tag=f"{tag}_sd")
    p.do("vector", lambda e: e.reciprocal(rows[0:1, tc:2 * tc], rows[0:1, 2 * tc:3 * tc]),
         reads=[(rows.name, 2)], writes=[(rows.name, 1)],
         tag=f"{tag}_ri")
    p.do("vector", lambda e: e.scalar_tensor_tensor(
        rows[0:1, 2 * tc:3 * tc], rows[0:1, 0:tc], -1.0, rows[0:1, tc:2 * tc],
        ALU.mult, ALU.mult),
        reads=[(rows.name, 0), (rows.name, 1)],
        writes=[(rows.name, 2)], tag=f"{tag}_brow")
    p.do("tensor", lambda e: e.matmul(
        bc[:, 0:tc], ctx.ones_row[:], rows[0:1, tc:2 * tc], start=True, stop=True),
        reads=[(rows.name, 1), ctx.ones_row.r()], writes=[bc.r()],
        tag=f"{tag}_bca")
    p.do("scalar", lambda e: e.copy(a_sb[:], bc[:, 0:tc]),
         reads=[bc.r()], writes=[a_sb.r()], tag=f"{tag}_eva")
    p.do("tensor", lambda e: e.matmul(
        bc[:, 0:tc], ctx.ones_row[:], rows[0:1, 2 * tc:3 * tc], start=True, stop=True),
        reads=[(rows.name, 2), ctx.ones_row.r()], writes=[bc.r()],
        tag=f"{tag}_bcb")
    p.do("scalar", lambda e: e.copy(b_sb[:], bc[:, 0:tc]),
         reads=[bc.r()], writes=[b_sb.r()], tag=f"{tag}_evb")



def emit_qmlp(ctx, xin, xout, w, pfx, dbg=None):
    p = ctx.p
    TC = 256
    NCH = T // TC
    F = 4096

    w1 = ctx.sb(pfx + "w1", [128, 8 * F], BF16)
    w2 = ctx.sb(pfx + "w2", [128, 32 * D], BF16)
    scl1 = ctx.sb(pfx + "scl1", [8, 2 * 512], F32)
    scl2 = ctx.sb(pfx + "scl2", [32, D], F32)
    h_sb = ctx.sb(pfx + "h", [128, 32 * TC], BF16)
    x_sb = ctx.sb(pfx + "x", [128, 8 * TC], F32)
    y_sb = ctx.sb(pfx + "y", [128, 8 * TC], F32)
    xn_sb = ctx.sb(pfx + "xn", [128, 8 * TC], BF16)
    t2_sb = ctx.sb(pfx + "t2", [128, 2 * TC], F32)
    sq = ctx.sb(pfx + "sq", [128, 2 * TC], F32)
    a1 = ctx.sb(pfx + "a1", [128, TC], F32)
    b1r = ctx.sb(pfx + "b1r", [128, TC], F32)
    a2 = ctx.sb(pfx + "a2", [128, TC], F32)
    b2r = ctx.sb(pfx + "b2r", [128, TC], F32)
    lnrows = ctx.sb(pfx + "lnrows", [1, 3 * TC], F32)

    sem_w = p.dma_sem(pfx + "sw")
    sem_x = p.dma_sem(pfx + "sx")
    sem_out = p.dma_sem(pfx + "so")
    sem_misc = p.dma_sem(pfx + "sm")
    ohot = ctx.get_ohot(F32)

    bank = ctx.banks

    b1c = load_cols(ctx, pfx + "b1c", w["fc1_bias"], 128, 32, sem_misc, bank[6])
    b2c = load_cols(ctx, pfx + "b2c", w["fc2_bias"], 128, 8, sem_misc, bank[6])
    gc = load_cols(ctx, pfx + "gc", w["gamma"], 128, 8, sem_misc, bank[6])
    bc_ = load_cols(ctx, pfx + "bc", w["beta"], 128, 8, sem_misc, bank[6])
    hb2 = ctx.sb(pfx + "hb2", [128, 8], F32)
    p.do("scalar", lambda e: e.mul(hb2[:], b2c[:], 0.5),
         reads=[b2c.r()], writes=[hb2.r()], tag=pfx + "hb2")

    p.dma("sync", scl2[:], w["fc2_scales"], sem_misc, writes=[scl2.r()], tag=pfx + "scl2")

    for dt in range(8):
        p.dma("gpsimd", w1[:, dt * F:(dt + 1) * F],
              w["fc1_q"][dt * 128:(dt + 1) * 128, :], p.dma_sem(f"{pfx}sw1_{dt}"),
              writes=[(w1.name, ("q", dt))], tag=f"{pfx}w1q{dt}")
    sem_s1 = [p.dma_sem(pfx + "ss1a"), p.dma_sem(pfx + "ss1b")]
    for c in range(F // 512):
        cc = c % 2
        p.dma("sync", scl1[:, cc * 512:(cc + 1) * 512],
              w["fc1_scales"][:, c * 512:(c + 1) * 512], sem_s1[cc],
              writes=[(scl1.name, cc)], tag=f"{pfx}scl1_{c}")
        for dt in range(8):
            bsel = bank[4 + (dt % 2)]
            p.do("tensor", lambda e, dt=dt, cc=cc, bsel=bsel: e.matmul(
                bsel[:, 0:512], ohot[0:8, dt * 128:(dt + 1) * 128],
                scl1[:, cc * 512:(cc + 1) * 512], start=True, stop=True),
                reads=[(scl1.name, cc), ohot.r()], writes=[bsel.r()],
                tag=f"{pfx}w1s{dt}_{c}")
            p.do("vector", lambda e, dt=dt, c=c, bsel=bsel: e.scalar_tensor_tensor(
                w1[:, dt * F + c * 512:dt * F + (c + 1) * 512],
                w1[:, dt * F + c * 512:dt * F + (c + 1) * 512],
                -128.0, bsel[:, 0:512], ALU.add, ALU.mult),
                reads=[(w1.name, ("q", dt)), bsel.r()],
                writes=[(w1.name, ("q", dt))], tag=f"{pfx}w1d{dt}_{c}")
    for g8 in range(4):
        p.dma("gpsimd",
              w2[:, g8 * 8 * D:(g8 + 1) * 8 * D].rearrange("p (ft n) -> p ft n", ft=8),
              w["fc2_q"][g8 * 8 * 128:(g8 + 1) * 8 * 128, :].rearrange(
                  "(ft p) n -> p ft n", p=128),
              p.dma_sem(f"{pfx}sw2_{g8}"),
              writes=[(w2.name, ("q", g8))], tag=f"{pfx}w2q{g8}")
    for ft in range(32):
        for c in range(D // 512):
            bsel = bank[4 + (c % 2)]
            p.do("tensor", lambda e, ft=ft, c=c, bsel=bsel: e.matmul(
                bsel[:, 0:512], ohot[0:32, ft * 128:(ft + 1) * 128],
                scl2[:, c * 512:(c + 1) * 512], start=True, stop=True),
                reads=[scl2.r(), ohot.r()], writes=[bsel.r()],
                tag=f"{pfx}w2s{ft}_{c}")
            p.do("vector", lambda e, ft=ft, c=c, bsel=bsel: e.scalar_tensor_tensor(
                w2[:, ft * D + c * 512:ft * D + (c + 1) * 512],
                w2[:, ft * D + c * 512:ft * D + (c + 1) * 512],
                -128.0, bsel[:, 0:512], ALU.add, ALU.mult),
                reads=[(w2.name, ("q", ft // 8)), bsel.r()],
                writes=[(w2.name, ("q", ft // 8))], tag=f"{pfx}w2d{ft}_{c}")

    ln_banks = {"s1": bank[4], "s2": bank[5], "bc": bank[6]}
    for c in range(NCH):
        xsl = xin[:, c * TC:(c + 1) * TC].rearrange("(j p) t -> p j t", p=128)
        p.dma("sync", x_sb[:].rearrange("p (j t) -> p j t", j=8), xsl, sem_x,
              writes=[x_sb.r()], tag=f"{pfx}x{c}")
        emit_ln(ctx, x_sb, "c", 8, TC, a1, b1r, sq, lnrows, ln_banks, f"{pfx}ln1_{c}")
        for j in range(8):
            p.do("vector", lambda e, j=j: e.tensor_mul(
                y_sb[:, j * TC:(j + 1) * TC], x_sb[:, j * TC:(j + 1) * TC], a1[:]),
                reads=[(x_sb.name, ("c", j)), a1.r()],
                writes=[(y_sb.name, ("c", j))], tag=f"{pfx}y1_{c}_{j}")
            p.do("vector", lambda e, j=j: e.tensor_add(
                y_sb[:, j * TC:(j + 1) * TC], y_sb[:, j * TC:(j + 1) * TC], b1r[:]),
                reads=[(y_sb.name, ("c", j)), b1r.r()],
                writes=[(y_sb.name, ("c", j))], tag=f"{pfx}y2_{c}_{j}")
        emit_ln(ctx, y_sb, "c", 8, TC, a2, b2r, sq, lnrows, ln_banks, f"{pfx}ln2_{c}")
        for j in range(8):
            jj = j % 2
            p.do("vector", lambda e, j=j, jj=jj: e.tensor_mul(
                t2_sb[:, jj * TC:(jj + 1) * TC], y_sb[:, j * TC:(j + 1) * TC], a2[:]),
                reads=[(y_sb.name, ("c", j)), a2.r()],
                writes=[(t2_sb.name, ("c", jj))], tag=f"{pfx}t2a_{c}_{j}")
            p.do("vector", lambda e, j=j, jj=jj: e.tensor_add(
                t2_sb[:, jj * TC:(jj + 1) * TC], t2_sb[:, jj * TC:(jj + 1) * TC], b2r[:]),
                reads=[(t2_sb.name, ("c", jj)), b2r.r()],
                writes=[(t2_sb.name, ("c", jj))], tag=f"{pfx}t2b_{c}_{j}")
            p.do("vector", lambda e, j=j, jj=jj: e.tensor_scalar(
                xn_sb[:, j * TC:(j + 1) * TC], t2_sb[:, jj * TC:(jj + 1) * TC],
                gc[:, j:j + 1], bc_[:, j:j + 1], ALU.mult, ALU.add),
                reads=[(t2_sb.name, ("c", jj)), gc.r(), bc_.r()],
                writes=[(xn_sb.name, ("c", j))], tag=f"{pfx}xn_{c}_{j}")
        if dbg is not None and c == 0:
            p.dma("gpsimd", dbg["w1"], w1[:], p.dma_sem("sdbg1"),
                  reads=[(w1.name, ("q", dt)) for dt in range(8)], tag="dbg_w1")
            p.dma("gpsimd", dbg["y"], y_sb[:], p.dma_sem("sdbg2"),
                  reads=[(y_sb.name, ("c", j)) for j in range(8)], tag="dbg_y")
            p.dma("gpsimd", dbg["xn"], xn_sb[:], p.dma_sem("sdbg3"),
                  reads=[(xn_sb.name, ("c", j)) for j in range(8)], tag="dbg_xn")
        for ft in range(32):
            bm = bank[ft % 2]
            for dt in range(8):
                p.do("tensor", lambda e, ft=ft, dt=dt, bm=bm: e.matmul(
                    bm[:, 0:TC], w1[:, dt * F + ft * 128:dt * F + (ft + 1) * 128],
                    xn_sb[:, dt * TC:(dt + 1) * TC],
                    start=(dt == 0), stop=(dt == 7)),
                    reads=[(w1.name, ("q", dt)), (xn_sb.name, ("c", dt))],
                    writes=[bm.r()], tag=f"{pfx}h_{c}_{ft}_{dt}"),
            p.do("scalar", lambda e, ft=ft, bm=bm: e.activation(
                h_sb[:, ft * TC:(ft + 1) * TC], bm[:, 0:TC], AF.Silu,
                bias=b1c[:, ft:ft + 1], scale=1.0),
                reads=[bm.r(), b1c.r()], writes=[(h_sb.name, ("c", ft))],
                tag=f"{pfx}hev_{c}_{ft}")
        if dbg is not None and c == 0:
            p.dma("gpsimd", dbg["h"], h_sb[:], p.dma_sem("sdbg4"),
                  reads=[(h_sb.name, ("c", ft)) for ft in range(32)], tag="dbg_h")
            p.dma("gpsimd", dbg["w2"], w2[:], p.dma_sem("sdbg5"),
                  reads=[(w2.name, ("q", g8)) for g8 in range(4)], tag="dbg_w2")
        for dot in range(8):
            bm = bank[2 + dot % 2]
            for ft in range(32):
                p.do("tensor", lambda e, dot=dot, ft=ft, bm=bm: e.matmul(
                    bm[:, 0:TC], w2[:, ft * D + dot * 128:ft * D + (dot + 1) * 128],
                    h_sb[:, ft * TC:(ft + 1) * TC],
                    start=(ft == 0), stop=(ft == 31)),
                    reads=[(w2.name, ("q", ft // 8)), (h_sb.name, ("c", ft))],
                    writes=[bm.r()], tag=f"{pfx}z_{c}_{dot}_{ft}")
            dd = dot % 2
            p.do("scalar", lambda e, dot=dot, dd=dd, bm=bm: e.activation(
                t2_sb[:, dd * TC:(dd + 1) * TC], bm[:, 0:TC], AF.Identity,
                bias=hb2[:, dot:dot + 1], scale=ctx.constcol(0.5)[:, 0:1]),
                reads=[bm.r(), hb2.r(), ctx.constcol(0.5).r()],
                writes=[(t2_sb.name, ("c", dd))], tag=f"{pfx}zev_{c}_{dot}")
            p.do("vector", lambda e, dot=dot: e.tensor_add(
                y_sb[:, dot * TC:(dot + 1) * TC], y_sb[:, dot * TC:(dot + 1) * TC],
                x_sb[:, dot * TC:(dot + 1) * TC]),
                reads=[(y_sb.name, ("c", dot)), (x_sb.name, ("c", dot))],
                writes=[(y_sb.name, ("c", dot))], tag=f"{pfx}o1_{c}_{dot}")
            p.do("vector", lambda e, dot=dot, dd=dd: e.tensor_add(
                y_sb[:, dot * TC:(dot + 1) * TC], y_sb[:, dot * TC:(dot + 1) * TC],
                t2_sb[:, dd * TC:(dd + 1) * TC]),
                reads=[(y_sb.name, ("c", dot)), (t2_sb.name, ("c", dd))],
                writes=[(y_sb.name, ("o", dot))], tag=f"{pfx}o2_{c}_{dot}")
        osl = xout[:, c * TC:(c + 1) * TC].rearrange("(j p) t -> p j t", p=128)
        p.dma("sync", osl, y_sb[:].rearrange("p (j t) -> p j t", j=8),
              sem_out, reads=[y_sb.r()] + [(y_sb.name, ("o", j)) for j in range(8)],
              tag=f"{pfx}out{c}")
    return sem_out



def _reshape_inputs_qmlp(L, inputs):
    pf = f"L{L}_ffn_"
    return {
        "fc1_q": np.ascontiguousarray(inputs[pf + "fc1_q"]),
        "fc1_scales": np.ascontiguousarray(inputs[pf + "fc1_scales"]),
        "fc1_bias": np.ascontiguousarray(inputs[pf + "fc1_bias"].reshape(32, 128)),
        "fc2_q": np.ascontiguousarray(inputs[pf + "fc2_q"]),
        "fc2_scales": np.ascontiguousarray(inputs[pf + "fc2_scales"]),
        "fc2_bias": np.ascontiguousarray(inputs[pf + "fc2_bias"].reshape(8, 128)),
        "gamma": np.ascontiguousarray(inputs[pf + "ln_gamma"].reshape(8, 128)),
        "beta": np.ascontiguousarray(inputs[pf + "ln_beta"].reshape(8, 128)),
    }


QMLP_DRAM_SPECS = [
    ("fc1_q", [1024, 4096], I32), ("fc1_scales", [8, 4096], F32),
    ("fc1_bias", [32, 128], F32), ("fc2_q", [4096, 1024], I32),
    ("fc2_scales", [32, 1024], F32), ("fc2_bias", [8, 128], F32),
    ("gamma", [8, 128], F32), ("beta", [8, 128], F32),
]


def build_qmlp_standalone(debug=False):
    nc = bass.Bass("TRN2", target_bir_lowering=False)
    stack = ExitStack()
    with stack:
        ctx = Ctx(nc, stack)
        xin = nc.dram_tensor("xT", [D, T], F32, kind="ExternalInput").ap()
        xout = nc.dram_tensor("yT", [D, T], F32, kind="ExternalOutput").ap()
        w = {}
        for name, shape, dt in QMLP_DRAM_SPECS:
            w[name] = nc.dram_tensor(name, shape, dt, kind="ExternalInput").ap()
        dbg = None
        if debug:
            dbg = {
                "w1": nc.dram_tensor("dbg_w1", [128, 8 * 4096], F32, kind="ExternalOutput").ap(),
                "y": nc.dram_tensor("dbg_y", [128, 8 * 256], F32, kind="ExternalOutput").ap(),
                "xn": nc.dram_tensor("dbg_xn", [128, 8 * 256], F32, kind="ExternalOutput").ap(),
                "h": nc.dram_tensor("dbg_h", [128, 32 * 256], F32, kind="ExternalOutput").ap(),
                "w2": nc.dram_tensor("dbg_w2", [128, 32 * 1024], F32, kind="ExternalOutput").ap(),
            }
        ctx.setup_consts()
        sem_out = emit_qmlp(ctx, xin, xout, w, "qm_", dbg=dbg)
        last = sem_out.inc_ops[-1]
        ctx.p._add_wait(ctx.p.do("sync", lambda e: e.nop(), tag="fin"), last)
        ctx.p.finalize()
    return nc





def emit_l2attn(ctx, xin, xout, w, pfx):
    p = ctx.p
    TC = 512
    NCH = T // TC
    bank = ctx.banks

    wq = ctx.sb(pfx + "wqs", [128, 8 * D], BF16)
    wk = ctx.sb(pfx + "wks", [128, 8 * D], BF16)
    wv = ctx.sb(pfx + "wvs", [128, 8 * D], BF16)
    xn_sb = ctx.sb(pfx + "xn", [128, 8 * T], BF16)
    x_sb = ctx.sb(pfx + "x", [128, 8 * TC], F32)
    k_sb = ctx.sb(pfx + "k", [128, 8 * TC], F32)
    v_sb = ctx.sb(pfx + "v", [128, 8 * TC], F32)
    sqt = ctx.sb(pfx + "sqt", [128, 2 * TC], F32)
    rs_bc = ctx.sb(pfx + "rsbc", [128, TC], F32)
    gf = ctx.sb(pfx + "gf", [128, 8], F32)
    gfs = ctx.sb(pfx + "gfs", [128, 8], F32)
    a1 = ctx.sb(pfx + "a1", [128, TC], F32)
    b1r = ctx.sb(pfx + "b1r", [128, TC], F32)
    lnrows = ctx.sb(pfx + "lnrows", [1, 3 * TC], F32)
    nrows = ctx.sb(pfx + "nrows", [1, 2 * TC], F32)

    for name, t in (("wq", wq), ("wk", wk), ("wv", wv)):
        p.dma("gpsimd", t[:].rearrange("p (j e) -> p j e", j=8),
              w[name].rearrange("(j p) e -> p j e", p=128),
              p.dma_sem(pfx + "s" + name), writes=[t.r()], tag=pfx + name)
    p.do("gpsimd", lambda e: e.memset(gf[:], 0.0), writes=[gf.r()], tag=pfx + "gf0")

    sem_x = p.dma_sem(pfx + "sx")
    sem_x2 = p.dma_sem(pfx + "sx2")
    sem_out = p.dma_sem(pfx + "so")
    ln_banks = {"s1": bank[4], "s2": bank[5], "bc": bank[6]}

    def proj_etile(wsb, et, rkey, dst, c, bm, extra_tag):
        for dt in range(8):
            p.do("tensor", lambda e, et=et, dt=dt, bm=bm: e.matmul(
                bm[:, 0:TC], wsb[:, dt * D + et * 128:dt * D + (et + 1) * 128],
                xn_sb[:, dt * T + c * TC:dt * T + (c + 1) * TC],
                start=(dt == 0), stop=(dt == 7)),
                reads=[wsb.r(), (xn_sb.name, rkey(dt))],
                writes=[bm.r()], tag=f"{pfx}{extra_tag}_{c}_{et}_{dt}")
        p.do("scalar", lambda e, et=et, bm=bm: e.copy(dst[:, et * TC:(et + 1) * TC], bm[:, 0:TC]),
             reads=[bm.r()], writes=[(dst.name, ("c", et))], tag=f"{pfx}{extra_tag}ev_{c}_{et}")

    for c in range(NCH):
        xsl = xin[:, c * TC:(c + 1) * TC].rearrange("(j p) t -> p j t", p=128)
        p.dma("sync", x_sb[:].rearrange("p (j t) -> p j t", j=8), xsl, sem_x,
              writes=[x_sb.r()], tag=f"{pfx}x{c}")
        emit_ln(ctx, x_sb, "c", 8, TC, a1, b1r, sqt, lnrows, ln_banks, f"{pfx}ln_{c}")
        for j in range(8):
            p.do("vector", lambda e, j=j, c=c: e.tensor_mul(
                xn_sb[:, j * T + c * TC:j * T + (c + 1) * TC],
                x_sb[:, j * TC:(j + 1) * TC], a1[:]),
                reads=[(x_sb.name, ("c", j)), a1.r()],
                writes=[(xn_sb.name, ("t", c, j))], tag=f"{pfx}y1_{c}_{j}")
            p.do("vector", lambda e, j=j, c=c: e.tensor_add(
                xn_sb[:, j * T + c * TC:j * T + (c + 1) * TC],
                xn_sb[:, j * T + c * TC:j * T + (c + 1) * TC], b1r[:]),
                reads=[(xn_sb.name, ("t", c, j)), b1r.r()],
                writes=[(xn_sb.name, ("c", j)), (xn_sb.name, ("done", c, j))],
                tag=f"{pfx}y2_{c}_{j}")
        for et in range(8):
            proj_etile(wk, et, lambda dt: ("c", dt), k_sb, c, bank[et % 2], "K")
            proj_etile(wv, et, lambda dt: ("c", dt), v_sb, c, bank[2 + et % 2], "V")
        for et in range(8):
            p.do("vector", lambda e, et=et: e.tensor_mul(
                sqt[:, (et % 2) * TC:(et % 2 + 1) * TC],
                k_sb[:, et * TC:(et + 1) * TC], k_sb[:, et * TC:(et + 1) * TC]),
                reads=[(k_sb.name, ("c", et))], writes=[sqt.r(et % 2)],
                tag=f"{pfx}k2_{c}_{et}")
            p.do("tensor", lambda e, et=et: e.matmul(
                bank[4][0:1, 0:TC], ctx.ones_col[:], sqt[:, (et % 2) * TC:(et % 2 + 1) * TC],
                start=(et == 0), stop=(et == 7)),
                reads=[sqt.r(et % 2), ctx.ones_col.r()], writes=[bank[4].r()],
                tag=f"{pfx}ssk_{c}_{et}")
        p.do("vector", lambda e: e.tensor_single_scalar(
            nrows[0:1, 0:TC], bank[4][0:1, 0:TC], 1e-24, ALU.max),
            reads=[bank[4].r()], writes=[(nrows.name, 0)], tag=f"{pfx}ssm_{c}")
        p.do("scalar", lambda e: e.activation(
            nrows[0:1, 0:TC], nrows[0:1, 0:TC], AF.Sqrt),
            reads=[(nrows.name, 0)], writes=[(nrows.name, 0)], tag=f"{pfx}ssq_{c}")
        p.do("vector", lambda e: e.reciprocal(nrows[0:1, TC:2 * TC], nrows[0:1, 0:TC]),
             reads=[(nrows.name, 0)], writes=[(nrows.name, 1)], tag=f"{pfx}srec_{c}")
        p.do("tensor", lambda e: e.matmul(
            bank[6][:, 0:TC], ctx.ones_row[:], nrows[0:1, TC:2 * TC], start=True, stop=True),
            reads=[(nrows.name, 1), ctx.ones_row.r()], writes=[bank[6].r()],
            tag=f"{pfx}sbc_{c}")
        p.do("scalar", lambda e: e.copy(rs_bc[:], bank[6][:, 0:TC]),
             reads=[bank[6].r()], writes=[rs_bc.r()], tag=f"{pfx}sbce_{c}")
        for et in range(8):
            p.do("vector", lambda e, et=et: e.tensor_mul(
                k_sb[:, et * TC:(et + 1) * TC], k_sb[:, et * TC:(et + 1) * TC],
                v_sb[:, et * TC:(et + 1) * TC]),
                reads=[(k_sb.name, ("c", et)), (v_sb.name, ("c", et))],
                writes=[(k_sb.name, ("c", et))], tag=f"{pfx}kv_{c}_{et}")
            p.do("vector", lambda e, et=et: e.tensor_mul(
                k_sb[:, et * TC:(et + 1) * TC], k_sb[:, et * TC:(et + 1) * TC],
                rs_bc[:]),
                reads=[(k_sb.name, ("c", et)), rs_bc.r()],
                writes=[(k_sb.name, ("c", et))], tag=f"{pfx}kvn_{c}_{et}")
            p.do("vector", lambda e, et=et: e.reduce_sum(
                sqt[:, (et % 2) * TC:(et % 2) * TC + 1], k_sb[:, et * TC:(et + 1) * TC],
                axis=AX.X),
                reads=[(k_sb.name, ("c", et))], writes=[sqt.r(et % 2)],
                tag=f"{pfx}red_{c}_{et}")
            p.do("vector", lambda e, et=et: e.tensor_add(
                gf[:, et:et + 1], gf[:, et:et + 1], sqt[:, (et % 2) * TC:(et % 2) * TC + 1]),
                reads=[gf.r(), sqt.r(et % 2)], writes=[gf.r()],
                tag=f"{pfx}gfa_{c}_{et}")
    p.do("scalar", lambda e: e.mul(gfs[:], gf[:], 1.0 / T),
         reads=[gf.r()], writes=[gfs.r()], tag=pfx + "gfs")

    q_sb = k_sb
    for c in range(NCH):
        xsl = xin[:, c * TC:(c + 1) * TC].rearrange("(j p) t -> p j t", p=128)
        p.dma("sync", x_sb[:].rearrange("p (j t) -> p j t", j=8), xsl, sem_x2,
              writes=[x_sb.r()], tag=f"{pfx}x2_{c}")
        for et in range(8):
            proj_etile(wq, et, lambda dt, c=c: ("done", c, dt), q_sb, c, bank[et % 2], "Q")
        for et in range(8):
            p.do("vector", lambda e, et=et: e.tensor_mul(
                sqt[:, (et % 2) * TC:(et % 2 + 1) * TC],
                q_sb[:, et * TC:(et + 1) * TC], q_sb[:, et * TC:(et + 1) * TC]),
                reads=[(q_sb.name, ("c", et))], writes=[sqt.r(et % 2)],
                tag=f"{pfx}q2_{c}_{et}")
            p.do("tensor", lambda e, et=et: e.matmul(
                bank[4][0:1, 0:TC], ctx.ones_col[:], sqt[:, (et % 2) * TC:(et % 2 + 1) * TC],
                start=(et == 0), stop=(et == 7)),
                reads=[sqt.r(et % 2), ctx.ones_col.r()], writes=[bank[4].r()],
                tag=f"{pfx}ssq2_{c}_{et}")
        p.do("vector", lambda e: e.tensor_single_scalar(
            nrows[0:1, 0:TC], bank[4][0:1, 0:TC], 1e-24, ALU.max),
            reads=[bank[4].r()], writes=[(nrows.name, 0)], tag=f"{pfx}qsm_{c}")
        p.do("scalar", lambda e: e.activation(
            nrows[0:1, 0:TC], nrows[0:1, 0:TC], AF.Sqrt),
            reads=[(nrows.name, 0)], writes=[(nrows.name, 0)], tag=f"{pfx}qsq_{c}")
        p.do("vector", lambda e: e.reciprocal(nrows[0:1, TC:2 * TC], nrows[0:1, 0:TC]),
             reads=[(nrows.name, 0)], writes=[(nrows.name, 1)], tag=f"{pfx}qrec_{c}")
        p.do("tensor", lambda e: e.matmul(
            bank[6][:, 0:TC], ctx.ones_row[:], nrows[0:1, TC:2 * TC], start=True, stop=True),
            reads=[(nrows.name, 1), ctx.ones_row.r()], writes=[bank[6].r()],
            tag=f"{pfx}qbc_{c}")
        p.do("scalar", lambda e: e.copy(rs_bc[:], bank[6][:, 0:TC]),
             reads=[bank[6].r()], writes=[rs_bc.r()], tag=f"{pfx}qbce_{c}")
        for et in range(8):
            p.do("vector", lambda e, et=et: e.tensor_mul(
                q_sb[:, et * TC:(et + 1) * TC], q_sb[:, et * TC:(et + 1) * TC], rs_bc[:]),
                reads=[(q_sb.name, ("c", et)), rs_bc.r()],
                writes=[(q_sb.name, ("c", et))], tag=f"{pfx}qn_{c}_{et}")
            p.do("vector", lambda e, et=et: e.tensor_scalar_mul(
                q_sb[:, et * TC:(et + 1) * TC], q_sb[:, et * TC:(et + 1) * TC],
                gfs[:, et:et + 1]),
                reads=[(q_sb.name, ("c", et)), gfs.r()],
                writes=[(q_sb.name, ("c", et))], tag=f"{pfx}qg_{c}_{et}")
            p.do("vector", lambda e, et=et: e.tensor_add(
                x_sb[:, et * TC:(et + 1) * TC], x_sb[:, et * TC:(et + 1) * TC],
                q_sb[:, et * TC:(et + 1) * TC]),
                reads=[(x_sb.name, None), (q_sb.name, ("c", et))],
                writes=[(x_sb.name, ("o", et))], tag=f"{pfx}oadd_{c}_{et}")
        osl = xout[:, c * TC:(c + 1) * TC].rearrange("(j p) t -> p j t", p=128)
        p.dma("sync", osl, x_sb[:].rearrange("p (j t) -> p j t", j=8), sem_out,
              reads=[x_sb.r()] + [(x_sb.name, ("o", j)) for j in range(8)],
              tag=f"{pfx}out{c}")
    return sem_out


L2A_DRAM_SPECS = [("wq", [1024, 1024], F32), ("wk", [1024, 1024], F32),
                  ("wv", [1024, 1024], F32)]


def build_l2attn_standalone():
    nc = bass.Bass("TRN2", target_bir_lowering=False)
    stack = ExitStack()
    with stack:
        ctx = Ctx(nc, stack)
        xin = nc.dram_tensor("xT", [D, T], F32, kind="ExternalInput").ap()
        xout = nc.dram_tensor("yT", [D, T], F32, kind="ExternalOutput").ap()
        w = {}
        for name, shape, dt in L2A_DRAM_SPECS:
            w[name] = nc.dram_tensor(name, shape, dt, kind="ExternalInput").ap()
        ctx.setup_consts()
        sem_out = emit_l2attn(ctx, xin, xout, w, "la_")
        last = sem_out.inc_ops[-1]
        ctx.p._add_wait(ctx.p.do("sync", lambda e: e.nop(), tag="fin"), last)
        ctx.p.finalize()
    return nc





def emit_l0f(ctx, xin, xout, w, pfx, dbg=None):
    p = ctx.p
    TC = 512
    NCH = T // TC
    FE = 512
    bank = ctx.banks

    xn_sb = ctx.sb(pfx + "xn", [128, 8 * T], BF16)
    x_sb = ctx.sb(pfx + "x", [128, 8 * TC], F32)
    acc = ctx.sb(pfx + "acc", [128, 8 * 1024], F32)
    rw_sb = ctx.sb(pfx + "rw", [128, 8 * 17], F32)
    wT = ctx.sb(pfx + "wT", [17, T], BF16)
    ohot_bf = ctx.get_ohot(BF16)
    wg_sb = ctx.sb(pfx + "wg", [128, 2 * 8 * FE], BF16)
    wu_sb = ctx.sb(pfx + "wu", [128, 2 * 8 * FE], BF16)
    wd_sb = ctx.sb(pfx + "wd", [128, 2 * 4 * 1024], BF16)
    h_sb = ctx.sb(pfx + "h", [128, 4 * TC], BF16)
    gs_sb = ctx.sb(pfx + "gs", [128, TC], F32)
    wbc_sb = ctx.sb(pfx + "wbc", [128, TC], F32)
    rout = ctx.sb(pfx + "rout", [128, 32 * 17], F32)
    m1i = ctx.sb(pfx + "m1i", [128, 512], F32)
    cmb = ctx.sb(pfx + "cmb", [128, 32 * 17], F32)
    cls = ctx.sb(pfx + "cls", [128, 96], F32)
    sqt = ctx.sb(pfx + "sq", [128, 2 * TC], F32)
    a1 = ctx.sb(pfx + "a1", [128, TC], F32)
    b1r = ctx.sb(pfx + "b1r", [128, TC], F32)
    lnrows = ctx.sb(pfx + "lnrows", [1, 3 * TC], F32)

    sem_x = p.dma_sem(pfx + "sx")
    sem_x2 = p.dma_sem(pfx + "sx2")
    sem_out = p.dma_sem(pfx + "so")
    sem_wg = [p.dma_sem(pfx + f"swg{i}") for i in range(2)]
    sem_wu = [p.dma_sem(pfx + f"swu{i}") for i in range(2)]
    sem_wd = [p.dma_sem(pfx + f"swd{i}") for i in range(2)]
    ln_banks = {"s1": bank[4], "s2": bank[5], "bc": bank[6]}

    p.dma("sync", rw_sb[:].rearrange("p (j e) -> p j e", j=8),
          w["rw17"].rearrange("(j p) e -> p j e", p=128),
          p.dma_sem(pfx + "srw"), writes=[rw_sb.r()], tag=pfx + "rw")

    for c in range(NCH):
        xsl = xin[:, c * TC:(c + 1) * TC].rearrange("(j p) t -> p j t", p=128)
        p.dma("sync", x_sb[:].rearrange("p (j t) -> p j t", j=8), xsl, sem_x,
              writes=[x_sb.r()], tag=f"{pfx}x{c}")
        emit_ln(ctx, x_sb, "c", 8, TC, a1, b1r, sqt, lnrows, ln_banks, f"{pfx}ln_{c}")
        for j in range(8):
            p.do("vector", lambda e, j=j: e.tensor_mul(
                x_sb[:, j * TC:(j + 1) * TC], x_sb[:, j * TC:(j + 1) * TC], a1[:]),
                reads=[(x_sb.name, ("c", j)), a1.r()],
                writes=[(x_sb.name, ("c", j))], tag=f"{pfx}y1_{c}_{j}")
            p.do("vector", lambda e, j=j: e.tensor_add(
                x_sb[:, j * TC:(j + 1) * TC], x_sb[:, j * TC:(j + 1) * TC], b1r[:]),
                reads=[(x_sb.name, ("c", j)), b1r.r()],
                writes=[(x_sb.name, ("y", j))], tag=f"{pfx}y2_{c}_{j}")
            p.do("vector", lambda e, j=j, c=c: e.tensor_copy(
                xn_sb[:, j * T + c * TC:j * T + (c + 1) * TC],
                x_sb[:, j * TC:(j + 1) * TC]),
                reads=[(x_sb.name, ("y", j))],
                writes=[(xn_sb.name, ("xn", c, j))], tag=f"{pfx}xncp_{c}_{j}")
        for ts in range(4):
            bR = bank[2]
            g = c * 4 + ts
            for dt in range(8):
                p.do("tensor", lambda e, ts=ts, dt=dt, bR=bR: e.matmul(
                    bR[:, 0:17], x_sb[:, dt * TC + ts * 128:dt * TC + (ts + 1) * 128],
                    rw_sb[:, dt * 17:(dt + 1) * 17],
                    start=(dt == 0), stop=(dt == 7)),
                    reads=[(x_sb.name, ("y", dt)), rw_sb.r()],
                    writes=[bR.r()], tag=f"{pfx}rl_{c}_{ts}_{dt}")
            p.do("scalar", lambda e, g=g, bR=bR: e.copy(
                rout[:, g * 17:(g + 1) * 17], bR[:, 0:17]),
                reads=[bR.r()], writes=[(rout.name, g)], tag=f"{pfx}re_{c}_{ts}")

    ALLR = [(rout.name, g) for g in range(32)]
    r3 = rout[:].rearrange("p (g k) -> p g k", k=17)
    c3 = lambda sl: cls[:, sl].rearrange("p (g o) -> p g o", o=1).broadcast_to([128, 32, 16])
    w3 = cmb[:].rearrange("p (g k) -> p g k", k=17)
    e3 = r3[:, :, 0:16]
    p3 = w3[:, :, 0:16]
    p.do("vector", lambda e: e.reduce_max(
        cls[:, 0:32], e3, axis=AX.X, negate=True),
        reads=ALLR, writes=[(cls.name, "nm")], tag=pfx + "nm")
    p.do("vector", lambda e: e.tensor_tensor(
        e3, e3, c3(slice(0, 32)), ALU.add),
        reads=ALLR + [(cls.name, "nm")], writes=[(rout.name, "e")], tag=pfx + "dsub")
    p.do("scalar", lambda e: e.activation(e3, e3, AF.Exp),
         reads=[(rout.name, "e")], writes=[(rout.name, "e")], tag=pfx + "exp")
    p.do("vector", lambda e: e.tensor_single_scalar(
        m1i[:].rearrange("p (g k) -> p g k", k=16), e3, 1.0, ALU.is_lt),
        reads=[(rout.name, "e")], writes=[m1i.r()], tag=pfx + "m1i")
    p.do("vector", lambda e: e.tensor_tensor(
        p3, e3, m1i[:].rearrange("p (g k) -> p g k", k=16), ALU.mult),
        reads=[(rout.name, "e"), m1i.r()], writes=[(cmb.name, "p")], tag=pfx + "p2")
    p.do("vector", lambda e: e.reduce_max(cls[:, 32:64], p3, axis=AX.X),
         reads=[(cmb.name, "p")], writes=[(cls.name, "m2")], tag=pfx + "m2")
    p.do("vector", lambda e: e.tensor_tensor(
        p3, p3, c3(slice(32, 64)), ALU.subtract),
        reads=[(cmb.name, "p"), (cls.name, "m2")], writes=[(cmb.name, "p")], tag=pfx + "d2")
    p.do("vector", lambda e: e.tensor_single_scalar(p3, p3, 0.0, ALU.is_ge),
         reads=[(cmb.name, "p")], writes=[(cmb.name, "p")], tag=pfx + "mk2")
    p.do("vector", lambda e: e.scalar_tensor_tensor(
        m1i[:].rearrange("p (g k) -> p g k", k=16), m1i[:].rearrange("p (g k) -> p g k", k=16),
        -1.0, p3, ALU.mult, ALU.add),
        reads=[m1i.r(), (cmb.name, "p")], writes=[m1i.r()], tag=pfx + "msk")
    p.do("vector", lambda e: e.tensor_single_scalar(
        cls[:, 32:64], cls[:, 32:64], 1.0, ALU.add),
        reads=[(cls.name, "m2")], writes=[(cls.name, "m2")], tag=pfx + "dn")
    p.do("vector", lambda e: e.tensor_single_scalar(m1i[:], m1i[:], 1.0, ALU.add),
         reads=[m1i.r()], writes=[m1i.r()], tag=pfx + "msk1")
    p.do("vector", lambda e: e.reciprocal(cls[:, 64:96], cls[:, 32:64]),
         reads=[(cls.name, "m2")], writes=[(cls.name, "rc")], tag=pfx + "rc")
    p.do("vector", lambda e: e.tensor_tensor(
        p3, e3, m1i[:].rearrange("p (g k) -> p g k", k=16), ALU.mult),
        reads=[(rout.name, "e"), m1i.r()], writes=[(cmb.name, "p")], tag=pfx + "wmul")
    p.do("gpsimd", lambda e: e.tensor_tensor(
        p3, p3, c3(slice(64, 96)), ALU.mult),
        reads=[(cmb.name, "p"), (cls.name, "rc")], writes=[(cmb.name, "w")], tag=pfx + "wfin")
    p.do("scalar", lambda e: e.activation(
        w3[:, :, 16:17], r3[:, :, 16:17], AF.Sigmoid),
        reads=ALLR, writes=[(cmb.name, "g")], tag=pfx + "gate")
    for g in range(32):
        p.do("tensor", lambda e, g=g: e.transpose(
            bank[3][0:17, 0:128], cmb[:, g * 17:(g + 1) * 17], ctx.idn[:]),
            reads=[(cmb.name, "w"), (cmb.name, "g"), ctx.idn.r()],
            writes=[bank[3].r()], tag=f"{pfx}wtt_{g}")
        p.do("scalar", lambda e, g=g: e.copy(
            wT[0:17, g * 128:(g + 1) * 128], bank[3][0:17, 0:128]),
            reads=[bank[3].r()], writes=[(wT.name, ("g", g))],
            tag=f"{pfx}wte_{g}")

    def load_expert(ei, buf):
        if ei < 16:
            p.dma("gpsimd", wg_sb[:, buf * 8 * FE:(buf + 1) * 8 * FE].rearrange(
                "p (j f) -> p j f", j=8),
                w["Wg"][ei].rearrange("(j p) f -> p j f", p=128),
                sem_wg[buf], writes=[(wg_sb.name, buf)], tag=f"{pfx}wg{ei}")
            p.dma("gpsimd", wu_sb[:, buf * 8 * FE:(buf + 1) * 8 * FE].rearrange(
                "p (j f) -> p j f", j=8),
                w["Wu"][ei].rearrange("(j p) f -> p j f", p=128),
                sem_wu[buf], writes=[(wu_sb.name, buf)], tag=f"{pfx}wu{ei}")
            p.dma("gpsimd", wd_sb[:, buf * 4 * 1024:(buf + 1) * 4 * 1024].rearrange(
                "p (j n) -> p j n", j=4),
                w["Wd"][ei].rearrange("(j p) n -> p j n", p=128),
                sem_wd[buf], writes=[(wd_sb.name, buf)], tag=f"{pfx}wd{ei}")
        else:
            q = ei - 16
            p.dma("gpsimd", wg_sb[:, buf * 8 * FE:(buf + 1) * 8 * FE].rearrange(
                "p (j f) -> p j f", j=8),
                w["Sg"][:, q * FE:(q + 1) * FE].rearrange("(j p) f -> p j f", p=128),
                sem_wg[buf], writes=[(wg_sb.name, buf)], tag=f"{pfx}sg{q}")
            p.dma("gpsimd", wu_sb[:, buf * 8 * FE:(buf + 1) * 8 * FE].rearrange(
                "p (j f) -> p j f", j=8),
                w["Su"][:, q * FE:(q + 1) * FE].rearrange("(j p) f -> p j f", p=128),
                sem_wu[buf], writes=[(wu_sb.name, buf)], tag=f"{pfx}su{q}")
            p.dma("gpsimd", wd_sb[:, buf * 4 * 1024:(buf + 1) * 4 * 1024].rearrange(
                "p (j n) -> p j n", j=4),
                w["Sd"][q * FE:(q + 1) * FE, :].rearrange("(j p) n -> p j n", p=128),
                sem_wd[buf], writes=[(wd_sb.name, buf)], tag=f"{pfx}sd{q}")

    if dbg is not None:
        p.dma("gpsimd", dbg["wT"], wT[:], p.dma_sem("sdwT"),
              reads=[(wT.name, ("g", g)) for g in range(32)], tag="dbg_wT")
        p.dma("gpsimd", dbg["xn"], xn_sb[:, 0:T], p.dma_sem("sdxn"),
              reads=[(xn_sb.name, ("xn", c, 0)) for c in range(8)], tag="dbg_xn")
    NEXP = 20
    for s in range(4):
        p.do("gpsimd", lambda e: e.memset(acc[:], 0.0), writes=[acc.r()],
             tag=f"{pfx}acc0_{s}")
        for ei in range(NEXP):
            buf = ei % 2
            if s == 0 and ei < 1:
                load_expert(ei, buf)
            for cl in range(2):
                c = s * 2 + cl
                wrow = ei if ei < 16 else 16
                p.do("tensor", lambda e, wrow=wrow, c=c: e.matmul(
                    bank[2][:, 0:TC], ohot_bf[0:17, wrow * 128:(wrow + 1) * 128],
                    wT[0:17, c * TC:(c + 1) * TC], start=True, stop=True),
                    reads=[(wT.name, ("g", c * 4)), (wT.name, ("g", c * 4 + 1)),
                           (wT.name, ("g", c * 4 + 2)), (wT.name, ("g", c * 4 + 3)),
                           ohot_bf.r()],
                    writes=[bank[2].r()], tag=f"{pfx}wbc_{s}_{ei}_{cl}")
                p.do("scalar", lambda e: e.copy(wbc_sb[:], bank[2][:, 0:TC]),
                     reads=[bank[2].r()], writes=[wbc_sb.r()],
                     tag=f"{pfx}wbce_{s}_{ei}_{cl}")
                for ft in range(4):
                    bg, bu = bank[0], bank[1]
                    for dt in range(8):
                        p.do("tensor", lambda e, ft=ft, dt=dt, buf=buf, c=c, bg=bg: e.matmul(
                            bg[:, 0:TC],
                            wg_sb[:, buf * 8 * FE + dt * FE + ft * 128:buf * 8 * FE + dt * FE + (ft + 1) * 128],
                            xn_sb[:, dt * T + c * TC:dt * T + (c + 1) * TC],
                            start=(dt == 0), stop=(dt == 7)),
                            reads=[(wg_sb.name, buf), (xn_sb.name, ("xn", c, dt))],
                            writes=[bg.r()], tag=f"{pfx}g_{s}_{ei}_{cl}_{ft}_{dt}")
                        p.do("tensor", lambda e, ft=ft, dt=dt, buf=buf, c=c, bu=bu: e.matmul(
                            bu[:, 0:TC],
                            wu_sb[:, buf * 8 * FE + dt * FE + ft * 128:buf * 8 * FE + dt * FE + (ft + 1) * 128],
                            xn_sb[:, dt * T + c * TC:dt * T + (c + 1) * TC],
                            start=(dt == 0), stop=(dt == 7)),
                            reads=[(wu_sb.name, buf), (xn_sb.name, ("xn", c, dt))],
                            writes=[bu.r()], tag=f"{pfx}u_{s}_{ei}_{cl}_{ft}_{dt}")
                    p.do("scalar", lambda e, bg=bg: e.activation(
                        gs_sb[:], bg[:, 0:TC], AF.Silu),
                        reads=[bg.r()], writes=[gs_sb.r()],
                        tag=f"{pfx}sil_{s}_{ei}_{cl}_{ft}")
                    p.do("vector", lambda e, ft=ft, bu=bu: e.tensor_mul(
                        h_sb[:, ft * TC:(ft + 1) * TC], bu[:, 0:TC], wbc_sb[:]),
                        reads=[bu.r(), wbc_sb.r()], writes=[(h_sb.name, ft)],
                        tag=f"{pfx}hu_{s}_{ei}_{cl}_{ft}")
                    p.do("vector", lambda e, ft=ft: e.tensor_mul(
                        h_sb[:, ft * TC:(ft + 1) * TC], h_sb[:, ft * TC:(ft + 1) * TC],
                        gs_sb[:]),
                        reads=[(h_sb.name, ft), gs_sb.r()], writes=[(h_sb.name, ft)],
                        tag=f"{pfx}hg_{s}_{ei}_{cl}_{ft}")
                if ei + 1 < NEXP and cl == 0:
                    load_expert(ei + 1, (ei + 1) % 2)
                elif ei == NEXP - 1 and cl == 0 and s < 3:
                    load_expert(0, 0)
                if dbg is not None and s == 0 and ei == 0 and cl == 0:
                    p.dma("gpsimd", dbg["h"], h_sb[:], p.dma_sem("sdh"),
                          reads=[(h_sb.name, ft) for ft in range(4)], tag="dbg_h")
                    p.dma("gpsimd", dbg["wbc"], wbc_sb[:], p.dma_sem("sdwb"),
                          reads=[wbc_sb.r()], tag="dbg_wbc")
                for dot in range(8):
                    bz = bank[3 + dot % 2]
                    for ft in range(4):
                        p.do("tensor", lambda e, dot=dot, ft=ft, buf=buf, bz=bz: e.matmul(
                            bz[:, 0:TC],
                            wd_sb[:, buf * 4 * 1024 + ft * 1024 + dot * 128:buf * 4 * 1024 + ft * 1024 + (dot + 1) * 128],
                            h_sb[:, ft * TC:(ft + 1) * TC],
                            start=(ft == 0), stop=(ft == 3)),
                            reads=[(wd_sb.name, buf), (h_sb.name, ft)],
                            writes=[bz.r()], tag=f"{pfx}z_{s}_{ei}_{cl}_{dot}_{ft}")
                    p.do("vector", lambda e, dot=dot, cl=cl, bz=bz: e.tensor_add(
                        acc[:, dot * 1024 + cl * TC:dot * 1024 + (cl + 1) * TC],
                        acc[:, dot * 1024 + cl * TC:dot * 1024 + (cl + 1) * TC],
                        bz[:, 0:TC]),
                        reads=[(acc.name, (cl, dot)), bz.r()],
                        writes=[(acc.name, (cl, dot))],
                        tag=f"{pfx}za_{s}_{ei}_{cl}_{dot}")
        for cl in range(2):
            c = s * 2 + cl
            xsl = xin[:, c * TC:(c + 1) * TC].rearrange("(j p) t -> p j t", p=128)
            p.dma("sync", x_sb[:].rearrange("p (j t) -> p j t", j=8), xsl, sem_x2,
                  writes=[x_sb.r()], tag=f"{pfx}x2_{c}")
            for j in range(8):
                p.do("vector", lambda e, j=j, cl=cl: e.tensor_add(
                    x_sb[:, j * TC:(j + 1) * TC], x_sb[:, j * TC:(j + 1) * TC],
                    acc[:, j * 1024 + cl * TC:j * 1024 + (cl + 1) * TC]),
                    reads=[(x_sb.name, None), (acc.name, (cl, j))],
                    writes=[(x_sb.name, ("o", j))], tag=f"{pfx}oa_{c}_{j}")
            osl = xout[:, c * TC:(c + 1) * TC].rearrange("(j p) t -> p j t", p=128)
            p.dma("sync", osl, x_sb[:].rearrange("p (j t) -> p j t", j=8), sem_out,
                  reads=[x_sb.r()] + [(x_sb.name, ("o", j)) for j in range(8)],
                  tag=f"{pfx}out{c}")
    return sem_out


L0F_DRAM_SPECS = [
    ("rw17", [1024, 17], F32), ("Wg", [16, 1024, 512], F32),
    ("Wu", [16, 1024, 512], F32), ("Wd", [16, 512, 1024], F32),
    ("Sg", [1024, 2048], F32), ("Su", [1024, 2048], F32),
    ("Sd", [2048, 1024], F32),
]


def build_l0f_standalone(debug=False):
    nc = bass.Bass("TRN2", target_bir_lowering=False)
    stack = ExitStack()
    with stack:
        ctx = Ctx(nc, stack)
        xin = nc.dram_tensor("xT", [D, T], F32, kind="ExternalInput").ap()
        xout = nc.dram_tensor("yT", [D, T], F32, kind="ExternalOutput").ap()
        w = {}
        for name, shape, dt in L0F_DRAM_SPECS:
            w[name] = nc.dram_tensor(name, shape, dt, kind="ExternalInput").ap()
        dbg = None if not debug else {
            "wT": nc.dram_tensor("dbg_wT", [17, T], F32, kind="ExternalOutput").ap(),
            "xn": nc.dram_tensor("dbg_xn", [128, T], F32, kind="ExternalOutput").ap(),
            "h": nc.dram_tensor("dbg_h", [128, 4 * 512], F32, kind="ExternalOutput").ap(),
            "wbc": nc.dram_tensor("dbg_wbc", [128, 512], F32, kind="ExternalOutput").ap(),
        }
        ctx.setup_consts()
        sem_out = emit_l0f(ctx, xin, xout, w, "mf_", dbg=dbg)
        last = sem_out.inc_ops[-1]
        ctx.p._add_wait(ctx.p.do("sync", lambda e: e.nop(), tag="fin"), last)
        ctx.p.finalize()
    return nc





def emit_l3a(ctx, xin, xout, w, stage, pfx):
    p = ctx.p
    TC = 512
    NCH = T // TC
    bank = ctx.banks
    INV2PI = 1.0 / (2 * np.pi)

    xn_sb = ctx.sb(pfx + "xn", [128, 8 * T], BF16)
    x_sb = ctx.sb(pfx + "x", [128, 8 * TC], F32)
    wk = ctx.sb(pfx + "wk", [128, 8 * 128], BF16)
    wv = ctx.sb(pfx + "wv", [128, 8 * 128], BF16)
    wo = ctx.sb(pfx + "wo", [128, 8 * D], BF16)
    scr1 = ctx.sb(pfx + "scr1", [128, T], F32)
    v_sb = ctx.sb(pfx + "v", [128, T], BF16)
    ck = ctx.sb(pfx + "ck", [128, T], BF16)
    sk = ctx.sb(pfx + "sk", [128, T], BF16)
    cr = ctx.sb(pfx + "cr", [128, T], BF16)
    sr = ctx.sb(pfx + "sr", [128, T], BF16)
    prod = ctx.sb(pfx + "prod", [128, T], BF16)
    mem = ctx.sb(pfx + "mem", [128, T], BF16)
    isq = ctx.sb(pfx + "isq", [128, TC], F32)
    qs = ctx.sb(pfx + "qs", [128, TC], F32)
    ki = ctx.sb(pfx + "ki", [128, TC], I32)
    kf = ctx.sb(pfx + "kf", [128, TC], F32)
    frow = ctx.sb(pfx + "frow", [1, 1024], F32)
    trow = ctx.sb(pfx + "trow", [1, TC], F32)
    gpair = ctx.sb(pfx + "gpair", [2, 2], F32)
    gcols = ctx.sb(pfx + "gcols", [128, 2], F32)
    blk2 = ctx.sb(pfx + "blk2", [2, 128], F32)
    c2pi = ctx.constcol(2 * np.pi)
    czero = ctx.constcol(0.0)
    sqt = ctx.sb(pfx + "sq", [128, 2 * TC], F32)
    a1 = ctx.sb(pfx + "a1", [128, TC], F32)
    b1r = ctx.sb(pfx + "b1r", [128, TC], F32)
    lnrows = ctx.sb(pfx + "lnrows", [1, 3 * TC], F32)

    sem_x = p.dma_sem(pfx + "sx")
    sem_x2 = p.dma_sem(pfx + "sx2")
    sem_out = p.dma_sem(pfx + "so")
    sem_st = p.dma_sem(pfx + "sst")
    sem_cb = p.dma_sem(pfx + "scb")
    sem_g = p.dma_sem(pfx + "sg")
    ln_banks = {"s1": bank[4], "s2": bank[5], "bc": bank[6]}

    p.dma("gpsimd", wo[:].rearrange("p (j e) -> p j e", j=8),
          w["Wo"].rearrange("(j p) e -> p j e", p=128),
          p.dma_sem(pfx + "sWo"), writes=[wo.r()], tag=pfx + "Wo")
    sem_wk = p.dma_sem(pfx + "sWk")
    sem_wv = p.dma_sem(pfx + "sWv")
    p.dma("sync", frow[:], w["freqs"], p.dma_sem(pfx + "sf"),
          writes=[frow.r()], tag=pfx + "freqs")
    p.do("gpsimd", lambda e: e.memset(blk2[:], 1.0), writes=[blk2.r()], tag=pfx + "bk0")
    p.do("gpsimd", lambda e: e.affine_select(
        out=blk2[:], in_=blk2[:], compare_op=ALU.is_equal, fill=0.0,
        base=0, pattern=[[-1, 2], [0, 64]], channel_multiplier=1),
        reads=[blk2.r()], writes=[blk2.r()], tag=pfx + "bk1")
    def emit_trow(c, tg):
        p.do("gpsimd", lambda e, c=c: e.iota(trow[:], pattern=[[1, TC]], base=c * TC,
                                             channel_multiplier=0,
                                             allow_small_or_imprecise_dtypes=True),
             writes=[trow.r()], tag=tg)
    def emit_isq(c, tg):
        emit_trow(c, tg + "t")
        p.do("scalar", lambda e: e.activation(
            lnrows[0:1, 0:TC], trow[0:1, 0:TC], AF.Sqrt,
            bias=ctx.constcol(1.0)[0:1, 0:1], scale=1.0),
            reads=[trow.r(), ctx.constcol(1.0).r()], writes=[(lnrows.name, 0)],
            tag=tg + "s")
        p.do("vector", lambda e: e.reciprocal(lnrows[0:1, TC:2 * TC], lnrows[0:1, 0:TC]),
             reads=[(lnrows.name, 0)], writes=[(lnrows.name, 1)], tag=tg + "r")
        p.do("tensor", lambda e: e.matmul(
            bank[3][:, 0:TC], ctx.ones_row[:], lnrows[0:1, TC:2 * TC],
            start=True, stop=True),
            reads=[(lnrows.name, 1), ctx.ones_row.r()], writes=[bank[3].r()],
            tag=tg + "b")
        p.do("scalar", lambda e: e.copy(isq[:], bank[3][:, 0:TC]),
             reads=[bank[3].r()], writes=[isq.r()], tag=tg + "e")

    for c in range(NCH):
        xsl = xin[:, c * TC:(c + 1) * TC].rearrange("(j p) t -> p j t", p=128)
        p.dma("sync", x_sb[:].rearrange("p (j t) -> p j t", j=8), xsl, sem_x,
              writes=[x_sb.r()], tag=f"{pfx}x{c}")
        emit_ln(ctx, x_sb, "c", 8, TC, a1, b1r, sqt, lnrows, ln_banks, f"{pfx}ln_{c}")
        for j in range(8):
            p.do("vector", lambda e, j=j: e.tensor_mul(
                x_sb[:, j * TC:(j + 1) * TC], x_sb[:, j * TC:(j + 1) * TC], a1[:]),
                reads=[(x_sb.name, ("c", j)), a1.r()],
                writes=[(x_sb.name, ("c", j))], tag=f"{pfx}y1_{c}_{j}")
            p.do("vector", lambda e, j=j, c=c: e.tensor_tensor(
                xn_sb[:, j * T + c * TC:j * T + (c + 1) * TC],
                x_sb[:, j * TC:(j + 1) * TC], b1r[:], ALU.add),
                reads=[(x_sb.name, ("c", j)), b1r.r()],
                writes=[(xn_sb.name, ("xn", c, j))], tag=f"{pfx}y2_{c}_{j}")

    def reduce_sin(src_ap, src_reads, dst, dst_key, c, shift, scale_in, tagp):
        p.do("vector", lambda e: e.tensor_scalar(
            qs[:], src_ap, scale_in * INV2PI, shift * INV2PI, ALU.mult, ALU.add),
            reads=src_reads, writes=[qs.r()], tag=tagp + "q")
        p.do("vector", lambda e: e.tensor_copy(ki[:], qs[:]),
             reads=[qs.r()], writes=[ki.r()], tag=tagp + "ki")
        p.do("vector", lambda e: e.tensor_copy(kf[:], ki[:]),
             reads=[ki.r()], writes=[kf.r()], tag=tagp + "kf")
        p.do("vector", lambda e: e.tensor_sub(qs[:], qs[:], kf[:]),
             reads=[qs.r(), kf.r()], writes=[qs.r()], tag=tagp + "d")
        p.do("scalar", lambda e: e.activation(
            dst[:, c * TC:(c + 1) * TC], qs[:], AF.Sin,
            bias=czero[:, 0:1], scale=c2pi[:, 0:1]),
            reads=[qs.r(), czero.r(), c2pi.r()], writes=[(dst.name, dst_key)],
            tag=tagp + "sin")

    for hi in range(8):
        p.dma("sync", gpair[:], w["gate"][hi], sem_g, writes=[gpair.r()],
              tag=f"{pfx}gp{hi}")
        p.do("tensor", lambda e: e.matmul(
            bank[3][:, 0:2], blk2[:], gpair[:], start=True, stop=True),
            reads=[blk2.r(), gpair.r()], writes=[bank[3].r()], tag=f"{pfx}gx{hi}")
        p.do("scalar", lambda e: e.copy(gcols[:], bank[3][:, 0:2]),
             reads=[bank[3].r()], writes=[gcols.r()], tag=f"{pfx}ge{hi}")
        p.dma("gpsimd", wk[:].rearrange("p (j e) -> p j e", j=8),
              w["Wk"][:, hi * 128:(hi + 1) * 128].rearrange("(j p) e -> p j e", p=128),
              sem_wk, writes=[wk.r()], tag=f"{pfx}wk{hi}")
        p.dma("gpsimd", wv[:].rearrange("p (j e) -> p j e", j=8),
              w["Wv"][:, hi * 128:(hi + 1) * 128].rearrange("(j p) e -> p j e", p=128),
              sem_wv, writes=[wv.r()], tag=f"{pfx}wv{hi}")
        for c in range(NCH):
            for which, wsb, dst, dkey in (("kr", wk, scr1, ("kr", c)),
                                          ("v", wv, v_sb, ("v", c))):
                bm = bank[0] if which == "kr" else bank[1]
                for dt in range(8):
                    p.do("tensor", lambda e, hi=hi, dt=dt, c=c, wsb=wsb, bm=bm: e.matmul(
                        bm[:, 0:TC], wsb[:, dt * 128:(dt + 1) * 128],
                        xn_sb[:, dt * T + c * TC:dt * T + (c + 1) * TC],
                        start=(dt == 0), stop=(dt == 7)),
                        reads=[wsb.r(), (xn_sb.name, ("xn", c, dt))],
                        writes=[bm.r()], tag=f"{pfx}{which}_{hi}_{c}_{dt}")
                wl = [(dst.name, dkey)]
                if which == "kr" and c == 0:
                    wl.append((dst.name, None))
                p.do("scalar", lambda e, c=c, dst=dst, bm=bm: e.copy(
                    dst[:, c * TC:(c + 1) * TC], bm[:, 0:TC]),
                    reads=[bm.r()], writes=wl,
                    tag=f"{pfx}{which}e_{hi}_{c}")
        for c in range(NCH):
            emit_trow(c, f"{pfx}ti_{hi}_{c}")
            p.do("tensor", lambda e, hi=hi, c=c: e.matmul(
                bank[2][:, 0:TC], frow[0:1, hi * 128:(hi + 1) * 128],
                trow[0:1, 0:TC], start=True, stop=True),
                reads=[frow.r(), trow.r()], writes=[bank[2].r()],
                tag=f"{pfx}ang_{hi}_{c}")
            reduce_sin(bank[2][:, 0:TC], [bank[2].r()], sr, ("sr", c), c,
                       0.0, 1.0, f"{pfx}sr_{hi}_{c}_")
            reduce_sin(bank[2][:, 0:TC], [bank[2].r()], cr, ("cr", c), c,
                       np.pi / 2, 1.0, f"{pfx}cr_{hi}_{c}_")
            reduce_sin(scr1[:, c * TC:(c + 1) * TC], [(scr1.name, ("kr", c))],
                       sk, ("sk", c), c, 0.0, 10.0, f"{pfx}sk_{hi}_{c}_")
            reduce_sin(scr1[:, c * TC:(c + 1) * TC], [(scr1.name, ("kr", c))],
                       ck, ("ck", c), c, np.pi / 2, 10.0, f"{pfx}ck_{hi}_{c}_")
        rck = [(ck.name, ("ck", c)) for c in range(NCH)]
        rsk = [(sk.name, ("sk", c)) for c in range(NCH)]
        rcr = [(cr.name, ("cr", c)) for c in range(NCH)]
        rsr = [(sr.name, ("sr", c)) for c in range(NCH)]
        rv = [(v_sb.name, ("v", c)) for c in range(NCH)]
        ones_bc = ctx.ones_col[:].broadcast_to([128, T])
        streams = [(cr, rcr, cr, rcr, False, 0), (sr, rsr, sr, rsr, False, 0),
                   (ck, rck, ck, rck, True, 1), (sk, rsk, sk, rsk, True, 1)]
        first = True
        for (bnd, rbnd, prb, rprb, shifted, gi) in streams:
            if shifted:
                p.do("vector", lambda e, bnd=bnd: e.tensor_tensor(
                    prod[:, 1:T], v_sb[:, 1:T], bnd[:, 0:T - 1], ALU.mult),
                    reads=rv + rbnd, writes=[prod.r()], tag=f"{pfx}pr_{hi}_{gi}")
                p.do("vector", lambda e: e.memset(prod[:, 0:1], 0.0),
                     reads=[prod.r()], writes=[prod.r()], tag=f"{pfx}pz_{hi}_{gi}")
            else:
                p.do("vector", lambda e, bnd=bnd: e.tensor_tensor(
                    prod[:], v_sb[:], bnd[:], ALU.mult),
                    reads=rv + rbnd, writes=[prod.r()], tag=f"{pfx}pr_{hi}_{gi}")
            p.do("vector", lambda e: e.tensor_tensor_scan(
                mem[:], ones_bc, prod[:], 0.0, ALU.mult, ALU.add),
                reads=[prod.r(), ctx.ones_col.r()], writes=[mem.r()],
                tag=f"{pfx}scan_{hi}_{gi}")
            p.do("vector", lambda e, prb=prb: e.tensor_tensor(
                mem[:], mem[:], prb[:], ALU.mult),
                reads=[mem.r()] + rprb, writes=[mem.r()], tag=f"{pfx}pm_{hi}_{gi}")
            if first:
                p.do("vector", lambda e, gi=gi: e.tensor_scalar_mul(
                    scr1[:], mem[:], gcols[:, gi:gi + 1]),
                    reads=[mem.r(), gcols.r()], writes=[(scr1.name, None)],
                    tag=f"{pfx}cb_{hi}_{gi}")
                first = False
            else:
                p.do("vector", lambda e, gi=gi: e.scalar_tensor_tensor(
                    scr1[:], mem[:], gcols[:, gi:gi + 1], scr1[:],
                    ALU.mult, ALU.add),
                    reads=[mem.r(), gcols.r(), (scr1.name, "cb")],
                    writes=[(scr1.name, "cb")], tag=f"{pfx}cb_{hi}_{gi}")
        for c in range(NCH):
            emit_isq(c, f"{pfx}isq_{hi}_{c}")
            p.do("vector", lambda e, c=c: e.tensor_tensor(
                prod[:, c * TC:(c + 1) * TC], scr1[:, c * TC:(c + 1) * TC],
                isq[:], ALU.mult),
                reads=[(scr1.name, "cb"), isq.r()],
                writes=[prod.r()], tag=f"{pfx}cbi_{hi}_{c}")
        p.dma("sync", stage[hi * 128:(hi + 1) * 128, :], prod[:], sem_st,
              reads=[prod.r()], writes=[("stage", hi)], tag=f"{pfx}st_{hi}")

    cb_sb = xn_sb
    for c in range(NCH):
        xsl = xin[:, c * TC:(c + 1) * TC].rearrange("(j p) t -> p j t", p=128)
        p.dma("sync", x_sb[:].rearrange("p (j t) -> p j t", j=8), xsl, sem_x2,
              writes=[x_sb.r()], tag=f"{pfx}x2_{c}")
        csl = stage[:, c * TC:(c + 1) * TC].rearrange("(j p) t -> p j t", p=128)
        p.dma("sync", cb_sb[:, 0:8 * TC].rearrange("p (j t) -> p j t", j=8), csl,
              sem_cb, reads=[("stage", j) for j in range(8)],
              writes=[(cb_sb.name, None)], tag=f"{pfx}cb2_{c}")
        for dot in range(8):
            bm = bank[dot % 2]
            for ci in range(8):
                p.do("tensor", lambda e, dot=dot, ci=ci, bm=bm: e.matmul(
                    bm[:, 0:TC], wo[:, ci * D + dot * 128:ci * D + (dot + 1) * 128],
                    cb_sb[:, ci * TC:(ci + 1) * TC],
                    start=(ci == 0), stop=(ci == 7)),
                    reads=[wo.r(), (cb_sb.name, None)],
                    writes=[bm.r()], tag=f"{pfx}wo_{c}_{dot}_{ci}")
            p.do("vector", lambda e, dot=dot, bm=bm: e.tensor_add(
                x_sb[:, dot * TC:(dot + 1) * TC], x_sb[:, dot * TC:(dot + 1) * TC],
                bm[:, 0:TC]),
                reads=[(x_sb.name, None), bm.r()],
                writes=[(x_sb.name, ("o", dot))], tag=f"{pfx}oa_{c}_{dot}")
        osl = xout[:, c * TC:(c + 1) * TC].rearrange("(j p) t -> p j t", p=128)
        p.dma("sync", osl, x_sb[:].rearrange("p (j t) -> p j t", j=8), sem_out,
              reads=[x_sb.r()] + [(x_sb.name, ("o", j)) for j in range(8)],
              tag=f"{pfx}out{c}")
    return sem_out


L3A_DRAM_SPECS = [("Wk", [1024, 1024], F32), ("Wv", [1024, 1024], F32),
                  ("Wo", [1024, 1024], F32), ("freqs", [1, 1024], F32),
                  ("gate", [8, 2, 2], F32)]


def build_l3a_standalone():
    nc = bass.Bass("TRN2", target_bir_lowering=False)
    stack = ExitStack()
    with stack:
        ctx = Ctx(nc, stack)
        xin = nc.dram_tensor("xT", [D, T], F32, kind="ExternalInput").ap()
        xout = nc.dram_tensor("yT", [D, T], F32, kind="ExternalOutput").ap()
        stage = nc.dram_tensor("l3stage", [D, T], BF16, kind="Internal").ap()
        w = {}
        for name, shape, dt in L3A_DRAM_SPECS:
            w[name] = nc.dram_tensor(name, shape, dt, kind="ExternalInput").ap()
        ctx.setup_consts()
        sem_out = emit_l3a(ctx, xin, xout, w, stage, "ha_")
        last = sem_out.inc_ops[-1]
        ctx.p._add_wait(ctx.p.do("sync", lambda e: e.nop(), tag="fin"), last)
        ctx.p.finalize()
    return nc





def emit_l1f(ctx, xin, xout, w, ys, xns, pfx):
    p = ctx.p
    TC = 512
    NCH = T // TC
    TS = 1024
    bank = ctx.banks
    ohot_bf = ctx.get_ohot(BF16)

    xsh = ctx.sb(pfx + "xsh", [128, 8 * TC], F32)
    h_sb = ctx.sb(pfx + "h", [128, 32 * TS], BF16)
    xn_sb = TBuf(h_sb.ap[:, 0:8 * TC], h_sb.name + "_xn")
    w1c = ctx.sb(pfx + "w1c", [128, 2 * 8 * 512], BF16)
    w2h = ctx.sb(pfx + "w2h", [128, 16 * D], BF16)
    acc = ctx.sb(pfx + "acc", [128, 8 * TS], F32)
    wg_sb = ctx.sb(pfx + "wg", [128, 8 * 8], BF16)
    bg_sb = ctx.sb(pfx + "bgsb", [1, 8], BF16)
    b1c = ctx.sb(pfx + "b1c", [128, 32], F32)
    b2row = ctx.sb(pfx + "b2row", [1, D], BF16)
    b1rows = ctx.sb(pfx + "b1rows", [32, 128], F32)
    onr512 = ctx.sb(pfx + "onr512", [1, TC], BF16)
    w8T = ctx.sb(pfx + "w8T", [8, T], BF16)
    _hf32 = h_sb.ap.bitcast(F32)
    rout = TBuf(_hf32[:, 2048:2304], h_sb.name + "_rout")
    m1i = TBuf(_hf32[:, 2304:2560], h_sb.name + "_m1i")
    cmb = TBuf(_hf32[:, 2560:2816], h_sb.name + "_cmb")
    cls = ctx.sb(pfx + "cls", [128, 128], F32)
    wbc_sb = ctx.sb(pfx + "wbc", [128, TC], F32)
    t1_sb = ctx.sb(pfx + "t1", [128, TC], F32)
    sqt = ctx.sb(pfx + "sq", [128, 2 * TC], F32)
    a1 = ctx.sb(pfx + "a1", [128, TC], F32)
    b1r = ctx.sb(pfx + "b1r", [128, TC], F32)
    a2 = ctx.sb(pfx + "a2", [128, TC], F32)
    b2r = ctx.sb(pfx + "b2r", [128, TC], F32)
    lnrows = ctx.sb(pfx + "lnrows", [1, 3 * TC], F32)
    gc = load_cols(ctx, pfx + "gc", w["gamma"], 128, 8, None, bank[6])
    bc_ = load_cols(ctx, pfx + "bc", w["beta"], 128, 8, None, bank[6])
    xn_super = xsh.ap.bitcast(BF16)

    sem_x = p.dma_sem(pfx + "sx")
    sem_y = p.dma_sem(pfx + "sy")
    sem_xn = p.dma_sem(pfx + "sxn")
    sem_xs = p.dma_sem(pfx + "sxs")
    sem_x2 = p.dma_sem(pfx + "sx2")
    sem_y2 = p.dma_sem(pfx + "sy2")
    sem_out = p.dma_sem(pfx + "so")
    sem_w1 = [p.dma_sem(pfx + f"sw1_{i}") for i in range(2)]
    sem_w2 = p.dma_sem(pfx + "sw2")
    sem_eb = p.dma_sem(pfx + "seb")
    sem_eb2 = p.dma_sem(pfx + "seb2")
    ln_banks = {"s1": bank[4], "s2": bank[5], "bc": bank[6]}

    p.dma("gpsimd", wg_sb[:].rearrange("p (j e) -> p j e", j=8),
          w["WgT"].rearrange("(j p) e -> p j e", p=128),
          p.dma_sem(pfx + "swg"), writes=[wg_sb.r()], tag=pfx + "wg")
    p.dma("gpsimd", bg_sb[:], w["bg"], p.dma_sem(pfx + "sbg"),
          writes=[bg_sb.r()], tag=pfx + "bg")
    p.do("gpsimd", lambda e: e.memset(onr512[:], 1.0), writes=[onr512.r()],
         tag=pfx + "onr")

    for c in range(NCH):
        xsl = xin[:, c * TC:(c + 1) * TC].rearrange("(j p) t -> p j t", p=128)
        p.dma("sync", xsh[:].rearrange("p (j t) -> p j t", j=8), xsl, sem_x,
              writes=[xsh.r()], tag=f"{pfx}x{c}")
        emit_ln(ctx, xsh, "c", 8, TC, a1, b1r, sqt, lnrows, ln_banks, f"{pfx}ln1_{c}")
        for j in range(8):
            p.do("vector", lambda e, j=j: e.tensor_mul(
                xsh[:, j * TC:(j + 1) * TC], xsh[:, j * TC:(j + 1) * TC], a1[:]),
                reads=[(xsh.name, ("c", j)), a1.r()],
                writes=[(xsh.name, ("c", j))], tag=f"{pfx}y1_{c}_{j}")
            p.do("vector", lambda e, j=j: e.tensor_add(
                xsh[:, j * TC:(j + 1) * TC], xsh[:, j * TC:(j + 1) * TC], b1r[:]),
                reads=[(xsh.name, ("c", j)), b1r.r()],
                writes=[(xsh.name, ("c", j)), (xsh.name, ("y", j))], tag=f"{pfx}y2_{c}_{j}")
        ysl = ys[:, c * TC:(c + 1) * TC].rearrange("(j p) t -> p j t", p=128)
        p.dma("sync", ysl, xsh[:].rearrange("p (j t) -> p j t", j=8), sem_y,
              reads=[xsh.r()] + [(xsh.name, ("y", j)) for j in range(8)],
              tag=f"{pfx}ys{c}")
        emit_ln(ctx, xsh, "y", 8, TC, a2, b2r, sqt, lnrows, ln_banks,
                f"{pfx}ln2_{c}")
        for j in range(8):
            jj = j % 2
            p.do("vector", lambda e, j=j, jj=jj: e.tensor_mul(
                sqt[:, jj * TC:(jj + 1) * TC], xsh[:, j * TC:(j + 1) * TC], a2[:]),
                reads=[(xsh.name, ("y", j)), a2.r()],
                writes=[(sqt.name, jj), sqt.r()], tag=f"{pfx}t2a_{c}_{j}")
            p.do("vector", lambda e, j=j, jj=jj: e.tensor_add(
                sqt[:, jj * TC:(jj + 1) * TC], sqt[:, jj * TC:(jj + 1) * TC], b2r[:]),
                reads=[(sqt.name, jj), b2r.r()],
                writes=[(sqt.name, jj)], tag=f"{pfx}t2b_{c}_{j}")
            p.do("vector", lambda e, j=j, jj=jj: e.tensor_scalar(
                xn_sb[:, j * TC:(j + 1) * TC], sqt[:, jj * TC:(jj + 1) * TC],
                gc[:, j:j + 1], bc_[:, j:j + 1], ALU.mult, ALU.add),
                reads=[(sqt.name, jj), gc.r(), bc_.r()],
                writes=[(xn_sb.name, ("c", j))], tag=f"{pfx}xnp_{c}_{j}")
        xnsl = xns[:, c * TC:(c + 1) * TC].rearrange("(j p) t -> p j t", p=128)
        p.dma("sync", xnsl, xn_sb[:].rearrange("p (j t) -> p j t", j=8), sem_xn,
              reads=[xn_sb.r()] + [(xn_sb.name, ("c", j)) for j in range(8)],
              tag=f"{pfx}xns{c}")
        for ts in range(4):
            g = c * 4 + ts
            bR = bank[2]
            for dt in range(8):
                p.do("tensor", lambda e, ts=ts, dt=dt, bR=bR: e.matmul(
                    bR[:, 0:8], xn_sb[:, dt * TC + ts * 128:dt * TC + (ts + 1) * 128],
                    wg_sb[:, dt * 8:(dt + 1) * 8],
                    start=(dt == 0), stop=False),
                    reads=[(xn_sb.name, ("c", dt)), wg_sb.r()],
                    writes=[bR.r()], tag=f"{pfx}rl_{c}_{ts}_{dt}")
            p.do("tensor", lambda e, bR=bR: e.matmul(
                bR[:, 0:8], ctx.ones_row_bf[:], bg_sb[:],
                start=False, stop=True),
                reads=[bg_sb.r(), ctx.ones_row_bf.r()],
                writes=[bR.r()], tag=f"{pfx}rb_{c}_{ts}")
            p.do("scalar", lambda e, g=g, bR=bR: e.copy(
                rout[:, g * 8:(g + 1) * 8], bR[:, 0:8]),
                reads=[bR.r()], writes=[(rout.name, g)], tag=f"{pfx}re_{c}_{ts}")

    NE = 8
    ALLR = [(rout.name, g) for g in range(32)]
    r3 = rout[:].rearrange("p (g k) -> p g k", k=NE)
    c3 = lambda sl: cls[:, sl].rearrange("p (g o) -> p g o", o=1).broadcast_to([128, 32, NE])
    q3 = cmb[:].rearrange("p (g k) -> p g k", k=NE)
    m3 = m1i[:].rearrange("p (g k) -> p g k", k=NE)
    p.do("vector", lambda e: e.reduce_max(
        cls[:, 0:32], r3, axis=AX.X, negate=True),
        reads=ALLR, writes=[(cls.name, "nm")], tag=pfx + "nm")
    p.do("vector", lambda e: e.tensor_tensor(
        r3, r3, c3(slice(0, 32)), ALU.add),
        reads=ALLR + [(cls.name, "nm")], writes=[(rout.name, "e")], tag=pfx + "dsub")
    p.do("scalar", lambda e: e.activation(rout[:], rout[:], AF.Exp),
         reads=[(rout.name, "e")], writes=[(rout.name, "e")], tag=pfx + "exp")
    p.do("vector", lambda e: e.reduce_sum(cls[:, 96:128], r3, axis=AX.X),
         reads=[(rout.name, "e")], writes=[(cls.name, "S")], tag=pfx + "S")
    p.do("vector", lambda e: e.tensor_single_scalar(m1i[:], rout[:], 1.0, ALU.is_lt),
         reads=[(rout.name, "e")], writes=[m1i.r()], tag=pfx + "m1i")
    p.do("vector", lambda e: e.tensor_mul(cmb[:], rout[:], m1i[:]),
         reads=[(rout.name, "e"), m1i.r()], writes=[cmb.r()], tag=pfx + "p2")
    p.do("vector", lambda e: e.reduce_max(cls[:, 32:64], q3, axis=AX.X),
         reads=[cmb.r()], writes=[(cls.name, "m2")], tag=pfx + "m2")
    p.do("vector", lambda e: e.tensor_tensor(
        q3, q3, c3(slice(32, 64)), ALU.subtract),
        reads=[cmb.r(), (cls.name, "m2")], writes=[cmb.r()], tag=pfx + "d2")
    p.do("vector", lambda e: e.tensor_single_scalar(cmb[:], cmb[:], 0.0, ALU.is_ge),
         reads=[cmb.r()], writes=[cmb.r()], tag=pfx + "mk2")
    p.do("vector", lambda e: e.scalar_tensor_tensor(
        m1i[:], m1i[:], -1.0, cmb[:], ALU.mult, ALU.add),
        reads=[m1i.r(), cmb.r()], writes=[m1i.r()], tag=pfx + "msk")
    p.do("vector", lambda e: e.tensor_single_scalar(
        cls[:, 32:64], cls[:, 32:64], 1.0, ALU.add),
        reads=[(cls.name, "m2")], writes=[(cls.name, "m2")], tag=pfx + "dn")
    p.do("vector", lambda e: e.tensor_single_scalar(m1i[:], m1i[:], 1.0, ALU.add),
         reads=[m1i.r()], writes=[m1i.r()], tag=pfx + "msk1")
    p.do("vector", lambda e: e.scalar_tensor_tensor(
        cls[:, 32:64], cls[:, 96:128], 1e-8, cls[:, 32:64], ALU.mult, ALU.add),
        reads=[(cls.name, "S"), (cls.name, "m2")], writes=[(cls.name, "m2")],
        tag=pfx + "dne")
    p.do("vector", lambda e: e.tensor_mul(cmb[:], rout[:], m1i[:]),
         reads=[(rout.name, "e"), m1i.r()], writes=[cmb.r()], tag=pfx + "wmul")
    p.do("vector", lambda e: e.reciprocal(cls[:, 64:96], cls[:, 32:64]),
         reads=[(cls.name, "m2")], writes=[(cls.name, "rc")], tag=pfx + "rc")
    p.do("gpsimd", lambda e: e.tensor_tensor(
        q3, q3, c3(slice(64, 96)), ALU.mult),
        reads=[cmb.r(), (cls.name, "rc")], writes=[(cmb.name, "w")], tag=pfx + "wfin")
    for g in range(32):
        p.do("tensor", lambda e, g=g: e.transpose(
            bank[3][0:8, 0:128], cmb[:, g * 8:(g + 1) * 8], ctx.idn[:]),
            reads=[(cmb.name, "w"), ctx.idn.r()],
            writes=[bank[3].r()], tag=f"{pfx}wtt_{g}")
        p.do("scalar", lambda e, g=g: e.copy(
            w8T[0:8, g * 128:(g + 1) * 128], bank[3][0:8, 0:128]),
            reads=[bank[3].r()], writes=[(w8T.name, ("g", g))],
            tag=f"{pfx}wte_{g}")

    p.claim(xn_sb.name, tag=pfx + "xnclaim")
    p.claim(h_sb.name, tag=pfx + "hclaim")
    W8TALL = [(w8T.name, ("g", g)) for g in range(32)]
    for s in range(4):
        xssl = xns[:, s * TS:(s + 1) * TS].rearrange("(j p) t -> p j t", p=128)
        p.dma("sync", xn_super.rearrange("p (j t) -> p j t", j=8), xssl, sem_xs,
              writes=[(xsh.name, None)], tag=f"{pfx}xs{s}")
        p.do("gpsimd", lambda e: e.memset(acc[:], 0.0), writes=[acc.r()],
             tag=f"{pfx}acc0_{s}")
        for ei in range(8):
            p.dma("sync", b1rows[:], w["b1"][ei], sem_eb,
                  writes=[b1rows.r()], tag=f"{pfx}b1r{s}_{ei}")
            p.do("tensor", lambda e: e.transpose(
                bank[3][:, 0:32], b1rows[:], ctx.idn[0:32, 0:32]),
                reads=[b1rows.r(), ctx.idn.r()], writes=[bank[3].r()],
                tag=f"{pfx}b1t{s}_{ei}")
            p.do("scalar", lambda e: e.copy(b1c[:], bank[3][:, 0:32]),
                 reads=[bank[3].r()], writes=[b1c.r()], tag=f"{pfx}b1e{s}_{ei}")
            p.dma("gpsimd", b2row[:], w["b2"][ei], sem_eb2,
                  writes=[b2row.r()], tag=f"{pfx}b2r{s}_{ei}")
            for fb in range(8):
                buf = fb % 2
                p.dma("gpsimd",
                      w1c[:, buf * 4096:(buf + 1) * 4096].rearrange(
                          "p (j f) -> p j f", j=8),
                      w["W1T"][ei][:, fb * 512:(fb + 1) * 512].rearrange(
                          "(j p) f -> p j f", p=128),
                      sem_w1[buf], writes=[(w1c.name, buf)], tag=f"{pfx}w1_{s}_{ei}_{fb}")
                for ftl in range(4):
                    ft = fb * 4 + ftl
                    for th in range(2):
                        bm = bank[(ft * 2 + th) % 2]
                        for dt in range(8):
                            p.do("tensor", lambda e, dt=dt, ftl=ftl, th=th, buf=buf, bm=bm: e.matmul(
                                bm[:, 0:TC],
                                w1c[:, buf * 4096 + dt * 512 + ftl * 128:buf * 4096 + dt * 512 + (ftl + 1) * 128],
                                xn_super[:, dt * TS + th * TC:dt * TS + (th + 1) * TC],
                                start=(dt == 0), stop=(dt == 7)),
                                reads=[(w1c.name, buf), (xsh.name, None)],
                                writes=[bm.r()], tag=f"{pfx}h_{s}_{ei}_{ft}_{th}_{dt}")
                        p.do("scalar", lambda e, ft=ft, th=th, bm=bm: e.activation(
                            h_sb[:, ft * TS + th * TC:ft * TS + (th + 1) * TC],
                            bm[:, 0:TC], AF.Gelu, bias=b1c[:, ft:ft + 1], scale=1.0),
                            reads=[bm.r(), b1c.r()], writes=[(h_sb.name, (ft, th))],
                            tag=f"{pfx}hev_{s}_{ei}_{ft}_{th}")
            for hf in range(2):
                p.dma("gpsimd",
                      w2h[:].rearrange("p (j n) -> p j n", j=16),
                      w["W2T"][ei][hf * 2048:(hf + 1) * 2048, :].rearrange(
                          "(j p) n -> p j n", p=128),
                      sem_w2, writes=[(w2h.name, None)], tag=f"{pfx}w2_{s}_{ei}_{hf}")
                for th in range(2):
                    p.do("tensor", lambda e, ei=ei, s=s, th=th: e.matmul(
                        bank[2][:, 0:TC], ohot_bf[0:8, ei * 128:(ei + 1) * 128],
                        w8T[0:8, s * TS + th * TC:s * TS + (th + 1) * TC],
                        start=True, stop=True),
                        reads=W8TALL + [ohot_bf.r()],
                        writes=[bank[2].r()], tag=f"{pfx}wbc_{s}_{ei}_{hf}_{th}")
                    p.do("scalar", lambda e: e.copy(wbc_sb[:], bank[2][:, 0:TC]),
                         reads=[bank[2].r()], writes=[wbc_sb.r()],
                         tag=f"{pfx}wbce_{s}_{ei}_{hf}_{th}")
                    for dot in range(8):
                        bz = bank[3 + dot % 2]
                        for fj in range(16):
                            p.do("tensor", lambda e, dot=dot, fj=fj, hf=hf, th=th, bz=bz: e.matmul(
                                bz[:, 0:TC],
                                w2h[:, fj * D + dot * 128:fj * D + (dot + 1) * 128],
                                h_sb[:, (hf * 16 + fj) * TS + th * TC:(hf * 16 + fj) * TS + (th + 1) * TC],
                                start=(fj == 0), stop=(hf == 1 and fj == 15)),
                                reads=[(w2h.name, None), (h_sb.name, (hf * 16 + fj, th))],
                                writes=[bz.r()], tag=f"{pfx}z_{s}_{ei}_{hf}_{th}_{dot}_{fj}")
                        if hf == 0:
                            p.do("tensor", lambda e, dot=dot, bz=bz: e.matmul(
                                bz[:, 0:TC], b2row[0:1, dot * 128:(dot + 1) * 128],
                                onr512[:], start=False, stop=True),
                                reads=[b2row.r(), onr512.r()],
                                writes=[bz.r()], tag=f"{pfx}zb_{s}_{ei}_{th}_{dot}")
                        p.do("vector", lambda e, bz=bz: e.tensor_mul(
                            t1_sb[:], bz[:, 0:TC], wbc_sb[:]),
                            reads=[bz.r(), wbc_sb.r()], writes=[t1_sb.r()],
                            tag=f"{pfx}zt_{s}_{ei}_{hf}_{th}_{dot}")
                        p.do("vector", lambda e, dot=dot, th=th: e.tensor_add(
                            acc[:, dot * TS + th * TC:dot * TS + (th + 1) * TC],
                            acc[:, dot * TS + th * TC:dot * TS + (th + 1) * TC],
                            t1_sb[:]),
                            reads=[(acc.name, (dot, th)), t1_sb.r()],
                            writes=[(acc.name, (dot, th))],
                            tag=f"{pfx}za_{s}_{ei}_{hf}_{th}_{dot}")
        for cl in range(2):
            c = s * 2 + cl
            xsl = xin[:, c * TC:(c + 1) * TC].rearrange("(j p) t -> p j t", p=128)
            p.dma("sync", xsh[:].rearrange("p (j t) -> p j t", j=8), xsl, sem_x2,
                  writes=[(xsh.name, None)], tag=f"{pfx}x2_{c}")
            for j in range(8):
                p.dma("sync", t1_sb[:], ys[j * 128:(j + 1) * 128, c * TC:(c + 1) * TC],
                      sem_y2, writes=[t1_sb.r()], tag=f"{pfx}y2_{c}_{j}")
                p.do("vector", lambda e, j=j: e.tensor_add(
                    xsh[:, j * TC:(j + 1) * TC], xsh[:, j * TC:(j + 1) * TC], t1_sb[:]),
                    reads=[(xsh.name, None), t1_sb.r()],
                    writes=[(xsh.name, ("o", j))], tag=f"{pfx}oa_{c}_{j}")
                p.do("vector", lambda e, j=j, cl=cl: e.tensor_add(
                    xsh[:, j * TC:(j + 1) * TC], xsh[:, j * TC:(j + 1) * TC],
                    acc[:, j * TS + cl * TC:j * TS + (cl + 1) * TC]),
                    reads=[(xsh.name, ("o", j)), (acc.name, (j, cl))],
                    writes=[(xsh.name, ("o", j))], tag=f"{pfx}ob_{c}_{j}")
            osl = xout[:, c * TC:(c + 1) * TC].rearrange("(j p) t -> p j t", p=128)
            p.dma("sync", osl, xsh[:].rearrange("p (j t) -> p j t", j=8), sem_out,
                  reads=[xsh.r()] + [(xsh.name, ("o", j)) for j in range(8)],
                  tag=f"{pfx}out{c}")
    return sem_out




L1F_DRAM_SPECS = [
    ("WgT", [1024, 8], F32), ("bg", [1, 8], F32),
    ("W1T", [8, 1024, 4096], F32), ("W2T", [8, 4096, 1024], F32),
    ("b1", [8, 32, 128], F32), ("b2", [8, 1, 1024], F32),
    ("gamma", [8, 128], F32), ("beta", [8, 128], F32),
]


def build_l1f_standalone():
    nc = bass.Bass("TRN2", target_bir_lowering=False)
    stack = ExitStack()
    with stack:
        ctx = Ctx(nc, stack)
        xin = nc.dram_tensor("xT", [D, T], F32, kind="ExternalInput").ap()
        xout = nc.dram_tensor("yT", [D, T], F32, kind="ExternalOutput").ap()
        ys = nc.dram_tensor("l1y", [D, T], F32, kind="Internal").ap()
        xns = nc.dram_tensor("l1xn", [D, T], BF16, kind="Internal").ap()
        w = {}
        for name, shape, dt in L1F_DRAM_SPECS:
            w[name] = nc.dram_tensor(name, shape, dt, kind="ExternalInput").ap()
        ctx.setup_consts()
        sem_out = emit_l1f(ctx, xin, xout, w, ys, xns, "gf_")
        last = sem_out.inc_ops[-1]
        ctx.p._add_wait(ctx.p.do("sync", lambda e: e.nop(), tag="fin"), last)
        ctx.p.finalize()
    return nc





def emit_attn_core(ctx, pfx, big, qt, kt, vhm, bqch, bkch, outw, obcol,
                   xin, xout, masks4, cinv8, x_sb=None):
    p = ctx.p
    TC = 512
    NCH = T // TC
    bank = ctx.banks
    nm = big.name
    bigf = big.ap.bitcast(F32)
    at = TBuf(big.ap[:, 0:16384], nm)
    oc = TBuf(big.ap[:, 16384:20480], nm)
    ot = TBuf(big.ap[:, 20480:24576], nm)
    kbuf = TBuf(big.ap[0:64, 24576:28672], nm)
    vaug = TBuf(big.ap[:, 28672:30752], nm)
    qbuf = TBuf(big.ap[0:64, 30752:31264], nm)
    if x_sb is None:
        x_sb = ctx.sb(pfx + "cx", [128, 8 * TC], F32)
    bcols = ctx.sb(pfx + "bcols", [128, 16], F32)
    b2x8 = ctx.sb(pfx + "b2x8", [2, 2 * 8], F32)
    rlc = ctx.sb(pfx + "rlc", [128, 1], F32)
    t1c = ctx.sb(pfx + "t1c", [128, TC], F32)
    sel2 = ctx.sb(pfx + "sel2", [2, 256], F32)

    sem_q = p.dma_sem(pfx + "sq")
    sem_k = p.dma_sem(pfx + "sk")
    sem_v = p.dma_sem(pfx + "sv")
    sem_x = p.dma_sem(pfx + "sx")
    sem_out = p.dma_sem(pfx + "so")

    p.do("gpsimd", lambda e: e.memset(sel2[:], 1.0), writes=[sel2.r()], tag=pfx + "sel0")
    p.do("gpsimd", lambda e: e.affine_select(
        out=sel2[:], in_=sel2[:], compare_op=ALU.is_equal, fill=0.0,
        base=0, pattern=[[-1, 2], [0, 128]], channel_multiplier=1),
        reads=[sel2.r()], writes=[sel2.r()], tag=pfx + "sel1")

    p.do("vector", lambda e: e.reduce_max(
        b2x8[:, 0:8], bqch[:].rearrange("p (a b) -> p a b", b=8), axis=AX.X),
        reads=[bqch.r()], writes=[(b2x8.name, "q")], tag=pfx + "bq")
    p.do("vector", lambda e: e.reduce_max(
        b2x8[:, 8:16], bkch[:].rearrange("p (a b) -> p a b", b=8), axis=AX.X),
        reads=[bkch.r()], writes=[(b2x8.name, "k")], tag=pfx + "bk")
    p.do("scalar", lambda e: e.activation(b2x8[:, 0:8], b2x8[:, 0:8], AF.Sqrt),
         reads=[(b2x8.name, "q")], writes=[(b2x8.name, "q")], tag=pfx + "sq1")
    p.do("scalar", lambda e: e.activation(b2x8[:, 8:16], b2x8[:, 8:16], AF.Sqrt),
         reads=[(b2x8.name, "k")], writes=[(b2x8.name, "k")], tag=pfx + "sq2")
    p.do("vector", lambda e: e.tensor_mul(b2x8[:, 0:8], b2x8[:, 0:8], b2x8[:, 8:16]),
         reads=[(b2x8.name, "q"), (b2x8.name, "k")], writes=[(b2x8.name, "q")],
         tag=pfx + "bmul")
    p.do("scalar", lambda e: e.mul(b2x8[:, 0:8], b2x8[:, 0:8], -0.125),
         reads=[(b2x8.name, "q")], writes=[(b2x8.name, "q")], tag=pfx + "bneg")
    for h in range(16):
        half, et = h % 2, h // 2
        p.do("tensor", lambda e, half=half, et=et: e.matmul(
            bank[6][:, 0:1], sel2[:, half * 128:(half + 1) * 128],
            b2x8[:, et:et + 1], start=True, stop=True),
            reads=[(b2x8.name, "q"), sel2.r()], writes=[bank[6].r()],
            tag=f"{pfx}bb{h}")
        p.do("scalar", lambda e, h=h: e.copy(bcols[:, h:h + 1], bank[6][:, 0:1]),
             reads=[bank[6].r()], writes=[(bcols.name, h)], tag=f"{pfx}be{h}")

    p.claim(nm, tag=pfx + "bigclaim")
    p.do("gpsimd", lambda e: e.memset(vaug[:], 1.0), writes=[(nm, "va")],
         tag=pfx + "vone")

    for qc in range(NCH):
        nkt = 4 * qc + 4
        xsl = xin[:, qc * TC:(qc + 1) * TC].rearrange("(j p) t -> p j t", p=128)
        p.dma("sync", x_sb[:].rearrange("p (j t) -> p j t", j=8), xsl, sem_x,
              writes=[x_sb.r()], tag=f"{pfx}x{qc}")
        for h in range(16):
            p.dma("sync", qbuf[:], qt[h * 64:(h + 1) * 64, qc * TC:(qc + 1) * TC],
                  sem_q, writes=[(nm, "qb")], tag=f"{pfx}q_{qc}_{h}")
            p.dma("sync", kbuf[0:64, 0:nkt * 128],
                  kt[h * 64:(h + 1) * 64, 0:nkt * 128],
                  sem_k, writes=[(nm, "kb")], tag=f"{pfx}k_{qc}_{h}")
            p.dma("sync",
                  vaug[:, 0:nkt * 65].rearrange("p (k c) -> p k c", c=65)[:, :, 0:64],
                  vhm[h, 0:nkt * 128, :].rearrange("(k p) c -> p k c", p=128),
                  sem_v, writes=[(nm, ("vdat",))], tag=f"{pfx}v_{qc}_{h}")
            for kti in range(nkt):
                bs = bank[kti % 2]
                p.do("tensor", lambda e, kti=kti, bs=bs: e.matmul(
                    bs[:, 0:TC], kbuf[0:64, kti * 128:(kti + 1) * 128], qbuf[:],
                    start=True, stop=True),
                    reads=[(nm, "kb"), (nm, "qb")], writes=[bs.r()],
                    tag=f"{pfx}s_{qc}_{h}_{kti}")
                p.do("scalar", lambda e, kti=kti, h=h, bs=bs: e.activation(
                    at[:, kti * TC:(kti + 1) * TC], bs[:, 0:TC], AF.Exp,
                    bias=bcols[:, h:h + 1], scale=cinv8[:, 0:1]),
                    reads=[bs.r(), (bcols.name, h), cinv8.r()],
                    writes=[(nm, ("at", kti))], tag=f"{pfx}e_{qc}_{h}_{kti}")
                off = kti - 4 * qc
                if off >= 0:
                    p.do("vector", lambda e, kti=kti, off=off: e.tensor_mul(
                        at[:, kti * TC:(kti + 1) * TC],
                        at[:, kti * TC:(kti + 1) * TC],
                        masks4[:, off * 512:(off + 1) * 512]),
                        reads=[(nm, ("at", kti)), masks4.r()],
                        writes=[(nm, ("at", kti))], tag=f"{pfx}m_{qc}_{h}_{kti}")
            for kti in range(nkt):
                for j in range(4):
                    bo = bank[2 + j]
                    p.do("tensor", lambda e, kti=kti, j=j, bo=bo, nkt=nkt: e.matmul(
                        bo[:, 0:65],
                        at[:, kti * TC + j * 128:kti * TC + (j + 1) * 128],
                        vaug[:, kti * 65:(kti + 1) * 65],
                        start=(kti == 0), stop=(kti == nkt - 1)),
                        reads=[(nm, ("at", kti)), (nm, "va"), (nm, ("vdat",))],
                        writes=[(bank[2 + j].name, None)], tag=f"{pfx}av_{qc}_{h}_{kti}_{j}")
            for j in range(4):
                bo = bank[2 + j]
                p.do("vector", lambda e, bo=bo: e.reciprocal(rlc[:], bo[:, 64:65]),
                     reads=[(bank[2 + j].name, None)], writes=[rlc.r()],
                     tag=f"{pfx}rl_{qc}_{h}_{j}")
                p.do("scalar", lambda e, j=j, h=h, bo=bo: e.activation(
                    oc[:, j * 1024 + h * 64:j * 1024 + (h + 1) * 64], bo[:, 0:64],
                    AF.Identity, bias=ctx.constcol(0.0)[:, 0:1], scale=rlc[:, 0:1]),
                    reads=[(bank[2 + j].name, None), rlc.r(), ctx.constcol(0.0).r()],
                    writes=[(nm, ("oc", j, h))], tag=f"{pfx}oe_{qc}_{h}_{j}")
        for j in range(4):
            for eb in range(8):
                p.do("tensor", lambda e, j=j, eb=eb: e.transpose(
                    ctx.bankT[:, 0:128],
                    oc[:, j * 1024 + eb * 128:j * 1024 + (eb + 1) * 128],
                    ctx.idn_bf[:]),
                    reads=[(nm, ("oc", j, hh)) for hh in range(2 * eb, 2 * eb + 2)]
                    + [ctx.idn_bf.r()],
                    writes=[ctx.bankT.r()], tag=f"{pfx}tr_{qc}_{j}_{eb}")
                p.do("vector", lambda e, j=j, eb=eb: e.tensor_copy(
                    ot[:, eb * 512 + j * 128:eb * 512 + (j + 1) * 128],
                    ctx.bankT[:, 0:128]),
                    reads=[ctx.bankT.r()], writes=[(nm, ("ot", j, eb))],
                    tag=f"{pfx}te_{qc}_{j}_{eb}")
        for dot in range(8):
            bm = bank[dot % 2]
            for eb in range(8):
                p.do("tensor", lambda e, dot=dot, eb=eb, bm=bm: e.matmul(
                    bm[:, 0:TC], outw[:, eb * 1024 + dot * 128:eb * 1024 + (dot + 1) * 128],
                    ot[:, eb * 512:(eb + 1) * 512],
                    start=(eb == 0), stop=(eb == 7)),
                    reads=[outw.r()] + [(nm, ("ot", j, eb)) for j in range(4)],
                    writes=[bm.r()], tag=f"{pfx}op_{qc}_{dot}_{eb}")
            if obcol is not None:
                p.do("scalar", lambda e, dot=dot, bm=bm: e.activation(
                    t1c[:], bm[:, 0:TC], AF.Identity,
                    bias=obcol[:, dot:dot + 1], scale=ctx.constcol(1.0)[:, 0:1]),
                    reads=[bm.r(), obcol.r(), ctx.constcol(1.0).r()],
                    writes=[t1c.r()], tag=f"{pfx}ob_{qc}_{dot}")
            else:
                p.do("scalar", lambda e, bm=bm: e.copy(t1c[:], bm[:, 0:TC]),
                     reads=[bm.r()], writes=[t1c.r()], tag=f"{pfx}oc_{qc}_{dot}")
            p.do("vector", lambda e, dot=dot: e.tensor_add(
                x_sb[:, dot * TC:(dot + 1) * TC], x_sb[:, dot * TC:(dot + 1) * TC],
                t1c[:]),
                reads=[(x_sb.name, None), t1c.r()],
                writes=[(x_sb.name, ("o", dot))], tag=f"{pfx}oa_{qc}_{dot}")
        osl = xout[:, qc * TC:(qc + 1) * TC].rearrange("(j p) t -> p j t", p=128)
        p.dma("sync", osl, x_sb[:].rearrange("p (j t) -> p j t", j=8), sem_out,
              reads=[x_sb.r()] + [(x_sb.name, ("o", j)) for j in range(8)],
              tag=f"{pfx}out{qc}")
    return sem_out




def emit_sin_chain(ctx, src_ap, src_reads, dst_ap, dst_writes, qs, ki, kf,
                   shift, scale_in, tag):
    p = ctx.p
    INV2PI = 1.0 / (2 * np.pi)
    c2pi = ctx.constcol(2 * np.pi)
    czero = ctx.constcol(0.0)
    p.do("vector", lambda e: e.tensor_scalar(
        qs[:], src_ap, scale_in * INV2PI, shift * INV2PI, ALU.mult, ALU.add),
        reads=src_reads, writes=[qs.r()], tag=tag + "q")
    p.do("vector", lambda e: e.tensor_copy(ki[:], qs[:]),
         reads=[qs.r()], writes=[ki.r()], tag=tag + "ki")
    p.do("vector", lambda e: e.tensor_copy(kf[:], ki[:]),
         reads=[ki.r()], writes=[kf.r()], tag=tag + "kf")
    p.do("vector", lambda e: e.tensor_sub(qs[:], qs[:], kf[:]),
         reads=[qs.r(), kf.r()], writes=[qs.r()], tag=tag + "d")
    p.do("scalar", lambda e: e.activation(
        dst_ap, qs[:], AF.Sin, bias=czero[:, 0:1], scale=c2pi[:, 0:1]),
        reads=[qs.r(), czero.r(), c2pi.r()], writes=dst_writes, tag=tag + "sin")



def emit_l0a(ctx, xin, xout, w, qts, kts, vhms, pfx):
    p = ctx.p
    TC = 512
    NCH = T // TC
    bank = ctx.banks

    big = ctx.sb(pfx + "big", [128, 32768], BF16)
    nm = big.name
    wA = ctx.sb(pfx + "wA", [128, 8 * D], BF16)
    wB = ctx.sb(pfx + "wB", [128, 8 * D], BF16)
    q1r = ctx.sb(pfx + "q1r", [128, 4 * D], BF16)
    q1rT = ctx.sb(pfx + "q1rT", [128, 8 * TC], BF16)
    q2t = ctx.sb(pfx + "q2t", [128, 8 * TC], BF16)
    ctab = ctx.sb(pfx + "ctab", [128, TC], F32)
    stab = ctx.sb(pfx + "stab", [128, TC], F32)
    mscr = ctx.sb(pfx + "mscr", [128, 2 * TC], F32)
    qs = ctx.sb(pfx + "qs", [128, TC], F32)
    ki = ctx.sb(pfx + "ki", [128, TC], I32)
    kf = ctx.sb(pfx + "kf", [128, TC], F32)
    sqq = ctx.sb(pfx + "sqq", [128, TC], F32)
    throw = ctx.sb(pfx + "throw", [1, TC], F32)
    trow = ctx.sb(pfx + "trow", [1, 128], F32)
    e2 = ctx.sb(pfx + "e2", [128, 2], F32)
    blk2 = ctx.sb(pfx + "blk2", [2, 128], F32)
    bvibc = ctx.sb(pfx + "bvibc", [128, D], F32)
    bvirow = ctx.sb(pfx + "bvirow", [1, D], F32)
    bqch = ctx.sb(pfx + "bqch", [2, 64], F32)
    bkch = ctx.sb(pfx + "bkch", [2, 64], F32)
    masks4 = ctx.sb(pfx + "masks4", [128, 4 * 512], BF16)
    ax = ctx.sb(pfx + "ax", [128, 8 * TC], F32)
    sqt = ctx.sb(pfx + "sqt", [128, 2 * TC], F32)
    a1 = ctx.sb(pfx + "a1", [128, TC], F32)
    b1r = ctx.sb(pfx + "b1r", [128, TC], F32)
    lnrows = ctx.sb(pfx + "lnrows", [1, 3 * TC], F32)
    bqic = load_cols(ctx, pfx + "bqic", w["bqi"], 128, 8, None, bank[6])
    bkic = load_cols(ctx, pfx + "bkic", w["bki"], 128, 8, None, bank[6])
    obc = load_cols(ctx, pfx + "obc", w["outb"], 128, 8, None, bank[6])
    outw = ctx.sb(pfx + "outw", [128, 8 * D], BF16)
    cinv8 = ctx.constcol(0.125)

    sem_x = p.dma_sem(pfx + "sx")
    sem_w = [p.dma_sem(pfx + f"sw{i}") for i in range(2)]
    sem_qt = p.dma_sem(pfx + "sqt")
    sem_kt = p.dma_sem(pfx + "skt")
    sem_v = [p.dma_sem(pfx + f"svv{i}") for i in range(4)]
    ln_banks = {"s1": bank[4], "s2": bank[5], "bc": bank[6]}

    p.do("gpsimd", lambda e: e.memset(masks4[:], 1.0), writes=[masks4.r()], tag=pfx + "mk0")
    p.do("gpsimd", lambda e: e.affine_select(
        out=masks4[:], in_=masks4[:], compare_op=ALU.is_ge, fill=0.0,
        base=0, pattern=[[-128, 4], [1, 512]], channel_multiplier=-1),
        reads=[masks4.r()], writes=[masks4.r()], tag=pfx + "mk1")
    p.do("gpsimd", lambda e: e.memset(blk2[:], 1.0), writes=[blk2.r()], tag=pfx + "b20")
    p.do("gpsimd", lambda e: e.affine_select(
        out=blk2[:], in_=blk2[:], compare_op=ALU.is_equal, fill=0.0,
        base=0, pattern=[[-1, 2], [0, 64]], channel_multiplier=1),
        reads=[blk2.r()], writes=[blk2.r()], tag=pfx + "b21")
    p.do("tensor", lambda e: e.transpose(bank[6][:, 0:2], blk2[:], ctx.idn[0:2, 0:2]),
         reads=[blk2.r(), ctx.idn.r()], writes=[bank[6].r()], tag=pfx + "e2t")
    p.do("scalar", lambda e: e.copy(e2[:], bank[6][:, 0:2]),
         reads=[bank[6].r()], writes=[e2.r()], tag=pfx + "e2e")
    p.do("gpsimd", lambda e: e.iota(throw[:], pattern=[[1, TC]], base=0,
                                    channel_multiplier=0,
                                    allow_small_or_imprecise_dtypes=True),
         writes=[throw.r()], tag=pfx + "thi")
    LOG1E4 = float(np.log(10000.0))
    p.do("scalar", lambda e: e.activation(
        throw[:], throw[:], AF.Exp, bias=ctx.constcol(2.0 * LOG1E4 / 1024.0)[0:1, 0:1],
        scale=ctx.constcol(-2.0 * LOG1E4 / 1024.0)[0:1, 0:1]),
        reads=[throw.r(), ctx.constcol(2.0 * LOG1E4 / 1024.0).r(),
               ctx.constcol(-2.0 * LOG1E4 / 1024.0).r()],
        writes=[throw.r()], tag=pfx + "the")
    p.dma("sync", bvirow[:], w["bvi"], p.dma_sem(pfx + "sbvi"),
          writes=[bvirow.r()], tag=pfx + "bvi")
    for eh in range(2):
        p.do("tensor", lambda e, eh=eh: e.matmul(
            bank[6][:, 0:512], ctx.ones_row[:], bvirow[0:1, eh * 512:(eh + 1) * 512],
            start=True, stop=True),
            reads=[bvirow.r(), ctx.ones_row.r()], writes=[bank[6].r()],
            tag=f"{pfx}bvb{eh}")
        p.do("scalar", lambda e, eh=eh: e.copy(
            bvibc[:, eh * 512:(eh + 1) * 512], bank[6][:, 0:512]),
            reads=[bank[6].r()], writes=[bvibc.r()], tag=f"{pfx}bve{eh}")

    for c in range(NCH):
        xsl = xin[:, c * TC:(c + 1) * TC].rearrange("(j p) t -> p j t", p=128)
        p.dma("sync", ax[:].rearrange("p (j t) -> p j t", j=8), xsl, sem_x,
              writes=[ax.r()], tag=f"{pfx}x{c}")
        emit_ln(ctx, ax, "c", 8, TC, a1, b1r, sqt, lnrows, ln_banks, f"{pfx}ln_{c}")
        for j in range(8):
            p.do("vector", lambda e, j=j: e.tensor_mul(
                ax[:, j * TC:(j + 1) * TC], ax[:, j * TC:(j + 1) * TC], a1[:]),
                reads=[(ax.name, ("c", j)), a1.r()],
                writes=[(ax.name, ("c", j))], tag=f"{pfx}y1_{c}_{j}")
            p.do("vector", lambda e, j=j, c=c: e.tensor_tensor(
                big[:, j * T + c * TC:j * T + (c + 1) * TC],
                ax[:, j * TC:(j + 1) * TC], b1r[:], ALU.add),
                reads=[(ax.name, ("c", j)), b1r.r()],
                writes=[(nm, ("xn", c, j))], tag=f"{pfx}y2_{c}_{j}")

    for path in ("q", "k", "v"):
        wa_d, wb_d = w[path + "T"], w[path + "iT"]
        p.dma("gpsimd", wA[:].rearrange("p (j e) -> p j e", j=8),
              wa_d.rearrange("(j p) e -> p j e", p=128), sem_w[0],
              writes=[wA.r()], tag=f"{pfx}wA_{path}")
        p.dma("gpsimd", wB[:].rearrange("p (j e) -> p j e", j=8),
              wb_d.rearrange("(j p) e -> p j e", p=128), sem_w[1],
              writes=[wB.r()], tag=f"{pfx}wB_{path}")
        for c in range(NCH):
            for tl in range(4):
                for eh in range(2):
                    bm = bank[eh]
                    for dt in range(8):
                        p.do("tensor", lambda e, dt=dt, eh=eh, c=c, tl=tl, bm=bm: e.matmul(
                            bm[:, 0:TC],
                            big[:, dt * T + c * TC + tl * 128:dt * T + c * TC + (tl + 1) * 128],
                            wA[:, dt * D + eh * 512:dt * D + (eh + 1) * 512],
                            start=(dt == 0), stop=(dt == 7)),
                            reads=[(nm, ("xn", c, dt)), wA.r()],
                            writes=[bm.r()], tag=f"{pfx}p1_{path}_{c}_{tl}_{eh}_{dt}")
                p.do("gpsimd", lambda e, c=c, tl=tl: e.iota(
                    trow[:], pattern=[[1, 128]], base=c * TC + tl * 128,
                    channel_multiplier=0, allow_small_or_imprecise_dtypes=True),
                    writes=[trow.r()], tag=f"{pfx}ti_{path}_{c}_{tl}")
                p.do("tensor", lambda e: e.matmul(
                    bank[2][:, 0:TC], trow[:], throw[:], start=True, stop=True),
                    reads=[trow.r(), throw.r()], writes=[bank[2].r()],
                    tag=f"{pfx}ang_{path}_{c}_{tl}")
                emit_sin_chain(ctx, bank[2][:, 0:TC], [bank[2].r()],
                               stab[:], [stab.r()], qs, ki, kf, 0.0, 1.0,
                               f"{pfx}ss_{path}_{c}_{tl}_")
                emit_sin_chain(ctx, bank[2][:, 0:TC], [bank[2].r()],
                               ctab[:], [ctab.r()], qs, ki, kf, np.pi / 2, 1.0,
                               f"{pfx}cs_{path}_{c}_{tl}_")
                def ev(b, s0):
                    return b[:, 0:TC].rearrange("p (i two) -> p i two", two=2)[:, :, s0]
                for eh in range(2):
                    bm = bank[eh]
                    xe = ev(bm, 0)
                    xo = ev(bm, 1)
                    ch = ctab[:, eh * 256:(eh + 1) * 256]
                    sh = stab[:, eh * 256:(eh + 1) * 256]
                    base = tl * D + eh * TC
                    ye = q1r[:, base:base + TC].rearrange("p (i two) -> p i two", two=2)[:, :, 0]
                    yo = q1r[:, base:base + TC].rearrange("p (i two) -> p i two", two=2)[:, :, 1]
                    p.do("vector", lambda e, xe=xe, ch=ch: e.tensor_tensor(
                        mscr[:, 0:256], xe, ch, ALU.mult),
                        reads=[bank[eh].r(), ctab.r()], writes=[(mscr.name, 0)],
                        tag=f"{pfx}r1_{path}_{c}_{tl}_{eh}")
                    p.do("vector", lambda e, xo=xo, sh=sh: e.tensor_tensor(
                        mscr[:, 256:512], xo, sh, ALU.mult),
                        reads=[bank[eh].r(), stab.r()], writes=[(mscr.name, 1)],
                        tag=f"{pfx}r2_{path}_{c}_{tl}_{eh}")
                    p.do("vector", lambda e, ye=ye: e.tensor_tensor(
                        ye, mscr[:, 0:256], mscr[:, 256:512], ALU.add),
                        reads=[(mscr.name, 0), (mscr.name, 1)],
                        writes=[(q1r.name, (tl, eh, 0))], tag=f"{pfx}r3_{path}_{c}_{tl}_{eh}")
                    p.do("vector", lambda e, xe=xe, sh=sh: e.tensor_tensor(
                        mscr[:, 0:256], xe, sh, ALU.mult),
                        reads=[bank[eh].r(), stab.r()], writes=[(mscr.name, 0)],
                        tag=f"{pfx}r4_{path}_{c}_{tl}_{eh}")
                    p.do("vector", lambda e, xo=xo, ch=ch: e.tensor_tensor(
                        mscr[:, 256:512], xo, ch, ALU.mult),
                        reads=[bank[eh].r(), ctab.r()], writes=[(mscr.name, 1)],
                        tag=f"{pfx}r5_{path}_{c}_{tl}_{eh}")
                    p.do("vector", lambda e, yo=yo: e.tensor_tensor(
                        yo, mscr[:, 256:512], mscr[:, 0:256], ALU.subtract),
                        reads=[(mscr.name, 0), (mscr.name, 1)],
                        writes=[(q1r.name, (tl, eh, 1))], tag=f"{pfx}r6_{path}_{c}_{tl}_{eh}")
                for eb in range(8):
                    p.do("tensor", lambda e, tl=tl, eb=eb: e.transpose(
                        ctx.bankT[:, 0:128],
                        q1r[:, tl * D + eb * 128:tl * D + (eb + 1) * 128],
                        ctx.idn_bf[:]),
                        reads=[(q1r.name, (tl, eb // 4, 0)), (q1r.name, (tl, eb // 4, 1)),
                               ctx.idn_bf.r()],
                        writes=[ctx.bankT.r()], tag=f"{pfx}t1_{path}_{c}_{tl}_{eb}")
                    p.do("vector", lambda e, tl=tl, eb=eb: e.tensor_copy(
                        q1rT[:, eb * TC + tl * 128:eb * TC + (tl + 1) * 128],
                        ctx.bankT[:, 0:128]),
                        reads=[ctx.bankT.r()], writes=[(q1rT.name, (eb, tl))],
                        tag=f"{pfx}t2_{path}_{c}_{tl}_{eb}")
            RQ1RT = [(q1rT.name, (eb, tl)) for eb in range(8) for tl in range(4)]
            if path in ("q", "k"):
                bic = bqic if path == "q" else bkic
                bch = bqch if path == "q" else bkch
                for et in range(8):
                    bm = bank[4 + et % 2]
                    for dt2 in range(8):
                        p.do("tensor", lambda e, et=et, dt2=dt2, bm=bm: e.matmul(
                            bm[:, 0:TC], wB[:, dt2 * D + et * 128:dt2 * D + (et + 1) * 128],
                            q1rT[:, dt2 * TC:(dt2 + 1) * TC],
                            start=(dt2 == 0), stop=(dt2 == 7)),
                            reads=[wB.r()] + RQ1RT, writes=[bm.r()],
                            tag=f"{pfx}p2_{path}_{c}_{et}_{dt2}")
                    p.do("scalar", lambda e, et=et, bm=bm: e.activation(
                        q2t[:, et * TC:(et + 1) * TC], bm[:, 0:TC], AF.Identity,
                        bias=bic[:, et:et + 1], scale=ctx.constcol(1.0)[:, 0:1]),
                        reads=[bm.r(), bic.r(), ctx.constcol(1.0).r()],
                        writes=[(q2t.name, et)], tag=f"{pfx}ev_{path}_{c}_{et}")
                    p.do("scalar", lambda e, et=et: e.activation(
                        sqq[:], q2t[:, et * TC:(et + 1) * TC], AF.Square),
                        reads=[(q2t.name, et)], writes=[sqq.r()],
                        tag=f"{pfx}sq_{path}_{c}_{et}")
                    p.do("tensor", lambda e: e.matmul(
                        bank[6][0:2, 0:TC], e2[:], sqq[:], start=True, stop=True),
                        reads=[e2.r(), sqq.r()], writes=[bank[6].r()],
                        tag=f"{pfx}n2_{path}_{c}_{et}")
                    p.do("vector", lambda e, et=et, c=c, bch=bch: e.reduce_max(
                        bch[:, et * 8 + c:et * 8 + c + 1], bank[6][0:2, 0:TC], axis=AX.X),
                        reads=[bank[6].r()], writes=[(bch.name, (et, c))],
                        tag=f"{pfx}nr_{path}_{c}_{et}")
                dst = qts if path == "q" else kts
                semd = sem_qt if path == "q" else sem_kt
                p.dma("sync", dst[:, c * TC:(c + 1) * TC].rearrange("(j p) t -> p j t", p=128),
                      q2t[:].rearrange("p (j t) -> p j t", j=8), semd,
                      reads=[q2t.r()] + [(q2t.name, et) for et in range(8)],
                      tag=f"{pfx}st_{path}_{c}")
            else:
                if c == 0:
                    p.claim(q2t.name, tag=pfx + "vq2claim")
                for tl in range(4):
                    for eh in range(2):
                        bm = bank[4 + eh]
                        for dt2 in range(8):
                            p.do("tensor", lambda e, tl=tl, eh=eh, dt2=dt2, bm=bm: e.matmul(
                                bm[:, 0:TC],
                                q1rT[:, dt2 * TC + tl * 128:dt2 * TC + (tl + 1) * 128],
                                wB[:, dt2 * D + eh * 512:dt2 * D + (eh + 1) * 512],
                                start=(dt2 == 0), stop=(dt2 == 7)),
                                reads=[wB.r()] + RQ1RT, writes=[bm.r()],
                                tag=f"{pfx}v2_{c}_{tl}_{eh}_{dt2}")
                        p.do("vector", lambda e, tl=tl, eh=eh, bm=bm: e.tensor_tensor(
                            q2t[:, tl * D + eh * TC:tl * D + (eh + 1) * TC],
                            bm[:, 0:TC], bvibc[:, eh * TC:(eh + 1) * TC], ALU.add),
                            reads=[bm.r(), bvibc.r()],
                            writes=[(q2t.name, ("v", tl, eh))], tag=f"{pfx}ve_{c}_{tl}_{eh}")
                    p.dma("sync",
                          vhms[:, c * TC + tl * 128:c * TC + (tl + 1) * 128, :].rearrange(
                              "h (o p) d -> p (o h) d", p=128),
                          q2t[:, tl * D:(tl + 1) * D].rearrange("p (h d) -> p h d", d=64),
                          sem_v[tl],
                          reads=[(q2t.name, ("v", tl, 0)), (q2t.name, ("v", tl, 1))],
                          tag=f"{pfx}vs_{c}_{tl}")

    return emit_attn_core(ctx, pfx + "c_", big, qts, kts, vhms, bqch, bkch,
                          outw_load(ctx, pfx, outw, w["outwT"]), obc,
                          xin, xout, masks4, cinv8, x_sb=ax)


def outw_load(ctx, pfx, outw, dram_ap):
    ctx.p.dma("gpsimd", outw[:].rearrange("p (j e) -> p j e", j=8),
              dram_ap.rearrange("(j p) e -> p j e", p=128),
              ctx.p.dma_sem(pfx + "sow"), writes=[outw.r()], tag=pfx + "outw")
    return outw


L0A_DRAM_SPECS = [
    ("qT", [1024, 1024], F32), ("kT", [1024, 1024], F32), ("vT", [1024, 1024], F32),
    ("qiT", [1024, 1024], F32), ("kiT", [1024, 1024], F32), ("viT", [1024, 1024], F32),
    ("bqi", [8, 128], F32), ("bki", [8, 128], F32), ("bvi", [1, 1024], F32),
    ("outwT", [1024, 1024], F32), ("outb", [8, 128], F32),
]


def build_l0a_standalone():
    nc = bass.Bass("TRN2", target_bir_lowering=False)
    stack = ExitStack()
    with stack:
        ctx = Ctx(nc, stack)
        xin = nc.dram_tensor("xT", [D, T], F32, kind="ExternalInput").ap()
        xout = nc.dram_tensor("yT", [D, T], F32, kind="ExternalOutput").ap()
        qts = nc.dram_tensor("l0qt", [D, T], BF16, kind="Internal").ap()
        kts = nc.dram_tensor("l0kt", [D, T], BF16, kind="Internal").ap()
        vhms = nc.dram_tensor("l0vh", [16, T, 64], BF16, kind="Internal").ap()
        w = {}
        for name, shape, dt in L0A_DRAM_SPECS:
            w[name] = nc.dram_tensor(name, shape, dt, kind="ExternalInput").ap()
        ctx.setup_consts()
        sem_out = emit_l0a(ctx, xin, xout, w, qts, kts, vhms, "aa_")
        last = sem_out.inc_ops[-1]
        ctx.p._add_wait(ctx.p.do("sync", lambda e: e.nop(), tag="fin"), last)
        ctx.p.finalize()
    return nc





def emit_l1a(ctx, xin, xout, w, qts, kts, vhms, pfx, stage=99):
    p = ctx.p
    TC = 512
    NCH = T // TC
    bank = ctx.banks

    big = ctx.sb(pfx + "big", [128, 32768], BF16)
    nm = big.name
    wq_sb = ctx.sb(pfx + "wq", [128, 8 * D], BF16)
    wdkv = ctx.sb(pfx + "wdkv", [128, 8 * 256], BF16)
    wuk = ctx.sb(pfx + "wuk", [128, 2 * D], BF16)
    wuv = ctx.sb(pfx + "wuv", [128, 2 * D], BF16)
    lat_sb = ctx.sb(pfx + "lat", [128, 32 * 256], BF16)
    latnT = ctx.sb(pfx + "latnT", [128, 2 * T], BF16)
    q1r = ctx.sb(pfx + "q1r", [128, 4 * D], BF16)
    q2t = ctx.sb(pfx + "q2t", [128, 8 * TC], BF16)
    ctab = ctx.sb(pfx + "ctab", [128, 128], F32)
    stab = ctx.sb(pfx + "stab", [128, 128], F32)
    mscr = ctx.sb(pfx + "mscr", [128, 2 * TC], F32)
    qs = ctx.sb(pfx + "qs", [128, 128], F32)
    ki = ctx.sb(pfx + "ki", [128, 128], I32)
    kf = ctx.sb(pfx + "kf", [128, 128], F32)
    sqq = ctx.sb(pfx + "sqq", [128, TC], F32)
    sqlat = ctx.sb(pfx + "sqlat", [128, 256], F32)
    stat = ctx.sb(pfx + "stat", [128, 6 * 32], F32)
    gb_bc = ctx.sb(pfx + "gbbc", [128, 2 * 256], F32)
    grow = ctx.sb(pfx + "grow", [1, 2 * 256], F32)
    frep = ctx.sb(pfx + "frep", [1, 128], F32)
    mrow = ctx.sb(pfx + "mrow", [1, 128], F32)
    orow = ctx.sb(pfx + "orow", [1, 128], F32)
    trow = ctx.sb(pfx + "trow", [1, 128], F32)
    e2 = ctx.sb(pfx + "e2", [128, 2], F32)
    blk2 = ctx.sb(pfx + "blk2", [2, 128], F32)
    bqch = ctx.sb(pfx + "bqch", [2, 64], F32)
    bkch = ctx.sb(pfx + "bkch", [2, 64], F32)
    masks4 = ctx.sb(pfx + "masks4", [128, 4 * 512], BF16)
    ax = ctx.sb(pfx + "ax", [128, 8 * TC], F32)
    sqt = ctx.sb(pfx + "sqt", [128, 2 * TC], F32)
    a1 = ctx.sb(pfx + "a1", [128, TC], F32)
    b1r = ctx.sb(pfx + "b1r", [128, TC], F32)
    lnrows = ctx.sb(pfx + "lnrows", [1, 3 * TC], F32)
    outw = TBuf(wq_sb.ap, wq_sb.name)
    cinv8 = ctx.constcol(0.125)

    sem_x = p.dma_sem(pfx + "sx")
    sem_qt = p.dma_sem(pfx + "sqt")
    sem_kt = p.dma_sem(pfx + "skt")
    sem_v = [p.dma_sem(pfx + f"svv{i}") for i in range(4)]
    ln_banks = {"s1": bank[4], "s2": bank[5], "bc": bank[6]}

    for name, t, jn in (("Wq", wq_sb, 8), ("Wdkv", wdkv, 8), ("Wuk", wuk, 2),
                        ("Wuv", wuv, 2)):
        p.dma("gpsimd", t[:].rearrange("p (j e) -> p j e", j=jn),
              w[name].rearrange("(j p) e -> p j e", p=128),
              p.dma_sem(pfx + "s" + name), writes=[t.r()], tag=pfx + name)
    p.dma("sync", grow[:], w["lngb"], p.dma_sem(pfx + "sgb"),
          writes=[grow.r()], tag=pfx + "lngb")
    for i in range(2):
        p.do("tensor", lambda e, i=i: e.matmul(
            bank[6][:, 0:256], ctx.ones_row[:], grow[0:1, i * 256:(i + 1) * 256],
            start=True, stop=True),
            reads=[grow.r(), ctx.ones_row.r()], writes=[bank[6].r()],
            tag=f"{pfx}gb{i}")
        p.do("scalar", lambda e, i=i: e.copy(gb_bc[:, i * 256:(i + 1) * 256],
                                             bank[6][:, 0:256]),
             reads=[bank[6].r()], writes=[gb_bc.r()], tag=f"{pfx}ge{i}")
    p.do("gpsimd", lambda e: e.memset(masks4[:], 1.0), writes=[masks4.r()], tag=pfx + "mk0")
    p.do("gpsimd", lambda e: e.affine_select(
        out=masks4[:], in_=masks4[:], compare_op=ALU.is_ge, fill=0.0,
        base=0, pattern=[[-128, 4], [1, 512]], channel_multiplier=-1),
        reads=[masks4.r()], writes=[masks4.r()], tag=pfx + "mk1")
    p.do("gpsimd", lambda e: e.memset(blk2[:], 1.0), writes=[blk2.r()], tag=pfx + "b20")
    p.do("gpsimd", lambda e: e.affine_select(
        out=blk2[:], in_=blk2[:], compare_op=ALU.is_equal, fill=0.0,
        base=0, pattern=[[-1, 2], [0, 64]], channel_multiplier=1),
        reads=[blk2.r()], writes=[blk2.r()], tag=pfx + "b21")
    p.do("tensor", lambda e: e.transpose(bank[6][:, 0:2], blk2[:], ctx.idn[0:2, 0:2]),
         reads=[blk2.r(), ctx.idn.r()], writes=[bank[6].r()], tag=pfx + "e2t")
    p.do("scalar", lambda e: e.copy(e2[:], bank[6][:, 0:2]),
         reads=[bank[6].r()], writes=[e2.r()], tag=pfx + "e2e")
    LOG1E4 = float(np.log(10000.0))
    p.do("gpsimd", lambda e: e.iota(frep[:], pattern=[[0, 4], [1, 32]], base=0,
                                    channel_multiplier=0,
                                    allow_small_or_imprecise_dtypes=True),
         writes=[frep.r()], tag=pfx + "fi")
    p.do("scalar", lambda e: e.activation(
        frep[:], frep[:], AF.Exp, bias=ctx.constcol(0.0)[0:1, 0:1],
        scale=ctx.constcol(-LOG1E4 / 32.0)[0:1, 0:1]),
        reads=[frep.r(), ctx.constcol(0.0).r(), ctx.constcol(-LOG1E4 / 32.0).r()],
        writes=[frep.r()], tag=pfx + "fe")

    for c in range(NCH):
        xsl = xin[:, c * TC:(c + 1) * TC].rearrange("(j p) t -> p j t", p=128)
        p.dma("sync", ax[:].rearrange("p (j t) -> p j t", j=8), xsl, sem_x,
              writes=[ax.r()], tag=f"{pfx}x{c}")
        emit_ln(ctx, ax, "c", 8, TC, a1, b1r, sqt, lnrows, ln_banks, f"{pfx}ln_{c}")
        for j in range(8):
            p.do("vector", lambda e, j=j: e.tensor_mul(
                ax[:, j * TC:(j + 1) * TC], ax[:, j * TC:(j + 1) * TC], a1[:]),
                reads=[(ax.name, ("c", j)), a1.r()],
                writes=[(ax.name, ("c", j))], tag=f"{pfx}y1_{c}_{j}")
            p.do("vector", lambda e, j=j, c=c: e.tensor_tensor(
                big[:, j * T + c * TC:j * T + (c + 1) * TC],
                ax[:, j * TC:(j + 1) * TC], b1r[:], ALU.add),
                reads=[(ax.name, ("c", j)), b1r.r()],
                writes=[(nm, ("xn", c, j))], tag=f"{pfx}y2_{c}_{j}")

    if stage < 1:
        return sem_x
    for c in range(NCH):
        for tl in range(4):
            g = c * 4 + tl
            bm = bank[g % 2]
            for dt in range(8):
                p.do("tensor", lambda e, dt=dt, c=c, tl=tl, bm=bm: e.matmul(
                    bm[:, 0:256],
                    big[:, dt * T + c * TC + tl * 128:dt * T + c * TC + (tl + 1) * 128],
                    wdkv[:, dt * 256:(dt + 1) * 256],
                    start=(dt == 0), stop=(dt == 7)),
                    reads=[(nm, ("xn", c, dt)), wdkv.r()],
                    writes=[bm.r()], tag=f"{pfx}lat_{g}_{dt}")
            p.do("scalar", lambda e, g=g, bm=bm: e.copy(
                lat_sb[:, g * 256:(g + 1) * 256], bm[:, 0:256]),
                reads=[bm.r()], writes=[(lat_sb.name, g)], tag=f"{pfx}le_{g}")
            if stage >= 0.8:
                p.do("vector", lambda e, g=g: e.reduce_sum(
                    stat[:, g:g + 1], lat_sb[:, g * 256:(g + 1) * 256], axis=AX.X),
                    reads=[(lat_sb.name, g)], writes=[(stat.name, ("s", g))],
                    tag=f"{pfx}ls_{g}")
            if stage >= 0.9:
                p.do("vector", lambda e, g=g: e.tensor_mul(
                    sqlat[:], lat_sb[:, g * 256:(g + 1) * 256],
                    lat_sb[:, g * 256:(g + 1) * 256]),
                    reads=[(lat_sb.name, g)], writes=[sqlat.r()], tag=f"{pfx}lq_{g}")
                p.do("vector", lambda e, g=g: e.reduce_sum(
                    stat[:, 32 + g:32 + g + 1], sqlat[:], axis=AX.X),
                    reads=[sqlat.r()], writes=[(stat.name, ("m", g))], tag=f"{pfx}lm_{g}")
    if stage < 1.5:
        return sem_x
    RS = [(stat.name, ("s", g)) for g in range(32)]
    RM = [(stat.name, ("m", g)) for g in range(32)]
    p.do("vector", lambda e: e.tensor_scalar_mul(stat[:, 64:96], stat[:, 0:32], 1.0 / 256),
         reads=RS, writes=[(stat.name, "mean")], tag=pfx + "mean")
    p.do("scalar", lambda e: e.mul(stat[:, 96:128], stat[:, 32:64], 1.0 / 256),
         reads=RM, writes=[(stat.name, "msn")], tag=pfx + "msn")
    p.do("gpsimd", lambda e: e.tensor_mul(
        stat[:, 128:160], stat[:, 64:96], stat[:, 64:96]),
        reads=[(stat.name, "mean")], writes=[(stat.name, "negm2")], tag=pfx + "m2c")
    p.do("vector", lambda e: e.tensor_sub(stat[:, 128:160], stat[:, 96:128],
                                          stat[:, 128:160]),
         reads=[(stat.name, "negm2"), (stat.name, "msn")],
         writes=[(stat.name, "var")], tag=pfx + "var")
    p.do("scalar", lambda e: e.activation(
        stat[:, 128:160], stat[:, 128:160], AF.Sqrt,
        bias=ctx.constcol(1e-5)[:, 0:1], scale=1.0),
        reads=[(stat.name, "var"), ctx.constcol(1e-5).r()],
        writes=[(stat.name, "sd")], tag=pfx + "sd")
    p.do("vector", lambda e: e.reciprocal(stat[:, 96:128], stat[:, 128:160]),
         reads=[(stat.name, "sd")], writes=[(stat.name, "rstd")], tag=pfx + "rstd")
    p.do("gpsimd", lambda e: e.tensor_mul(
        stat[:, 64:96], stat[:, 64:96], stat[:, 96:128]),
        reads=[(stat.name, "mean"), (stat.name, "rstd")],
        writes=[(stat.name, "bpos")], tag=pfx + "bpos")
    p.do("scalar", lambda e: e.mul(stat[:, 64:96], stat[:, 64:96], -1.0),
         reads=[(stat.name, "bpos")], writes=[(stat.name, "bcol")], tag=pfx + "bcol")
    if stage < 1.8:
        return sem_x
    for g in range(32):
        p.do("scalar", lambda e, g=g: e.activation(
            sqlat[:], lat_sb[:, g * 256:(g + 1) * 256], AF.Identity,
            bias=stat[:, 64 + g:64 + g + 1], scale=stat[:, 96 + g:96 + g + 1]),
            reads=[(lat_sb.name, g), (stat.name, "bcol"), (stat.name, "rstd")],
            writes=[sqlat.r()], tag=f"{pfx}af_{g}")
        p.do("vector", lambda e: e.tensor_mul(sqlat[:], sqlat[:], gb_bc[:, 0:256]),
             reads=[sqlat.r(), gb_bc.r()], writes=[sqlat.r()], tag=f"{pfx}ag_{g}")
        p.do("vector", lambda e, g=g: e.tensor_tensor(
            lat_sb[:, g * 256:(g + 1) * 256], sqlat[:], gb_bc[:, 256:512], ALU.add),
            reads=[sqlat.r(), gb_bc.r()], writes=[(lat_sb.name, ("n", g))],
            tag=f"{pfx}ab_{g}")
        for lb in range(2):
            p.do("tensor", lambda e, g=g, lb=lb: e.transpose(
                ctx.bankT[:, 0:128],
                lat_sb[:, g * 256 + lb * 128:g * 256 + (lb + 1) * 128],
                ctx.idn_bf[:]),
                reads=[(lat_sb.name, ("n", g)), ctx.idn_bf.r()],
                writes=[ctx.bankT.r()], tag=f"{pfx}lt_{g}_{lb}")
            p.do("vector", lambda e, g=g, lb=lb: e.tensor_copy(
                latnT[:, lb * T + g * 128:lb * T + (g + 1) * 128],
                ctx.bankT[:, 0:128]),
                reads=[ctx.bankT.r()], writes=[(latnT.name, (lb, g))],
                tag=f"{pfx}le2_{g}_{lb}")
    RLAT = [(latnT.name, (lb, g)) for lb in range(2) for g in range(32)]

    def rope_tables(c, tagp):
        p.do("gpsimd", lambda e, c=c: e.iota(
            mrow[:], pattern=[[128, 4], [0, 32]], base=c * TC,
            channel_multiplier=0, allow_small_or_imprecise_dtypes=True),
            writes=[mrow.r()], tag=tagp + "mi")
        p.do("vector", lambda e: e.tensor_mul(orow[:], frep[:], mrow[:]),
             reads=[frep.r(), mrow.r()], writes=[orow.r()], tag=tagp + "or")
        p.do("gpsimd", lambda e: e.iota(
            trow[:], pattern=[[1, 128]], base=0, channel_multiplier=0,
            allow_small_or_imprecise_dtypes=True),
            writes=[trow.r()], tag=tagp + "ti")
        p.do("tensor", lambda e: e.matmul(
            bank[2][:, 0:128], trow[:], frep[:], start=True, stop=False),
            reads=[trow.r(), frep.r()], writes=[bank[2].r()], tag=tagp + "a1")
        p.do("tensor", lambda e: e.matmul(
            bank[2][:, 0:128], ctx.ones_row[:], orow[:], start=False, stop=True),
            reads=[orow.r(), ctx.ones_row.r()], writes=[bank[2].r()], tag=tagp + "a2")
        emit_sin_chain(ctx, bank[2][:, 0:128], [bank[2].r()],
                       stab[:], [stab.r()], qs, ki, kf, 0.0, 1.0, tagp + "s_")
        emit_sin_chain(ctx, bank[2][:, 0:128], [bank[2].r()],
                       ctab[:], [ctab.r()], qs, ki, kf, np.pi / 2, 1.0, tagp + "c_")

    def rope_apply(psrc, preads, tl, tagp):
        x1 = psrc.rearrange("p (h i) -> p h i", i=64)[:, :, 0:32]
        x2 = psrc.rearrange("p (h i) -> p h i", i=64)[:, :, 32:64]
        ch = ctab[:, tl * 32:(tl + 1) * 32].rearrange(
            "p (o i) -> p o i", o=1).broadcast_to([128, 16, 32])
        sh = stab[:, tl * 32:(tl + 1) * 32].rearrange(
            "p (o i) -> p o i", o=1).broadcast_to([128, 16, 32])
        dst = q1r[:, tl * D:(tl + 1) * D].rearrange("p (h i) -> p h i", i=64)
        y1 = dst[:, :, 0:32]
        y2 = dst[:, :, 32:64]
        m1 = mscr[:, 0:TC].rearrange("p (h i) -> p h i", i=32)
        m2 = mscr[:, TC:2 * TC].rearrange("p (h i) -> p h i", i=32)
        p.do("vector", lambda e: e.tensor_tensor(m1, x1, ch, ALU.mult),
             reads=preads + [ctab.r()], writes=[(mscr.name, 0)], tag=tagp + "1")
        p.do("vector", lambda e: e.tensor_tensor(m2, x2, sh, ALU.mult),
             reads=preads + [stab.r()], writes=[(mscr.name, 1)], tag=tagp + "2")
        p.do("vector", lambda e: e.tensor_tensor(y1, m1, m2, ALU.subtract),
             reads=[(mscr.name, 0), (mscr.name, 1)],
             writes=[(q1r.name, (tl, 0))], tag=tagp + "3")
        p.do("vector", lambda e: e.tensor_tensor(m1, x1, sh, ALU.mult),
             reads=preads + [stab.r()], writes=[(mscr.name, 0)], tag=tagp + "4")
        p.do("vector", lambda e: e.tensor_tensor(m2, x2, ch, ALU.mult),
             reads=preads + [ctab.r()], writes=[(mscr.name, 1)], tag=tagp + "5")
        p.do("vector", lambda e: e.tensor_tensor(y2, m1, m2, ALU.add),
             reads=[(mscr.name, 0), (mscr.name, 1)],
             writes=[(q1r.name, (tl, 1))], tag=tagp + "6")

    def transpose_tile(tl, c, tagp):
        for eb in range(8):
            p.do("tensor", lambda e, tl=tl, eb=eb: e.transpose(
                ctx.bankT[:, 0:128],
                q1r[:, tl * D + eb * 128:tl * D + (eb + 1) * 128],
                ctx.idn_bf[:]),
                reads=[(q1r.name, (tl, 0)), (q1r.name, (tl, 1)), ctx.idn_bf.r()],
                writes=[ctx.bankT.r()], tag=f"{tagp}t1_{eb}")
            p.do("vector", lambda e, tl=tl, eb=eb: e.tensor_copy(
                q2t[:, eb * TC + tl * 128:eb * TC + (tl + 1) * 128],
                ctx.bankT[:, 0:128]),
                reads=[ctx.bankT.r()], writes=[(q2t.name, ("t", eb, tl))],
                tag=f"{tagp}t2_{eb}")

    def norms_store(c, bch, dst, semd, tagp):
        for et in range(8):
            p.do("scalar", lambda e, et=et: e.activation(
                sqq[:], q2t[:, et * TC:(et + 1) * TC], AF.Square),
                reads=[(q2t.name, ("t", et, tl2)) for tl2 in range(4)],
                writes=[sqq.r()], tag=f"{tagp}sq_{et}")
            p.do("tensor", lambda e: e.matmul(
                bank[6][0:2, 0:TC], e2[:], sqq[:], start=True, stop=True),
                reads=[e2.r(), sqq.r()], writes=[bank[6].r()], tag=f"{tagp}n2_{et}")
            p.do("vector", lambda e, et=et, c=c, bch=bch: e.reduce_max(
                bch[:, et * 8 + c:et * 8 + c + 1], bank[6][0:2, 0:TC], axis=AX.X),
                reads=[bank[6].r()], writes=[(bch.name, (et, c))], tag=f"{tagp}nr_{et}")
        p.dma("sync", dst[:, c * TC:(c + 1) * TC].rearrange("(j p) t -> p j t", p=128),
              q2t[:].rearrange("p (j t) -> p j t", j=8), semd,
              reads=[q2t.r()] + [(q2t.name, ("t", eb, tl2))
                                 for eb in range(8) for tl2 in range(4)],
              tag=f"{tagp}st")

    if stage < 2:
        return p.dma_sems[-1] if False else sem_x
    for c in range(NCH):
        rope_tables(c, f"{pfx}rt_{c}_")
        if c > 0:
            p.claim(q1r.name, tag=f"{pfx}clq_{c}")
        for tl in range(4):
            for eh in range(2):
                bm = bank[eh]
                for dt in range(8):
                    p.do("tensor", lambda e, dt=dt, eh=eh, c=c, tl=tl, bm=bm: e.matmul(
                        bm[:, 0:TC],
                        big[:, dt * T + c * TC + tl * 128:dt * T + c * TC + (tl + 1) * 128],
                        wq_sb[:, dt * D + eh * 512:dt * D + (eh + 1) * 512],
                        start=(dt == 0), stop=(dt == 7)),
                        reads=[(nm, ("xn", c, dt)), wq_sb.r()],
                        writes=[bm.r()], tag=f"{pfx}q_{c}_{tl}_{eh}_{dt}")
            p.do("scalar", lambda e, tl=tl: e.copy(sqt[:, 0:TC], bank[0][:, 0:TC]),
                 reads=[bank[0].r()], writes=[(sqt.name, 0)], tag=f"{pfx}qe0_{c}_{tl}")
            p.do("scalar", lambda e, tl=tl: e.copy(sqt[:, TC:2 * TC], bank[1][:, 0:TC]),
                 reads=[bank[1].r()], writes=[(sqt.name, 1)], tag=f"{pfx}qe1_{c}_{tl}")
            rope_apply(sqt[:], [(sqt.name, 0), (sqt.name, 1)], tl, f"{pfx}qr_{c}_{tl}_")
            transpose_tile(tl, c, f"{pfx}qt_{c}_{tl}_")
        norms_store(c, bqch, qts, sem_qt, f"{pfx}qn_{c}_")
        if stage < 3:
            continue
        for tl in range(4):
            for eh in range(2):
                bm = bank[eh]
                for lb in range(2):
                    p.do("tensor", lambda e, lb=lb, eh=eh, c=c, tl=tl, bm=bm: e.matmul(
                        bm[:, 0:TC],
                        latnT[:, lb * T + c * TC + tl * 128:lb * T + c * TC + (tl + 1) * 128],
                        wuk[:, lb * D + eh * 512:lb * D + (eh + 1) * 512],
                        start=(lb == 0), stop=(lb == 1)),
                        reads=RLAT + [wuk.r()],
                        writes=[bm.r()], tag=f"{pfx}k_{c}_{tl}_{eh}_{lb}")
            p.do("scalar", lambda e, tl=tl: e.copy(sqt[:, 0:TC], bank[0][:, 0:TC]),
                 reads=[bank[0].r()], writes=[(sqt.name, 0)], tag=f"{pfx}ke0_{c}_{tl}")
            p.do("scalar", lambda e, tl=tl: e.copy(sqt[:, TC:2 * TC], bank[1][:, 0:TC]),
                 reads=[bank[1].r()], writes=[(sqt.name, 1)], tag=f"{pfx}ke1_{c}_{tl}")
            rope_apply(sqt[:], [(sqt.name, 0), (sqt.name, 1)], tl, f"{pfx}kr_{c}_{tl}_")
            transpose_tile(tl, c, f"{pfx}kt_{c}_{tl}_")
        norms_store(c, bkch, kts, sem_kt, f"{pfx}kn_{c}_")
        if stage < 4:
            continue
        p.claim(q1r.name, tag=f"{pfx}clv_{c}")
        for tl in range(4):
            for eh in range(2):
                bm = bank[4 + eh]
                for lb in range(2):
                    p.do("tensor", lambda e, lb=lb, eh=eh, c=c, tl=tl, bm=bm: e.matmul(
                        bm[:, 0:TC],
                        latnT[:, lb * T + c * TC + tl * 128:lb * T + c * TC + (tl + 1) * 128],
                        wuv[:, lb * D + eh * 512:lb * D + (eh + 1) * 512],
                        start=(lb == 0), stop=(lb == 1)),
                        reads=RLAT + [wuv.r()],
                        writes=[bm.r()], tag=f"{pfx}v_{c}_{tl}_{eh}_{lb}")
                p.do("scalar", lambda e, tl=tl, eh=eh, bm=bm: e.copy(
                    q1r[:, tl * D + eh * TC:tl * D + (eh + 1) * TC], bm[:, 0:TC]),
                    reads=[bm.r()], writes=[(q1r.name, ("v", tl, eh))],
                    tag=f"{pfx}vev_{c}_{tl}_{eh}")
            p.dma("sync",
                  vhms[:, c * TC + tl * 128:c * TC + (tl + 1) * 128, :].rearrange(
                      "h (o p) d -> p (o h) d", p=128),
                  q1r[:, tl * D:(tl + 1) * D].rearrange("p (h d) -> p h d", d=64),
                  sem_v[tl],
                  reads=[(q1r.name, ("v", tl, 0)), (q1r.name, ("v", tl, 1))],
                  tag=f"{pfx}vs_{c}_{tl}")

    if stage < 5:
        return {1: sem_x, 2: sem_qt, 3: sem_kt, 4: sem_v[3]}[min(stage, 4)]
    return emit_attn_core(ctx, pfx + "c_", big, qts, kts, vhms, bqch, bkch,
                          outw_load(ctx, pfx, outw, w["Wo"]), None,
                          xin, xout, masks4, cinv8, x_sb=ax)


L1A_DRAM_SPECS = [
    ("Wq", [1024, 1024], F32), ("Wdkv", [1024, 256], F32),
    ("Wuk", [256, 1024], F32), ("Wuv", [256, 1024], F32),
    ("Wo", [1024, 1024], F32), ("lngb", [1, 512], F32),
]


def build_l1a_standalone():
    nc = bass.Bass("TRN2", target_bir_lowering=False)
    stack = ExitStack()
    with stack:
        ctx = Ctx(nc, stack)
        xin = nc.dram_tensor("xT", [D, T], F32, kind="ExternalInput").ap()
        xout = nc.dram_tensor("yT", [D, T], F32, kind="ExternalOutput").ap()
        qts = nc.dram_tensor("l1qt", [D, T], BF16, kind="Internal").ap()
        kts = nc.dram_tensor("l1kt", [D, T], BF16, kind="Internal").ap()
        vhms = nc.dram_tensor("l1vh", [16, T, 64], BF16, kind="Internal").ap()
        w = {}
        for name, shape, dt in L1A_DRAM_SPECS:
            w[name] = nc.dram_tensor(name, shape, dt, kind="ExternalInput").ap()
        ctx.setup_consts()
        import os
        sem_out = emit_l1a(ctx, xin, xout, w, qts, kts, vhms, "ba_",
                           stage=float(os.environ.get("L1A_STAGE", "99")))
        last = sem_out.inc_ops[-1]
        ctx.p._add_wait(ctx.p.do("sync", lambda e: e.nop(), tag="fin"), last)
        ctx.p.finalize()
    return nc



def _prep_l0a(inputs):
    wqi, wki, wvi = np.split(inputs["L0_attn_in_proj_w"], 3, axis=0)
    bqi, bki, bvi = np.split(inputs["L0_attn_in_proj_b"], 3)
    return {
        "qT": np.ascontiguousarray(inputs["L0_attn_wq"].T),
        "kT": np.ascontiguousarray(inputs["L0_attn_wk"].T),
        "vT": np.ascontiguousarray(inputs["L0_attn_wv"].T),
        "qiT": np.ascontiguousarray(wqi.T), "kiT": np.ascontiguousarray(wki.T),
        "viT": np.ascontiguousarray(wvi.T),
        "bqi": np.ascontiguousarray(bqi.reshape(8, 128)),
        "bki": np.ascontiguousarray(bki.reshape(8, 128)),
        "bvi": np.ascontiguousarray(bvi.reshape(1, 1024)),
        "outwT": np.ascontiguousarray(inputs["L0_attn_out_w"].T),
        "outb": np.ascontiguousarray(inputs["L0_attn_out_b"].reshape(8, 128)),
    }


def _prep_l0f(inputs):
    return {
        "rw17": np.ascontiguousarray(np.concatenate(
            [inputs["L0_ffn_router_w"].T, inputs["L0_ffn_shared_gate_w"]], axis=1)),
        "Wg": inputs["L0_ffn_Wg"], "Wu": inputs["L0_ffn_Wu"], "Wd": inputs["L0_ffn_Wd"],
        "Sg": inputs["L0_ffn_Sg"], "Su": inputs["L0_ffn_Su"], "Sd": inputs["L0_ffn_Sd"],
    }


def _prep_l1a(inputs):
    return {
        "Wq": inputs["L1_attn_W_q"], "Wdkv": inputs["L1_attn_W_dkv"],
        "Wuk": inputs["L1_attn_W_uk"], "Wuv": inputs["L1_attn_W_uv"],
        "Wo": inputs["L1_attn_W_o"],
        "lngb": np.ascontiguousarray(np.concatenate(
            [inputs["L1_attn_ln_gamma"], inputs["L1_attn_ln_beta"]]).reshape(1, 512)),
    }


def _prep_l1f(inputs):
    return {
        "WgT": np.ascontiguousarray(inputs["L1_ffn_Wg"].T),
        "bg": np.ascontiguousarray(inputs["L1_ffn_bg"].reshape(1, 8)),
        "W1T": np.ascontiguousarray(inputs["L1_ffn_W1"].transpose(0, 2, 1)),
        "W2T": np.ascontiguousarray(inputs["L1_ffn_W2"].transpose(0, 2, 1)),
        "b1": np.ascontiguousarray(inputs["L1_ffn_b1"].reshape(8, 32, 128)),
        "b2": np.ascontiguousarray(inputs["L1_ffn_b2"].reshape(8, 1, 1024)),
        "gamma": np.ascontiguousarray(inputs["L1_ffn_gamma"].reshape(8, 128)),
        "beta": np.ascontiguousarray(inputs["L1_ffn_beta"].reshape(8, 128)),
    }


def _prep_l2a(inputs):
    return {
        "wq": np.ascontiguousarray(inputs["L2_attn_W_q"].T),
        "wk": np.ascontiguousarray(inputs["L2_attn_W_k"].T),
        "wv": np.ascontiguousarray(inputs["L2_attn_W_v"].T),
    }


def _prep_qmlp(L, inputs):
    pf = f"L{L}_ffn_"
    return {
        "fc1_q": np.ascontiguousarray(inputs[pf + "fc1_q"].astype(np.int32)),
        "fc1_scales": inputs[pf + "fc1_scales"],
        "fc1_bias": np.ascontiguousarray(inputs[pf + "fc1_bias"].reshape(32, 128)),
        "fc2_q": np.ascontiguousarray(inputs[pf + "fc2_q"].astype(np.int32)),
        "fc2_scales": inputs[pf + "fc2_scales"],
        "fc2_bias": np.ascontiguousarray(inputs[pf + "fc2_bias"].reshape(8, 128)),
        "gamma": np.ascontiguousarray(inputs[pf + "ln_gamma"].reshape(8, 128)),
        "beta": np.ascontiguousarray(inputs[pf + "ln_beta"].reshape(8, 128)),
    }


def _prep_l3a(inputs):
    return {
        "Wk": inputs["L3_attn_Wk"], "Wv": inputs["L3_attn_Wv"],
        "Wo": inputs["L3_attn_Wo"],
        "freqs": np.ascontiguousarray(inputs["L3_attn_freqs"].reshape(1, 1024)),
        "gate": np.ascontiguousarray(inputs["L3_attn_gate"].reshape(8, 2, 2)),
    }


_PHASES = [
    ("l0a", build_l0a_standalone, _prep_l0a),
    ("l0f", build_l0f_standalone, _prep_l0f),
    ("l1a", build_l1a_standalone, _prep_l1a),
    ("l1f", build_l1f_standalone, _prep_l1f),
    ("l2a", build_l2attn_standalone, _prep_l2a),
    ("l2f", build_qmlp_standalone, lambda i: _prep_qmlp(2, i)),
    ("l3a", build_l3a_standalone, _prep_l3a),
    ("l3f", build_qmlp_standalone, lambda i: _prep_qmlp(3, i)),
]

_NC_CACHE = {}

_FUSED_SPECS = {
    "l0a": L0A_DRAM_SPECS, "l0f": L0F_DRAM_SPECS, "l1a": L1A_DRAM_SPECS,
    "l1f": L1F_DRAM_SPECS, "l2a": L2A_DRAM_SPECS,
    "l2f": QMLP_DRAM_SPECS, "l3a": L3A_DRAM_SPECS, "l3f": QMLP_DRAM_SPECS,
}


def build_fused(nphase=8, pstart=0):
    nc = bass.Bass("TRN2", target_bir_lowering=False)
    stack = ExitStack()
    with stack:
        ctx = Ctx(nc, stack)
        p = ctx.p
        xT = nc.dram_tensor("xT", [D, T], F32, kind="ExternalInput").ap()
        yT = nc.dram_tensor("yT", [D, T], F32, kind="ExternalOutput").ap()
        x0 = nc.dram_tensor("x0", [D, T], F32, kind="Internal").ap()
        x1 = nc.dram_tensor("x1", [D, T], F32, kind="Internal").ap()
        qts = nc.dram_tensor("qts", [D, T], BF16, kind="Internal").ap()
        kts = nc.dram_tensor("kts", [D, T], BF16, kind="Internal").ap()
        vhms = nc.dram_tensor("vhms", [16, T, 64], BF16, kind="Internal").ap()
        l3st = nc.dram_tensor("l3st", [D, T], BF16, kind="Internal").ap()
        l1y = nc.dram_tensor("l1y", [D, T], F32, kind="Internal").ap()
        l1xn = nc.dram_tensor("l1xn", [D, T], BF16, kind="Internal").ap()
        wmaps = {}
        for ph in ("l0a", "l0f", "l1a", "l1f", "l2a", "l2f", "l3a", "l3f"):
            wmaps[ph] = {}
            for name, shape, dt in _FUSED_SPECS[ph]:
                wmaps[ph][name] = nc.dram_tensor(
                    f"{ph}_{name}", shape, dt, kind="ExternalInput").ap()
        ctx.setup_consts()
        LOG1E4 = float(np.log(10000.0))
        for v in (0.0, 1.0, 0.5, 0.125, 2 * np.pi, 1e-5,
                  2.0 * LOG1E4 / 1024.0, -2.0 * LOG1E4 / 1024.0, -LOG1E4 / 32.0):
            ctx.constcol(v)
        seq = [
            ("l0a", lambda i, o: emit_l0a(ctx, i, o, wmaps["l0a"], qts, kts, vhms, "l0a_")),
            ("l0f", lambda i, o: emit_l0f(ctx, i, o, wmaps["l0f"], "l0f_")),
            ("l1a", lambda i, o: emit_l1a(ctx, i, o, wmaps["l1a"], qts, kts, vhms, "l1a_")),
            ("l1f", lambda i, o: emit_l1f(ctx, i, o, wmaps["l1f"], l1y, l1xn, "l1f_")),
            ("l2a", lambda i, o: emit_l2attn(ctx, i, o, wmaps["l2a"], "l2a_")),
            ("l2f", lambda i, o: emit_qmlp(ctx, i, o, wmaps["l2f"], "l2f_")),
            ("l3a", lambda i, o: emit_l3a(ctx, i, o, wmaps["l3a"], l3st, "l3a_")),
            ("l3f", lambda i, o: emit_qmlp(ctx, i, o, wmaps["l3f"], "l3f_")),
        ]
        cur = xT
        import os as _os
        nobar = _os.environ.get("NOBAR", "") == "1"
        seq = seq[pstart:pstart + nphase]
        for i, (phn, emitfn) in enumerate(seq):
            out = yT if i == len(seq) - 1 else (x0 if i % 2 == 0 else x1)
            ctx.push_phase()
            sem_last = emitfn(cur, out)
            ctx.pop_phase()
            if nobar:
                last = sem_last.inc_ops[-1]
                ctx.p._add_wait(ctx.p.do("sync", lambda e: e.nop(), tag="fin"), last)
            else:
                p.barrier(f"bar{i}")
            cur = out
        ctx.p.finalize()
    return nc


def kernel(**inputs):
    x = np.asarray(inputs["L0_attn_x"], dtype=np.float32)
    if "fused" not in _NC_CACHE:
        _NC_CACHE["fused"] = build_fused()
    nc = _NC_CACHE["fused"]
    wall = {}
    for (name, builder, prep), ph in zip(_PHASES, ("l0a", "l0f", "l1a", "l1f",
                                                   "l2a", "l2f", "l3a", "l3f")):
        for k, v in prep(inputs).items():
            wall[f"{ph}_{k}"] = v
    in_maps = [dict(wall, xT=np.ascontiguousarray(x[c].T)) for c in range(NCORE)]
    res = run_bass_kernel_spmd(nc, in_maps, core_ids=list(range(NCORE)))
    return np.stack([np.asarray(res.results[c]["yT"]).T
                     for c in range(NCORE)]).astype(np.float32)


# version 04
# speedup vs baseline: 1.0806x; 1.0806x over previous
import math
from bisect import bisect_left
from contextlib import ExitStack

import numpy as np
import concourse.bass as bass
import concourse.mybir as mybir
from concourse.bass_utils import run_bass_kernel_spmd

F32 = mybir.dt.float32
BF16 = mybir.dt.bfloat16
I32 = mybir.dt.int32
AF = mybir.ActivationFunctionType
ALU = mybir.AluOpType
AX = mybir.AxisListType

T = 4096
D = 1024
NCORE = 8

ENGINES = ("tensor", "vector", "scalar", "gpsimd", "sync")



class Sem:
    __slots__ = ("h", "name", "is_dma", "inc_idx", "inc_ops", "sid")
    _next = [0]

    def __init__(self, h, name, is_dma):
        self.h = h
        self.name = name
        self.is_dma = is_dma
        self.inc_idx = []
        self.inc_ops = []
        self.sid = Sem._next[0]
        Sem._next[0] += 1


class Op:
    __slots__ = ("eng", "idx", "fn", "waits", "incs", "reads", "writes",
                 "is_dma", "vc", "tag", "sem", "sem_pos")

    def __init__(self, eng, idx, fn, reads, writes, tag):
        self.eng = eng
        self.idx = idx
        self.fn = fn
        self.waits = []
        self.incs = []
        self.reads = reads
        self.writes = writes
        self.is_dma = False
        self.vc = None
        self.tag = tag
        self.sem = None
        self.sem_pos = -1


def _conflict(a, b):
    if a[0] != b[0]:
        return False
    return a[1] is None or b[1] is None or a[1] == b[1]


class Prog:
    def __init__(self, nc, stack, check=True):
        self.nc = nc
        self.stack = stack
        self.check = check
        self.ops = {e: [] for e in ENGINES}
        self.eng_sem = {e: self.new_sem("sem_" + e, is_dma=False) for e in ENGINES}
        self.dma_sems = []
        self.writers = {}
        self.readers = {}
        self.n_ops = 0
        self._dma_pool = []
        self._dma_pool_idx = 0
        self._pending_spoke = {}

    def new_sem(self, name, is_dma=False):
        h = self.stack.enter_context(self.nc.semaphore(name))
        s = Sem(h, name, is_dma)
        if is_dma:
            self.dma_sems.append(s)
        return s

    def dma_sem(self, name):
        if self._dma_pool_idx < len(self._dma_pool):
            s = self._dma_pool[self._dma_pool_idx]
            self._dma_pool_idx += 1
            return s
        s = self.new_sem(name, is_dma=True)
        self._dma_pool.append(s)
        self._dma_pool_idx += 1
        return s

    @staticmethod
    def _join(vc, other):
        for k, v in other.items():
            if vc.get(k, -1) < v:
                vc[k] = v

    @staticmethod
    def _done_key(op):
        if op.is_dma:
            return (op.sem.sid, op.sem_pos)
        return (op.eng, op.idx)

    def _happened(self, vc, op):
        k, v = self._done_key(op)
        return vc.get(k, -1) >= v

    def do(self, eng, fn, reads=(), writes=(), tag="", _dma_sem=None):
        ops = self.ops[eng]
        op = Op(eng, len(ops), fn, list(reads), list(writes), tag)
        vc = {}
        if ops:
            prev = ops[-1]
            vc = dict(prev.vc)
            if not prev.is_dma:
                vc[eng] = prev.idx
        op.vc = vc
        spoke = self._pending_spoke.pop(eng, None)
        if spoke is not None:
            self._add_wait(op, spoke)

        if _dma_sem is not None:
            op.is_dma = True
            op.sem = _dma_sem
            op.sem_pos = len(_dma_sem.inc_ops)

        needed = {}
        for r in op.reads:
            for w in self._find(self.writers, r):
                if not self._happened(vc, w):
                    needed[id(w)] = w
        for wreg in op.writes:
            for w in self._find(self.writers, wreg):
                if not self._happened(vc, w):
                    needed[id(w)] = w
            for rd in self._find(self.readers, wreg):
                if rd is not op and not self._happened(vc, rd):
                    needed[id(rd)] = rd
        for dep in needed.values():
            self._add_wait(op, dep)

        if op.is_dma:
            if self.check:
                for e in _dma_sem.inc_ops:
                    if not self._happened(op.vc, e):
                        raise RuntimeError(
                            f"DMA sem {_dma_sem.name} not serialized: "
                            f"'{tag}' not after '{e.tag}'")
            _dma_sem.inc_ops.append(op)
            op.incs.append(_dma_sem)

        for r in op.reads:
            self._commit(self.readers, r, op)
        for wreg in op.writes:
            self._commit(self.writers, wreg, op)
            rl = self.readers.get(wreg[0])
            if rl:
                rl[:] = [(reg, o) for (reg, o) in rl
                         if not ((reg == wreg or wreg[1] is None)
                                 and self._happened(op.vc, o))]
        ops.append(op)
        self.n_ops += 1
        return op

    def _commit(self, table, region, op):
        lst = table.setdefault(region[0], [])
        lst[:] = [(reg, o) for (reg, o) in lst
                  if not ((reg == region or region[1] is None)
                          and self._happened(op.vc, o))]
        lst.append((region, op))

    @staticmethod
    def _find(table, region):
        out = []
        for reg, o in table.get(region[0], ()):
            if _conflict(reg, region):
                out.append(o)
        return out

    def _add_wait(self, op, dep):
        if dep.is_dma:
            sem = dep.sem
            val = 16 * (dep.sem_pos + 1)
            self._join(op.vc, dep.vc)
            if op.vc.get(sem.sid, -1) < dep.sem_pos:
                op.vc[sem.sid] = dep.sem_pos
        else:
            sem = self.eng_sem[dep.eng]
            pos = bisect_left(sem.inc_idx, dep.idx)
            if pos == len(sem.inc_idx):
                dep.incs.append(sem)
                sem.inc_idx.append(dep.idx)
            val = pos + 1
            self._join(op.vc, dep.vc)
            if op.vc.get(dep.eng, -1) < dep.idx:
                op.vc[dep.eng] = dep.idx
        op.waits.append((sem, val))

    def dma(self, queue, out, in_, sem, reads=(), writes=(), tag=""):
        def fn(eng, out=out, in_=in_):
            return eng.dma_start(out=out, in_=in_)
        return self.do(queue, fn, reads=reads, writes=writes, tag=tag,
                       _dma_sem=sem)

    def claim(self, name, tag="claim"):
        return self.do("gpsimd", lambda e: e.nop(), writes=[(name, None)], tag=tag)

    def barrier(self, tag="barrier"):
        if not hasattr(self, "_bar_counts"):
            self._bar_counts = {}
        toks = {}
        for e in ENGINES:
            if e in ("vector", "sync"):
                continue
            for o in reversed(self.ops[e]):
                if not o.is_dma:
                    toks[e] = o
                    break
        changed = [s for s in self.dma_sems
                   if s.inc_ops and self._bar_counts.get(s.sid) != len(s.inc_ops)]
        for s in changed:
            self._bar_counts[s.sid] = len(s.inc_ops)
        hub = self.do("vector", lambda en: en.nop(), tag=f"{tag}/hub")
        for tk in toks.values():
            self._add_wait(hub, tk)
        for s in changed:
            self._add_wait(hub, s.inc_ops[-1])
        for e in ENGINES:
            if e != "vector":
                self._pending_spoke[e] = hub
        self.writers.clear()
        self.readers.clear()

    def finalize(self):
        blk = self.stack.enter_context(self.nc.Block())

        def run_stream(engine_obj, ops):
            for op in ops:
                tgt = {}
                for sem, v in op.waits:
                    cur = tgt.get(sem.sid)
                    if cur is None or cur[1] < v:
                        tgt[sem.sid] = (sem, v)
                for sem, v in tgt.values():
                    engine_obj.wait_ge(sem.h, v)
                ins = op.fn(engine_obj)
                for sem in op.incs:
                    assert ins is not None, f"op {op.tag} incs but no instr"
                    ins.then_inc(sem.h, 16 if sem.is_dma else 1)

        for eng in ENGINES:
            ops = self.ops[eng]
            getattr(blk, eng)(lambda e, ops=ops: run_stream(e, ops))
        return blk


class TBuf:
    __slots__ = ("ap", "name")

    def __init__(self, ap, name):
        self.ap = ap
        self.name = name

    def __getitem__(self, sl):
        return self.ap[sl]

    def r(self, key=None):
        return (self.name, key)


class Ctx:

    def __init__(self, nc, stack, check=True):
        self.nc = nc
        self.stack = stack
        self.p = Prog(nc, stack, check=check)
        self._consts = {}
        self._const_seq = 0
        self._phase_stack = None
        self._phase_names = []
        self.banks = {}
        self._sb_names = set()
        for i in range(7):
            self.banks[i] = self.psum(f"bank{i}", [128, 512], F32)
        self.bankT = self.psum("bankT", [128, 1024], BF16)
        self._setup = False

    def sb(self, name, shape, dtype, outer=False):
        assert name not in self._sb_names, name
        self._sb_names.add(name)
        st = self.stack if (outer or self._phase_stack is None) else self._phase_stack
        if not outer and self._phase_stack is not None:
            self._phase_names.append(name)
        ap = st.enter_context(self.nc.sbuf_tensor(name, shape, dtype))
        return TBuf(ap, name)

    def push_phase(self):
        self.p._dma_pool_idx = 0
        assert self._phase_stack is None
        self._phase_stack = ExitStack()
        self._phase_stack.__enter__()
        self._phase_names = []
        self._ohots = {}

    def pop_phase(self):
        self._phase_stack.__exit__(None, None, None)
        self._phase_stack = None
        for n in self._phase_names:
            self._sb_names.discard(n)
        self._phase_names = []

    def psum(self, name, shape, dtype):
        ap = self.stack.enter_context(self.nc.psum_tensor(name, shape, dtype))
        return TBuf(ap, name)

    def setup_consts(self):
        p = self.p
        self.ones_col = self.sb("c_ones_col", [128, 1], F32, outer=True)
        self.ones_col_bf = self.sb("c_ones_col_bf", [128, 1], BF16, outer=True)
        self.ones_row = self.sb("c_ones_row", [1, 128], F32, outer=True)
        self.ones_row_bf = self.sb("c_ones_row_bf", [1, 128], BF16, outer=True)
        self.idn = self.sb("c_idn", [128, 128], F32, outer=True)
        self.idn_bf = self.sb("c_idn_bf", [128, 128], BF16, outer=True)
        g = "gpsimd"
        for t in (self.ones_col, self.ones_col_bf, self.ones_row, self.ones_row_bf):
            p.do(g, lambda e, t=t: e.memset(t[:], 1.0), writes=[t.r()], tag="const")
        for t in (self.idn, self.idn_bf):
            p.do(g, lambda e, t=t: e.memset(t[:], 1.0), writes=[t.r()], tag="const")
            p.do(g, lambda e, t=t: e.affine_select(
                out=t[:], in_=t[:], compare_op=ALU.is_equal, fill=0.0,
                base=0, pattern=[[-1, 128]], channel_multiplier=1),
                reads=[t.r()], writes=[t.r()], tag="const")
        self._ohots = {}
        self._setup = True

    def get_ohot(self, dtype):
        if dtype not in self._ohots:
            self._const_seq += 1
            t = self.sb(f"c_ohot_{dtype}_{self._const_seq}", [32, 4096], dtype)
            self.p.do("gpsimd", lambda e: e.memset(t[:], 1.0),
                      writes=[t.r()], tag="const_ohot")
            self.p.do("gpsimd", lambda e: e.affine_select(
                out=t[:], in_=t[:], compare_op=ALU.is_equal,
                fill=0.0, base=0, pattern=[[-1, 32], [0, 128]], channel_multiplier=1),
                reads=[t.r()], writes=[t.r()], tag="const_ohot")
            self._ohots[dtype] = t
        return self._ohots[dtype]

    def constcol(self, value):
        key = float(value)
        if key not in self._consts:
            self._const_seq += 1
            t = self.sb(f"c_col{self._const_seq}", [128, 1], F32, outer=True)
            self.p.do("gpsimd", lambda e, t=t, v=key: e.memset(t[:], v),
                      writes=[t.r()], tag="constcol")
            self._consts[key] = t
        return self._consts[key]



def load_cols(ctx, name, dram_ap, n128, ncols, sem_unused, bank, dtype=F32):
    p = ctx.p
    sem = p.dma_sem(name + "_ld")
    rows = ctx.sb(name + "_rows", [ncols, 128], dtype)
    cols = ctx.sb(name, [128, ncols], dtype)
    p.dma("sync", rows[:], dram_ap, sem, writes=[rows.r()], tag=f"ld_{name}")
    idn = ctx.idn if dtype == F32 else ctx.idn_bf
    p.do("tensor", lambda e: e.transpose(bank[:, 0:ncols], rows[:], idn[0:ncols, 0:ncols]),
         reads=[rows.r(), idn.r()], writes=[bank.r()], tag=f"tr_{name}")
    p.do("scalar", lambda e: e.copy(cols[:], bank[:, 0:ncols]),
         reads=[bank.r()], writes=[cols.r()], tag=f"ev_{name}")
    return cols


def emit_ln(ctx, x_sb, x_key, ntile, tc, a_sb, b_sb, sq, rows, banks, tag, eps=1e-5):
    p = ctx.p
    s1, s2, bc = banks["s1"], banks["s2"], banks["bc"]
    rows_bf = TBuf(rows.ap.bitcast(BF16)[0:1, 6 * tc:8 * tc], rows.name)
    for j in range(ntile):
        jj = j % 2
        p.do("vector", lambda e, j=j, jj=jj: e.tensor_copy(
            sq[:, jj * tc:(jj + 1) * tc], x_sb[:, j * tc:(j + 1) * tc]),
            reads=[(x_sb.name, (x_key, j))], writes=[sq.r(("xb", jj))],
            tag=f"{tag}_xb_{j}")
        p.do("tensor", lambda e, j=j, jj=jj: e.matmul(
            s1[0:1, 0:tc], ctx.ones_col_bf[:], sq[:, jj * tc:(jj + 1) * tc],
            start=(j == 0), stop=(j == ntile - 1)),
            reads=[sq.r(("xb", jj)), ctx.ones_col_bf.r()],
            writes=[s1.r()], tag=f"{tag}_s1_{j}")
        p.do("scalar", lambda e, j=j, jj=jj: e.activation(
            sq[:, (2 + jj) * tc:(3 + jj) * tc], x_sb[:, j * tc:(j + 1) * tc],
            AF.Square),
            reads=[(x_sb.name, (x_key, j))], writes=[sq.r(("sq", jj))],
            tag=f"{tag}_sq_{j}")
        p.do("tensor", lambda e, j=j, jj=jj: e.matmul(
            s2[0:1, 0:tc], ctx.ones_col_bf[:], sq[:, (2 + jj) * tc:(3 + jj) * tc],
            start=(j == 0), stop=(j == ntile - 1)),
            reads=[sq.r(("sq", jj)), ctx.ones_col_bf.r()],
            writes=[s2.r()], tag=f"{tag}_s2_{j}")
    invd = 1.0 / (ntile * 128)
    p.do("vector", lambda e: e.tensor_scalar_mul(rows[0:1, 0:tc], s1[0:1, 0:tc], invd),
         reads=[s1.r()], writes=[(rows.name, 0)], tag=f"{tag}_mean")
    p.do("vector", lambda e: e.tensor_scalar_mul(rows[0:1, tc:2 * tc], s2[0:1, 0:tc], invd),
         reads=[s2.r()], writes=[(rows.name, 1)], tag=f"{tag}_ms")
    p.do("vector", lambda e: e.scalar_tensor_tensor(
        rows[0:1, 2 * tc:3 * tc], rows[0:1, 0:tc], -1.0, rows[0:1, 0:tc],
        ALU.mult, ALU.mult),
        reads=[(rows.name, 0)], writes=[(rows.name, 2)], tag=f"{tag}_negm2")
    p.do("vector", lambda e: e.tensor_add(
        rows[0:1, 2 * tc:3 * tc], rows[0:1, 2 * tc:3 * tc], rows[0:1, tc:2 * tc]),
        reads=[(rows.name, 2), (rows.name, 1)],
        writes=[(rows.name, 2)], tag=f"{tag}_var")
    ceps = ctx.constcol(eps)
    p.do("scalar", lambda e: e.activation(
        rows[0:1, 2 * tc:3 * tc], rows[0:1, 2 * tc:3 * tc], AF.Sqrt,
        bias=ceps[0:1, 0:1], scale=1.0),
        reads=[(rows.name, 2), ceps.r()], writes=[(rows.name, 2)],
        tag=f"{tag}_sd")
    p.do("vector", lambda e: e.reciprocal(rows[0:1, tc:2 * tc], rows[0:1, 2 * tc:3 * tc]),
         reads=[(rows.name, 2)], writes=[(rows.name, 1)],
         tag=f"{tag}_ri")
    p.do("vector", lambda e: e.scalar_tensor_tensor(
        rows[0:1, 2 * tc:3 * tc], rows[0:1, 0:tc], -1.0, rows[0:1, tc:2 * tc],
        ALU.mult, ALU.mult),
        reads=[(rows.name, 0), (rows.name, 1)],
        writes=[(rows.name, 2)], tag=f"{tag}_brow")
    p.do("vector", lambda e: e.tensor_copy(rows_bf[0:1, 0:tc], rows[0:1, tc:2 * tc]),
         reads=[(rows.name, 1)], writes=[(rows.name, "abf")], tag=f"{tag}_abf")
    p.do("vector", lambda e: e.tensor_copy(rows_bf[0:1, tc:2 * tc], rows[0:1, 2 * tc:3 * tc]),
         reads=[(rows.name, 2)], writes=[(rows.name, "bbf")], tag=f"{tag}_bbf")
    p.do("tensor", lambda e: e.matmul(
        bc[:, 0:tc], ctx.ones_row_bf[:], rows_bf[0:1, 0:tc], start=True, stop=True),
        reads=[(rows.name, "abf"), ctx.ones_row_bf.r()], writes=[bc.r()],
        tag=f"{tag}_bca")
    p.do("scalar", lambda e: e.copy(a_sb[:], bc[:, 0:tc]),
         reads=[bc.r()], writes=[a_sb.r()], tag=f"{tag}_eva")
    p.do("tensor", lambda e: e.matmul(
        bc[:, 0:tc], ctx.ones_row_bf[:], rows_bf[0:1, tc:2 * tc], start=True, stop=True),
        reads=[(rows.name, "bbf"), ctx.ones_row_bf.r()], writes=[bc.r()],
        tag=f"{tag}_bcb")
    p.do("scalar", lambda e: e.copy(b_sb[:], bc[:, 0:tc]),
         reads=[bc.r()], writes=[b_sb.r()], tag=f"{tag}_evb")



def emit_qmlp(ctx, xin, xout, w, pfx, dbg=None):
    p = ctx.p
    TC = 256
    NCH = T // TC
    F = 4096

    w1 = ctx.sb(pfx + "w1", [128, 8 * F], BF16)
    w2 = ctx.sb(pfx + "w2", [128, 32 * D], BF16)
    scl1 = ctx.sb(pfx + "scl1", [8, 2 * 512], F32)
    scl2 = ctx.sb(pfx + "scl2", [32, D], F32)
    h_sb = ctx.sb(pfx + "h", [128, 32 * TC], BF16)
    x_sb = ctx.sb(pfx + "x", [128, 8 * TC], F32)
    y_sb = ctx.sb(pfx + "y", [128, 8 * TC], F32)
    xn_sb = ctx.sb(pfx + "xn", [128, 8 * TC], BF16)
    t2_sb = ctx.sb(pfx + "t2", [128, 2 * TC], F32)
    sq = ctx.sb(pfx + "sq", [128, 4 * TC], BF16)
    a1 = ctx.sb(pfx + "a1", [128, TC], BF16)
    b1r = ctx.sb(pfx + "b1r", [128, TC], BF16)
    a2 = ctx.sb(pfx + "a2", [128, TC], BF16)
    b2r = ctx.sb(pfx + "b2r", [128, TC], BF16)
    lnrows = ctx.sb(pfx + "lnrows", [1, 4 * TC], F32)

    sem_w = p.dma_sem(pfx + "sw")
    sem_x = p.dma_sem(pfx + "sx")
    sem_out = p.dma_sem(pfx + "so")
    sem_misc = p.dma_sem(pfx + "sm")
    ohot = ctx.get_ohot(F32)

    bank = ctx.banks

    b1c = load_cols(ctx, pfx + "b1c", w["fc1_bias"], 128, 32, sem_misc, bank[6])
    b2c = load_cols(ctx, pfx + "b2c", w["fc2_bias"], 128, 8, sem_misc, bank[6])
    gc = load_cols(ctx, pfx + "gc", w["gamma"], 128, 8, sem_misc, bank[6])
    bc_ = load_cols(ctx, pfx + "bc", w["beta"], 128, 8, sem_misc, bank[6])
    hb2 = ctx.sb(pfx + "hb2", [128, 8], F32)
    p.do("scalar", lambda e: e.mul(hb2[:], b2c[:], 0.5),
         reads=[b2c.r()], writes=[hb2.r()], tag=pfx + "hb2")

    p.dma("sync", scl2[:], w["fc2_scales"], sem_misc, writes=[scl2.r()], tag=pfx + "scl2")

    for dt in range(8):
        p.dma("gpsimd", w1[:, dt * F:(dt + 1) * F],
              w["fc1_q"][dt * 128:(dt + 1) * 128, :], p.dma_sem(f"{pfx}sw1_{dt}"),
              writes=[(w1.name, ("q", dt))], tag=f"{pfx}w1q{dt}")
    sem_s1 = [p.dma_sem(pfx + "ss1a"), p.dma_sem(pfx + "ss1b")]
    for c in range(F // 512):
        cc = c % 2
        p.dma("sync", scl1[:, cc * 512:(cc + 1) * 512],
              w["fc1_scales"][:, c * 512:(c + 1) * 512], sem_s1[cc],
              writes=[(scl1.name, cc)], tag=f"{pfx}scl1_{c}")
        for dt in range(8):
            bsel = bank[4 + (dt % 2)]
            p.do("tensor", lambda e, dt=dt, cc=cc, bsel=bsel: e.matmul(
                bsel[:, 0:512], ohot[0:8, dt * 128:(dt + 1) * 128],
                scl1[:, cc * 512:(cc + 1) * 512], start=True, stop=True),
                reads=[(scl1.name, cc), ohot.r()], writes=[bsel.r()],
                tag=f"{pfx}w1s{dt}_{c}")
            p.do("vector", lambda e, dt=dt, c=c, bsel=bsel: e.scalar_tensor_tensor(
                w1[:, dt * F + c * 512:dt * F + (c + 1) * 512],
                w1[:, dt * F + c * 512:dt * F + (c + 1) * 512],
                -128.0, bsel[:, 0:512], ALU.add, ALU.mult),
                reads=[(w1.name, ("q", dt)), bsel.r()],
                writes=[(w1.name, ("q", dt))], tag=f"{pfx}w1d{dt}_{c}")
    for g8 in range(4):
        p.dma("gpsimd",
              w2[:, g8 * 8 * D:(g8 + 1) * 8 * D].rearrange("p (ft n) -> p ft n", ft=8),
              w["fc2_q"][g8 * 8 * 128:(g8 + 1) * 8 * 128, :].rearrange(
                  "(ft p) n -> p ft n", p=128),
              p.dma_sem(f"{pfx}sw2_{g8}"),
              writes=[(w2.name, ("q", g8))], tag=f"{pfx}w2q{g8}")
    for ft in range(32):
        for c in range(D // 512):
            bsel = bank[4 + (c % 2)]
            p.do("tensor", lambda e, ft=ft, c=c, bsel=bsel: e.matmul(
                bsel[:, 0:512], ohot[0:32, ft * 128:(ft + 1) * 128],
                scl2[:, c * 512:(c + 1) * 512], start=True, stop=True),
                reads=[scl2.r(), ohot.r()], writes=[bsel.r()],
                tag=f"{pfx}w2s{ft}_{c}")
            p.do("vector", lambda e, ft=ft, c=c, bsel=bsel: e.scalar_tensor_tensor(
                w2[:, ft * D + c * 512:ft * D + (c + 1) * 512],
                w2[:, ft * D + c * 512:ft * D + (c + 1) * 512],
                -128.0, bsel[:, 0:512], ALU.add, ALU.mult),
                reads=[(w2.name, ("q", ft // 8)), bsel.r()],
                writes=[(w2.name, ("q", ft // 8))], tag=f"{pfx}w2d{ft}_{c}")

    ln_banks = {"s1": bank[4], "s2": bank[5], "bc": bank[6]}
    for c in range(NCH):
        xsl = xin[:, c * TC:(c + 1) * TC].rearrange("(j p) t -> p j t", p=128)
        p.dma("sync", x_sb[:].rearrange("p (j t) -> p j t", j=8), xsl, sem_x,
              writes=[x_sb.r()], tag=f"{pfx}x{c}")
        emit_ln(ctx, x_sb, "c", 8, TC, a1, b1r, sq, lnrows, ln_banks, f"{pfx}ln1_{c}")
        for j in range(8):
            p.do("vector", lambda e, j=j: e.tensor_mul(
                y_sb[:, j * TC:(j + 1) * TC], x_sb[:, j * TC:(j + 1) * TC], a1[:]),
                reads=[(x_sb.name, ("c", j)), a1.r()],
                writes=[(y_sb.name, ("c", j))], tag=f"{pfx}y1_{c}_{j}")
            p.do("vector", lambda e, j=j: e.tensor_add(
                y_sb[:, j * TC:(j + 1) * TC], y_sb[:, j * TC:(j + 1) * TC], b1r[:]),
                reads=[(y_sb.name, ("c", j)), b1r.r()],
                writes=[(y_sb.name, ("c", j))], tag=f"{pfx}y2_{c}_{j}")
        emit_ln(ctx, y_sb, "c", 8, TC, a2, b2r, sq, lnrows, ln_banks, f"{pfx}ln2_{c}")
        for j in range(8):
            jj = j % 2
            p.do("vector", lambda e, j=j, jj=jj: e.tensor_mul(
                t2_sb[:, jj * TC:(jj + 1) * TC], y_sb[:, j * TC:(j + 1) * TC], a2[:]),
                reads=[(y_sb.name, ("c", j)), a2.r()],
                writes=[(t2_sb.name, ("c", jj))], tag=f"{pfx}t2a_{c}_{j}")
            p.do("vector", lambda e, j=j, jj=jj: e.tensor_add(
                t2_sb[:, jj * TC:(jj + 1) * TC], t2_sb[:, jj * TC:(jj + 1) * TC], b2r[:]),
                reads=[(t2_sb.name, ("c", jj)), b2r.r()],
                writes=[(t2_sb.name, ("c", jj))], tag=f"{pfx}t2b_{c}_{j}")
            p.do("vector", lambda e, j=j, jj=jj: e.tensor_scalar(
                xn_sb[:, j * TC:(j + 1) * TC], t2_sb[:, jj * TC:(jj + 1) * TC],
                gc[:, j:j + 1], bc_[:, j:j + 1], ALU.mult, ALU.add),
                reads=[(t2_sb.name, ("c", jj)), gc.r(), bc_.r()],
                writes=[(xn_sb.name, ("c", j))], tag=f"{pfx}xn_{c}_{j}")
        if dbg is not None and c == 0:
            p.dma("gpsimd", dbg["w1"], w1[:], p.dma_sem("sdbg1"),
                  reads=[(w1.name, ("q", dt)) for dt in range(8)], tag="dbg_w1")
            p.dma("gpsimd", dbg["y"], y_sb[:], p.dma_sem("sdbg2"),
                  reads=[(y_sb.name, ("c", j)) for j in range(8)], tag="dbg_y")
            p.dma("gpsimd", dbg["xn"], xn_sb[:], p.dma_sem("sdbg3"),
                  reads=[(xn_sb.name, ("c", j)) for j in range(8)], tag="dbg_xn")
        for ft in range(32):
            bm = bank[ft % 2]
            for dt in range(8):
                p.do("tensor", lambda e, ft=ft, dt=dt, bm=bm: e.matmul(
                    bm[:, 0:TC], w1[:, dt * F + ft * 128:dt * F + (ft + 1) * 128],
                    xn_sb[:, dt * TC:(dt + 1) * TC],
                    start=(dt == 0), stop=(dt == 7)),
                    reads=[(w1.name, ("q", dt)), (xn_sb.name, ("c", dt))],
                    writes=[bm.r()], tag=f"{pfx}h_{c}_{ft}_{dt}"),
            p.do("scalar", lambda e, ft=ft, bm=bm: e.activation(
                h_sb[:, ft * TC:(ft + 1) * TC], bm[:, 0:TC], AF.Silu,
                bias=b1c[:, ft:ft + 1], scale=1.0),
                reads=[bm.r(), b1c.r()], writes=[(h_sb.name, ("c", ft))],
                tag=f"{pfx}hev_{c}_{ft}")
        if dbg is not None and c == 0:
            p.dma("gpsimd", dbg["h"], h_sb[:], p.dma_sem("sdbg4"),
                  reads=[(h_sb.name, ("c", ft)) for ft in range(32)], tag="dbg_h")
            p.dma("gpsimd", dbg["w2"], w2[:], p.dma_sem("sdbg5"),
                  reads=[(w2.name, ("q", g8)) for g8 in range(4)], tag="dbg_w2")
        for dot in range(8):
            bm = bank[2 + dot % 2]
            for ft in range(32):
                p.do("tensor", lambda e, dot=dot, ft=ft, bm=bm: e.matmul(
                    bm[:, 0:TC], w2[:, ft * D + dot * 128:ft * D + (dot + 1) * 128],
                    h_sb[:, ft * TC:(ft + 1) * TC],
                    start=(ft == 0), stop=(ft == 31)),
                    reads=[(w2.name, ("q", ft // 8)), (h_sb.name, ("c", ft))],
                    writes=[bm.r()], tag=f"{pfx}z_{c}_{dot}_{ft}")
            dd = dot % 2
            p.do("scalar", lambda e, dot=dot, dd=dd, bm=bm: e.activation(
                t2_sb[:, dd * TC:(dd + 1) * TC], bm[:, 0:TC], AF.Identity,
                bias=hb2[:, dot:dot + 1], scale=ctx.constcol(0.5)[:, 0:1]),
                reads=[bm.r(), hb2.r(), ctx.constcol(0.5).r()],
                writes=[(t2_sb.name, ("c", dd))], tag=f"{pfx}zev_{c}_{dot}")
            p.do("vector", lambda e, dot=dot: e.tensor_add(
                y_sb[:, dot * TC:(dot + 1) * TC], y_sb[:, dot * TC:(dot + 1) * TC],
                x_sb[:, dot * TC:(dot + 1) * TC]),
                reads=[(y_sb.name, ("c", dot)), (x_sb.name, ("c", dot))],
                writes=[(y_sb.name, ("c", dot))], tag=f"{pfx}o1_{c}_{dot}")
            p.do("vector", lambda e, dot=dot, dd=dd: e.tensor_add(
                y_sb[:, dot * TC:(dot + 1) * TC], y_sb[:, dot * TC:(dot + 1) * TC],
                t2_sb[:, dd * TC:(dd + 1) * TC]),
                reads=[(y_sb.name, ("c", dot)), (t2_sb.name, ("c", dd))],
                writes=[(y_sb.name, ("o", dot))], tag=f"{pfx}o2_{c}_{dot}")
        osl = xout[:, c * TC:(c + 1) * TC].rearrange("(j p) t -> p j t", p=128)
        p.dma("sync", osl, y_sb[:].rearrange("p (j t) -> p j t", j=8),
              sem_out, reads=[y_sb.r()] + [(y_sb.name, ("o", j)) for j in range(8)],
              tag=f"{pfx}out{c}")
    return sem_out



def _reshape_inputs_qmlp(L, inputs):
    pf = f"L{L}_ffn_"
    return {
        "fc1_q": np.ascontiguousarray(inputs[pf + "fc1_q"]),
        "fc1_scales": np.ascontiguousarray(inputs[pf + "fc1_scales"]),
        "fc1_bias": np.ascontiguousarray(inputs[pf + "fc1_bias"].reshape(32, 128)),
        "fc2_q": np.ascontiguousarray(inputs[pf + "fc2_q"]),
        "fc2_scales": np.ascontiguousarray(inputs[pf + "fc2_scales"]),
        "fc2_bias": np.ascontiguousarray(inputs[pf + "fc2_bias"].reshape(8, 128)),
        "gamma": np.ascontiguousarray(inputs[pf + "ln_gamma"].reshape(8, 128)),
        "beta": np.ascontiguousarray(inputs[pf + "ln_beta"].reshape(8, 128)),
    }


QMLP_DRAM_SPECS = [
    ("fc1_q", [1024, 4096], I32), ("fc1_scales", [8, 4096], F32),
    ("fc1_bias", [32, 128], F32), ("fc2_q", [4096, 1024], I32),
    ("fc2_scales", [32, 1024], F32), ("fc2_bias", [8, 128], F32),
    ("gamma", [8, 128], F32), ("beta", [8, 128], F32),
]


def build_qmlp_standalone(debug=False):
    nc = bass.Bass("TRN2", target_bir_lowering=False)
    stack = ExitStack()
    with stack:
        ctx = Ctx(nc, stack)
        xin = nc.dram_tensor("xT", [D, T], F32, kind="ExternalInput").ap()
        xout = nc.dram_tensor("yT", [D, T], F32, kind="ExternalOutput").ap()
        w = {}
        for name, shape, dt in QMLP_DRAM_SPECS:
            w[name] = nc.dram_tensor(name, shape, dt, kind="ExternalInput").ap()
        dbg = None
        if debug:
            dbg = {
                "w1": nc.dram_tensor("dbg_w1", [128, 8 * 4096], F32, kind="ExternalOutput").ap(),
                "y": nc.dram_tensor("dbg_y", [128, 8 * 256], F32, kind="ExternalOutput").ap(),
                "xn": nc.dram_tensor("dbg_xn", [128, 8 * 256], F32, kind="ExternalOutput").ap(),
                "h": nc.dram_tensor("dbg_h", [128, 32 * 256], F32, kind="ExternalOutput").ap(),
                "w2": nc.dram_tensor("dbg_w2", [128, 32 * 1024], F32, kind="ExternalOutput").ap(),
            }
        ctx.setup_consts()
        sem_out = emit_qmlp(ctx, xin, xout, w, "qm_", dbg=dbg)
        last = sem_out.inc_ops[-1]
        ctx.p._add_wait(ctx.p.do("sync", lambda e: e.nop(), tag="fin"), last)
        ctx.p.finalize()
    return nc





def emit_l2attn(ctx, xin, xout, w, pfx):
    p = ctx.p
    TC = 512
    NCH = T // TC
    bank = ctx.banks

    wq = ctx.sb(pfx + "wqs", [128, 8 * D], BF16)
    wk = ctx.sb(pfx + "wks", [128, 8 * D], BF16)
    wv = ctx.sb(pfx + "wvs", [128, 8 * D], BF16)
    xn_sb = ctx.sb(pfx + "xn", [128, 8 * T], BF16)
    x_sb = ctx.sb(pfx + "x", [128, 8 * TC], F32)
    k_sb = ctx.sb(pfx + "k", [128, 8 * TC], F32)
    v_sb = ctx.sb(pfx + "v", [128, 8 * TC], F32)
    sqt = ctx.sb(pfx + "sqt", [128, 2 * TC], F32)
    lnsq = ctx.sb(pfx + "lnsq", [128, 4 * TC], BF16)
    rs_bc = ctx.sb(pfx + "rsbc", [128, TC], F32)
    gf = ctx.sb(pfx + "gf", [128, 8], F32)
    gfs = ctx.sb(pfx + "gfs", [128, 8], F32)
    a1 = ctx.sb(pfx + "a1", [128, TC], BF16)
    b1r = ctx.sb(pfx + "b1r", [128, TC], BF16)
    lnrows = ctx.sb(pfx + "lnrows", [1, 4 * TC], F32)
    nrows = ctx.sb(pfx + "nrows", [1, 2 * TC], F32)

    for name, t in (("wq", wq), ("wk", wk), ("wv", wv)):
        p.dma("gpsimd", t[:].rearrange("p (j e) -> p j e", j=8),
              w[name].rearrange("(j p) e -> p j e", p=128),
              p.dma_sem(pfx + "s" + name), writes=[t.r()], tag=pfx + name)
    p.do("gpsimd", lambda e: e.memset(gf[:], 0.0), writes=[gf.r()], tag=pfx + "gf0")

    sem_x = p.dma_sem(pfx + "sx")
    sem_x2 = p.dma_sem(pfx + "sx2")
    sem_out = p.dma_sem(pfx + "so")
    ln_banks = {"s1": bank[4], "s2": bank[5], "bc": bank[6]}

    def proj_etile(wsb, et, rkey, dst, c, bm, extra_tag):
        for dt in range(8):
            p.do("tensor", lambda e, et=et, dt=dt, bm=bm: e.matmul(
                bm[:, 0:TC], wsb[:, dt * D + et * 128:dt * D + (et + 1) * 128],
                xn_sb[:, dt * T + c * TC:dt * T + (c + 1) * TC],
                start=(dt == 0), stop=(dt == 7)),
                reads=[wsb.r(), (xn_sb.name, rkey(dt))],
                writes=[bm.r()], tag=f"{pfx}{extra_tag}_{c}_{et}_{dt}")
        p.do("scalar", lambda e, et=et, bm=bm: e.copy(dst[:, et * TC:(et + 1) * TC], bm[:, 0:TC]),
             reads=[bm.r()], writes=[(dst.name, ("c", et))], tag=f"{pfx}{extra_tag}ev_{c}_{et}")

    for c in range(NCH):
        xsl = xin[:, c * TC:(c + 1) * TC].rearrange("(j p) t -> p j t", p=128)
        p.dma("sync", x_sb[:].rearrange("p (j t) -> p j t", j=8), xsl, sem_x,
              writes=[x_sb.r()], tag=f"{pfx}x{c}")
        emit_ln(ctx, x_sb, "c", 8, TC, a1, b1r, lnsq, lnrows, ln_banks, f"{pfx}ln_{c}")
        for j in range(8):
            p.do("vector", lambda e, j=j, c=c: e.tensor_mul(
                xn_sb[:, j * T + c * TC:j * T + (c + 1) * TC],
                x_sb[:, j * TC:(j + 1) * TC], a1[:]),
                reads=[(x_sb.name, ("c", j)), a1.r()],
                writes=[(xn_sb.name, ("t", c, j))], tag=f"{pfx}y1_{c}_{j}")
            p.do("vector", lambda e, j=j, c=c: e.tensor_add(
                xn_sb[:, j * T + c * TC:j * T + (c + 1) * TC],
                xn_sb[:, j * T + c * TC:j * T + (c + 1) * TC], b1r[:]),
                reads=[(xn_sb.name, ("t", c, j)), b1r.r()],
                writes=[(xn_sb.name, ("c", j)), (xn_sb.name, ("done", c, j))],
                tag=f"{pfx}y2_{c}_{j}")
        for et in range(8):
            proj_etile(wk, et, lambda dt: ("c", dt), k_sb, c, bank[et % 2], "K")
            proj_etile(wv, et, lambda dt: ("c", dt), v_sb, c, bank[2 + et % 2], "V")
        for et in range(8):
            p.do("vector", lambda e, et=et: e.tensor_mul(
                sqt[:, (et % 2) * TC:(et % 2 + 1) * TC],
                k_sb[:, et * TC:(et + 1) * TC], k_sb[:, et * TC:(et + 1) * TC]),
                reads=[(k_sb.name, ("c", et))], writes=[sqt.r(et % 2)],
                tag=f"{pfx}k2_{c}_{et}")
            p.do("tensor", lambda e, et=et: e.matmul(
                bank[4][0:1, 0:TC], ctx.ones_col[:], sqt[:, (et % 2) * TC:(et % 2 + 1) * TC],
                start=(et == 0), stop=(et == 7)),
                reads=[sqt.r(et % 2), ctx.ones_col.r()], writes=[bank[4].r()],
                tag=f"{pfx}ssk_{c}_{et}")
        p.do("vector", lambda e: e.tensor_single_scalar(
            nrows[0:1, 0:TC], bank[4][0:1, 0:TC], 1e-24, ALU.max),
            reads=[bank[4].r()], writes=[(nrows.name, 0)], tag=f"{pfx}ssm_{c}")
        p.do("scalar", lambda e: e.activation(
            nrows[0:1, 0:TC], nrows[0:1, 0:TC], AF.Sqrt),
            reads=[(nrows.name, 0)], writes=[(nrows.name, 0)], tag=f"{pfx}ssq_{c}")
        p.do("vector", lambda e: e.reciprocal(nrows[0:1, TC:2 * TC], nrows[0:1, 0:TC]),
             reads=[(nrows.name, 0)], writes=[(nrows.name, 1)], tag=f"{pfx}srec_{c}")
        p.do("tensor", lambda e: e.matmul(
            bank[6][:, 0:TC], ctx.ones_row[:], nrows[0:1, TC:2 * TC], start=True, stop=True),
            reads=[(nrows.name, 1), ctx.ones_row.r()], writes=[bank[6].r()],
            tag=f"{pfx}sbc_{c}")
        p.do("scalar", lambda e: e.copy(rs_bc[:], bank[6][:, 0:TC]),
             reads=[bank[6].r()], writes=[rs_bc.r()], tag=f"{pfx}sbce_{c}")
        for et in range(8):
            p.do("vector", lambda e, et=et: e.tensor_mul(
                k_sb[:, et * TC:(et + 1) * TC], k_sb[:, et * TC:(et + 1) * TC],
                v_sb[:, et * TC:(et + 1) * TC]),
                reads=[(k_sb.name, ("c", et)), (v_sb.name, ("c", et))],
                writes=[(k_sb.name, ("c", et))], tag=f"{pfx}kv_{c}_{et}")
            p.do("vector", lambda e, et=et: e.tensor_mul(
                k_sb[:, et * TC:(et + 1) * TC], k_sb[:, et * TC:(et + 1) * TC],
                rs_bc[:]),
                reads=[(k_sb.name, ("c", et)), rs_bc.r()],
                writes=[(k_sb.name, ("c", et))], tag=f"{pfx}kvn_{c}_{et}")
            p.do("vector", lambda e, et=et: e.reduce_sum(
                sqt[:, (et % 2) * TC:(et % 2) * TC + 1], k_sb[:, et * TC:(et + 1) * TC],
                axis=AX.X),
                reads=[(k_sb.name, ("c", et))], writes=[sqt.r(et % 2)],
                tag=f"{pfx}red_{c}_{et}")
            p.do("vector", lambda e, et=et: e.tensor_add(
                gf[:, et:et + 1], gf[:, et:et + 1], sqt[:, (et % 2) * TC:(et % 2) * TC + 1]),
                reads=[gf.r(), sqt.r(et % 2)], writes=[gf.r()],
                tag=f"{pfx}gfa_{c}_{et}")
    p.do("scalar", lambda e: e.mul(gfs[:], gf[:], 1.0 / T),
         reads=[gf.r()], writes=[gfs.r()], tag=pfx + "gfs")

    q_sb = k_sb
    for c in range(NCH):
        xsl = xin[:, c * TC:(c + 1) * TC].rearrange("(j p) t -> p j t", p=128)
        p.dma("sync", x_sb[:].rearrange("p (j t) -> p j t", j=8), xsl, sem_x2,
              writes=[x_sb.r()], tag=f"{pfx}x2_{c}")
        for et in range(8):
            proj_etile(wq, et, lambda dt, c=c: ("done", c, dt), q_sb, c, bank[et % 2], "Q")
        for et in range(8):
            p.do("vector", lambda e, et=et: e.tensor_mul(
                sqt[:, (et % 2) * TC:(et % 2 + 1) * TC],
                q_sb[:, et * TC:(et + 1) * TC], q_sb[:, et * TC:(et + 1) * TC]),
                reads=[(q_sb.name, ("c", et))], writes=[sqt.r(et % 2)],
                tag=f"{pfx}q2_{c}_{et}")
            p.do("tensor", lambda e, et=et: e.matmul(
                bank[4][0:1, 0:TC], ctx.ones_col[:], sqt[:, (et % 2) * TC:(et % 2 + 1) * TC],
                start=(et == 0), stop=(et == 7)),
                reads=[sqt.r(et % 2), ctx.ones_col.r()], writes=[bank[4].r()],
                tag=f"{pfx}ssq2_{c}_{et}")
        p.do("vector", lambda e: e.tensor_single_scalar(
            nrows[0:1, 0:TC], bank[4][0:1, 0:TC], 1e-24, ALU.max),
            reads=[bank[4].r()], writes=[(nrows.name, 0)], tag=f"{pfx}qsm_{c}")
        p.do("scalar", lambda e: e.activation(
            nrows[0:1, 0:TC], nrows[0:1, 0:TC], AF.Sqrt),
            reads=[(nrows.name, 0)], writes=[(nrows.name, 0)], tag=f"{pfx}qsq_{c}")
        p.do("vector", lambda e: e.reciprocal(nrows[0:1, TC:2 * TC], nrows[0:1, 0:TC]),
             reads=[(nrows.name, 0)], writes=[(nrows.name, 1)], tag=f"{pfx}qrec_{c}")
        p.do("tensor", lambda e: e.matmul(
            bank[6][:, 0:TC], ctx.ones_row[:], nrows[0:1, TC:2 * TC], start=True, stop=True),
            reads=[(nrows.name, 1), ctx.ones_row.r()], writes=[bank[6].r()],
            tag=f"{pfx}qbc_{c}")
        p.do("scalar", lambda e: e.copy(rs_bc[:], bank[6][:, 0:TC]),
             reads=[bank[6].r()], writes=[rs_bc.r()], tag=f"{pfx}qbce_{c}")
        for et in range(8):
            p.do("vector", lambda e, et=et: e.tensor_mul(
                q_sb[:, et * TC:(et + 1) * TC], q_sb[:, et * TC:(et + 1) * TC], rs_bc[:]),
                reads=[(q_sb.name, ("c", et)), rs_bc.r()],
                writes=[(q_sb.name, ("c", et))], tag=f"{pfx}qn_{c}_{et}")
            p.do("vector", lambda e, et=et: e.tensor_scalar_mul(
                q_sb[:, et * TC:(et + 1) * TC], q_sb[:, et * TC:(et + 1) * TC],
                gfs[:, et:et + 1]),
                reads=[(q_sb.name, ("c", et)), gfs.r()],
                writes=[(q_sb.name, ("c", et))], tag=f"{pfx}qg_{c}_{et}")
            p.do("vector", lambda e, et=et: e.tensor_add(
                x_sb[:, et * TC:(et + 1) * TC], x_sb[:, et * TC:(et + 1) * TC],
                q_sb[:, et * TC:(et + 1) * TC]),
                reads=[(x_sb.name, None), (q_sb.name, ("c", et))],
                writes=[(x_sb.name, ("o", et))], tag=f"{pfx}oadd_{c}_{et}")
        osl = xout[:, c * TC:(c + 1) * TC].rearrange("(j p) t -> p j t", p=128)
        p.dma("sync", osl, x_sb[:].rearrange("p (j t) -> p j t", j=8), sem_out,
              reads=[x_sb.r()] + [(x_sb.name, ("o", j)) for j in range(8)],
              tag=f"{pfx}out{c}")
    return sem_out


L2A_DRAM_SPECS = [("wq", [1024, 1024], F32), ("wk", [1024, 1024], F32),
                  ("wv", [1024, 1024], F32)]


def build_l2attn_standalone():
    nc = bass.Bass("TRN2", target_bir_lowering=False)
    stack = ExitStack()
    with stack:
        ctx = Ctx(nc, stack)
        xin = nc.dram_tensor("xT", [D, T], F32, kind="ExternalInput").ap()
        xout = nc.dram_tensor("yT", [D, T], F32, kind="ExternalOutput").ap()
        w = {}
        for name, shape, dt in L2A_DRAM_SPECS:
            w[name] = nc.dram_tensor(name, shape, dt, kind="ExternalInput").ap()
        ctx.setup_consts()
        sem_out = emit_l2attn(ctx, xin, xout, w, "la_")
        last = sem_out.inc_ops[-1]
        ctx.p._add_wait(ctx.p.do("sync", lambda e: e.nop(), tag="fin"), last)
        ctx.p.finalize()
    return nc





def emit_l0f(ctx, xin, xout, w, pfx, dbg=None):
    p = ctx.p
    TC = 512
    NCH = T // TC
    FE = 512
    bank = ctx.banks

    xn_sb = ctx.sb(pfx + "xn", [128, 8 * T], BF16)
    x_sb = ctx.sb(pfx + "x", [128, 8 * TC], F32)
    acc = ctx.sb(pfx + "acc", [128, 8 * 1024], F32)
    rw_sb = ctx.sb(pfx + "rw", [128, 8 * 17], F32)
    wT = ctx.sb(pfx + "wT", [17, T], BF16)
    ohot_bf = ctx.get_ohot(BF16)
    wg_sb = ctx.sb(pfx + "wg", [128, 2 * 8 * FE], BF16)
    wu_sb = ctx.sb(pfx + "wu", [128, 2 * 8 * FE], BF16)
    wd_sb = ctx.sb(pfx + "wd", [128, 2 * 4 * 1024], BF16)
    h_sb = ctx.sb(pfx + "h", [128, 4 * TC], BF16)
    gs_sb = ctx.sb(pfx + "gs", [128, TC], F32)
    wbc_sb = ctx.sb(pfx + "wbc", [128, TC], F32)
    rout = ctx.sb(pfx + "rout", [128, 32 * 17], F32)
    m1i = ctx.sb(pfx + "m1i", [128, 512], F32)
    cmb = ctx.sb(pfx + "cmb", [128, 32 * 17], F32)
    cls = ctx.sb(pfx + "cls", [128, 96], F32)
    sqt = ctx.sb(pfx + "sq", [128, 4 * TC], BF16)
    a1 = ctx.sb(pfx + "a1", [128, TC], BF16)
    b1r = ctx.sb(pfx + "b1r", [128, TC], BF16)
    lnrows = ctx.sb(pfx + "lnrows", [1, 4 * TC], F32)

    sem_x = p.dma_sem(pfx + "sx")
    sem_x2 = p.dma_sem(pfx + "sx2")
    sem_out = p.dma_sem(pfx + "so")
    sem_wg = [p.dma_sem(pfx + f"swg{i}") for i in range(2)]
    sem_wu = [p.dma_sem(pfx + f"swu{i}") for i in range(2)]
    sem_wd = [p.dma_sem(pfx + f"swd{i}") for i in range(2)]
    ln_banks = {"s1": bank[4], "s2": bank[5], "bc": bank[6]}

    p.dma("sync", rw_sb[:].rearrange("p (j e) -> p j e", j=8),
          w["rw17"].rearrange("(j p) e -> p j e", p=128),
          p.dma_sem(pfx + "srw"), writes=[rw_sb.r()], tag=pfx + "rw")

    for c in range(NCH):
        xsl = xin[:, c * TC:(c + 1) * TC].rearrange("(j p) t -> p j t", p=128)
        p.dma("sync", x_sb[:].rearrange("p (j t) -> p j t", j=8), xsl, sem_x,
              writes=[x_sb.r()], tag=f"{pfx}x{c}")
        emit_ln(ctx, x_sb, "c", 8, TC, a1, b1r, sqt, lnrows, ln_banks, f"{pfx}ln_{c}")
        for j in range(8):
            p.do("vector", lambda e, j=j: e.tensor_mul(
                x_sb[:, j * TC:(j + 1) * TC], x_sb[:, j * TC:(j + 1) * TC], a1[:]),
                reads=[(x_sb.name, ("c", j)), a1.r()],
                writes=[(x_sb.name, ("c", j))], tag=f"{pfx}y1_{c}_{j}")
            p.do("vector", lambda e, j=j: e.tensor_add(
                x_sb[:, j * TC:(j + 1) * TC], x_sb[:, j * TC:(j + 1) * TC], b1r[:]),
                reads=[(x_sb.name, ("c", j)), b1r.r()],
                writes=[(x_sb.name, ("y", j))], tag=f"{pfx}y2_{c}_{j}")
            p.do("vector", lambda e, j=j, c=c: e.tensor_copy(
                xn_sb[:, j * T + c * TC:j * T + (c + 1) * TC],
                x_sb[:, j * TC:(j + 1) * TC]),
                reads=[(x_sb.name, ("y", j))],
                writes=[(xn_sb.name, ("xn", c, j))], tag=f"{pfx}xncp_{c}_{j}")
        for ts in range(4):
            bR = bank[2]
            g = c * 4 + ts
            for dt in range(8):
                p.do("tensor", lambda e, ts=ts, dt=dt, bR=bR: e.matmul(
                    bR[:, 0:17], x_sb[:, dt * TC + ts * 128:dt * TC + (ts + 1) * 128],
                    rw_sb[:, dt * 17:(dt + 1) * 17],
                    start=(dt == 0), stop=(dt == 7)),
                    reads=[(x_sb.name, ("y", dt)), rw_sb.r()],
                    writes=[bR.r()], tag=f"{pfx}rl_{c}_{ts}_{dt}")
            p.do("scalar", lambda e, g=g, bR=bR: e.copy(
                rout[:, g * 17:(g + 1) * 17], bR[:, 0:17]),
                reads=[bR.r()], writes=[(rout.name, g)], tag=f"{pfx}re_{c}_{ts}")

    ALLR = [(rout.name, g) for g in range(32)]
    r3 = rout[:].rearrange("p (g k) -> p g k", k=17)
    c3 = lambda sl: cls[:, sl].rearrange("p (g o) -> p g o", o=1).broadcast_to([128, 32, 16])
    w3 = cmb[:].rearrange("p (g k) -> p g k", k=17)
    e3 = r3[:, :, 0:16]
    p3 = w3[:, :, 0:16]
    p.do("vector", lambda e: e.reduce_max(
        cls[:, 0:32], e3, axis=AX.X, negate=True),
        reads=ALLR, writes=[(cls.name, "nm")], tag=pfx + "nm")
    p.do("vector", lambda e: e.tensor_tensor(
        e3, e3, c3(slice(0, 32)), ALU.add),
        reads=ALLR + [(cls.name, "nm")], writes=[(rout.name, "e")], tag=pfx + "dsub")
    p.do("scalar", lambda e: e.activation(e3, e3, AF.Exp),
         reads=[(rout.name, "e")], writes=[(rout.name, "e")], tag=pfx + "exp")
    p.do("vector", lambda e: e.tensor_single_scalar(
        m1i[:].rearrange("p (g k) -> p g k", k=16), e3, 1.0, ALU.is_lt),
        reads=[(rout.name, "e")], writes=[m1i.r()], tag=pfx + "m1i")
    p.do("vector", lambda e: e.tensor_tensor(
        p3, e3, m1i[:].rearrange("p (g k) -> p g k", k=16), ALU.mult),
        reads=[(rout.name, "e"), m1i.r()], writes=[(cmb.name, "p")], tag=pfx + "p2")
    p.do("vector", lambda e: e.reduce_max(cls[:, 32:64], p3, axis=AX.X),
         reads=[(cmb.name, "p")], writes=[(cls.name, "m2")], tag=pfx + "m2")
    p.do("vector", lambda e: e.tensor_tensor(
        p3, p3, c3(slice(32, 64)), ALU.subtract),
        reads=[(cmb.name, "p"), (cls.name, "m2")], writes=[(cmb.name, "p")], tag=pfx + "d2")
    p.do("vector", lambda e: e.tensor_single_scalar(p3, p3, 0.0, ALU.is_ge),
         reads=[(cmb.name, "p")], writes=[(cmb.name, "p")], tag=pfx + "mk2")
    p.do("vector", lambda e: e.scalar_tensor_tensor(
        m1i[:].rearrange("p (g k) -> p g k", k=16), m1i[:].rearrange("p (g k) -> p g k", k=16),
        -1.0, p3, ALU.mult, ALU.add),
        reads=[m1i.r(), (cmb.name, "p")], writes=[m1i.r()], tag=pfx + "msk")
    p.do("vector", lambda e: e.tensor_single_scalar(
        cls[:, 32:64], cls[:, 32:64], 1.0, ALU.add),
        reads=[(cls.name, "m2")], writes=[(cls.name, "m2")], tag=pfx + "dn")
    p.do("vector", lambda e: e.tensor_single_scalar(m1i[:], m1i[:], 1.0, ALU.add),
         reads=[m1i.r()], writes=[m1i.r()], tag=pfx + "msk1")
    p.do("vector", lambda e: e.reciprocal(cls[:, 64:96], cls[:, 32:64]),
         reads=[(cls.name, "m2")], writes=[(cls.name, "rc")], tag=pfx + "rc")
    p.do("vector", lambda e: e.tensor_tensor(
        p3, e3, m1i[:].rearrange("p (g k) -> p g k", k=16), ALU.mult),
        reads=[(rout.name, "e"), m1i.r()], writes=[(cmb.name, "p")], tag=pfx + "wmul")
    p.do("gpsimd", lambda e: e.tensor_tensor(
        p3, p3, c3(slice(64, 96)), ALU.mult),
        reads=[(cmb.name, "p"), (cls.name, "rc")], writes=[(cmb.name, "w")], tag=pfx + "wfin")
    p.do("scalar", lambda e: e.activation(
        w3[:, :, 16:17], r3[:, :, 16:17], AF.Sigmoid),
        reads=ALLR, writes=[(cmb.name, "g")], tag=pfx + "gate")
    for g in range(32):
        p.do("tensor", lambda e, g=g: e.transpose(
            bank[3][0:17, 0:128], cmb[:, g * 17:(g + 1) * 17], ctx.idn[:]),
            reads=[(cmb.name, "w"), (cmb.name, "g"), ctx.idn.r()],
            writes=[bank[3].r()], tag=f"{pfx}wtt_{g}")
        p.do("scalar", lambda e, g=g: e.copy(
            wT[0:17, g * 128:(g + 1) * 128], bank[3][0:17, 0:128]),
            reads=[bank[3].r()], writes=[(wT.name, ("g", g))],
            tag=f"{pfx}wte_{g}")

    def load_expert(ei, buf):
        if ei < 16:
            p.dma("gpsimd", wg_sb[:, buf * 8 * FE:(buf + 1) * 8 * FE].rearrange(
                "p (j f) -> p j f", j=8),
                w["Wg"][ei].rearrange("(j p) f -> p j f", p=128),
                sem_wg[buf], writes=[(wg_sb.name, buf)], tag=f"{pfx}wg{ei}")
            p.dma("gpsimd", wu_sb[:, buf * 8 * FE:(buf + 1) * 8 * FE].rearrange(
                "p (j f) -> p j f", j=8),
                w["Wu"][ei].rearrange("(j p) f -> p j f", p=128),
                sem_wu[buf], writes=[(wu_sb.name, buf)], tag=f"{pfx}wu{ei}")
            p.dma("gpsimd", wd_sb[:, buf * 4 * 1024:(buf + 1) * 4 * 1024].rearrange(
                "p (j n) -> p j n", j=4),
                w["Wd"][ei].rearrange("(j p) n -> p j n", p=128),
                sem_wd[buf], writes=[(wd_sb.name, buf)], tag=f"{pfx}wd{ei}")
        else:
            q = ei - 16
            p.dma("gpsimd", wg_sb[:, buf * 8 * FE:(buf + 1) * 8 * FE].rearrange(
                "p (j f) -> p j f", j=8),
                w["Sg"][:, q * FE:(q + 1) * FE].rearrange("(j p) f -> p j f", p=128),
                sem_wg[buf], writes=[(wg_sb.name, buf)], tag=f"{pfx}sg{q}")
            p.dma("gpsimd", wu_sb[:, buf * 8 * FE:(buf + 1) * 8 * FE].rearrange(
                "p (j f) -> p j f", j=8),
                w["Su"][:, q * FE:(q + 1) * FE].rearrange("(j p) f -> p j f", p=128),
                sem_wu[buf], writes=[(wu_sb.name, buf)], tag=f"{pfx}su{q}")
            p.dma("gpsimd", wd_sb[:, buf * 4 * 1024:(buf + 1) * 4 * 1024].rearrange(
                "p (j n) -> p j n", j=4),
                w["Sd"][q * FE:(q + 1) * FE, :].rearrange("(j p) n -> p j n", p=128),
                sem_wd[buf], writes=[(wd_sb.name, buf)], tag=f"{pfx}sd{q}")

    if dbg is not None:
        p.dma("gpsimd", dbg["wT"], wT[:], p.dma_sem("sdwT"),
              reads=[(wT.name, ("g", g)) for g in range(32)], tag="dbg_wT")
        p.dma("gpsimd", dbg["xn"], xn_sb[:, 0:T], p.dma_sem("sdxn"),
              reads=[(xn_sb.name, ("xn", c, 0)) for c in range(8)], tag="dbg_xn")
    NEXP = 20
    for s in range(4):
        p.do("gpsimd", lambda e: e.memset(acc[:], 0.0), writes=[acc.r()],
             tag=f"{pfx}acc0_{s}")
        for ei in range(NEXP):
            buf = ei % 2
            if s == 0 and ei < 1:
                load_expert(ei, buf)
            for cl in range(2):
                c = s * 2 + cl
                wrow = ei if ei < 16 else 16
                p.do("tensor", lambda e, wrow=wrow, c=c: e.matmul(
                    bank[2][:, 0:TC], ohot_bf[0:17, wrow * 128:(wrow + 1) * 128],
                    wT[0:17, c * TC:(c + 1) * TC], start=True, stop=True),
                    reads=[(wT.name, ("g", c * 4)), (wT.name, ("g", c * 4 + 1)),
                           (wT.name, ("g", c * 4 + 2)), (wT.name, ("g", c * 4 + 3)),
                           ohot_bf.r()],
                    writes=[bank[2].r()], tag=f"{pfx}wbc_{s}_{ei}_{cl}")
                p.do("scalar", lambda e: e.copy(wbc_sb[:], bank[2][:, 0:TC]),
                     reads=[bank[2].r()], writes=[wbc_sb.r()],
                     tag=f"{pfx}wbce_{s}_{ei}_{cl}")
                for ft in range(4):
                    bg, bu = bank[0], bank[1]
                    for dt in range(8):
                        p.do("tensor", lambda e, ft=ft, dt=dt, buf=buf, c=c, bg=bg: e.matmul(
                            bg[:, 0:TC],
                            wg_sb[:, buf * 8 * FE + dt * FE + ft * 128:buf * 8 * FE + dt * FE + (ft + 1) * 128],
                            xn_sb[:, dt * T + c * TC:dt * T + (c + 1) * TC],
                            start=(dt == 0), stop=(dt == 7)),
                            reads=[(wg_sb.name, buf), (xn_sb.name, ("xn", c, dt))],
                            writes=[bg.r()], tag=f"{pfx}g_{s}_{ei}_{cl}_{ft}_{dt}")
                        p.do("tensor", lambda e, ft=ft, dt=dt, buf=buf, c=c, bu=bu: e.matmul(
                            bu[:, 0:TC],
                            wu_sb[:, buf * 8 * FE + dt * FE + ft * 128:buf * 8 * FE + dt * FE + (ft + 1) * 128],
                            xn_sb[:, dt * T + c * TC:dt * T + (c + 1) * TC],
                            start=(dt == 0), stop=(dt == 7)),
                            reads=[(wu_sb.name, buf), (xn_sb.name, ("xn", c, dt))],
                            writes=[bu.r()], tag=f"{pfx}u_{s}_{ei}_{cl}_{ft}_{dt}")
                    p.do("scalar", lambda e, bg=bg: e.activation(
                        gs_sb[:], bg[:, 0:TC], AF.Silu),
                        reads=[bg.r()], writes=[gs_sb.r()],
                        tag=f"{pfx}sil_{s}_{ei}_{cl}_{ft}")
                    p.do("vector", lambda e, ft=ft, bu=bu: e.tensor_mul(
                        h_sb[:, ft * TC:(ft + 1) * TC], bu[:, 0:TC], wbc_sb[:]),
                        reads=[bu.r(), wbc_sb.r()], writes=[(h_sb.name, ft)],
                        tag=f"{pfx}hu_{s}_{ei}_{cl}_{ft}")
                    p.do("vector", lambda e, ft=ft: e.tensor_mul(
                        h_sb[:, ft * TC:(ft + 1) * TC], h_sb[:, ft * TC:(ft + 1) * TC],
                        gs_sb[:]),
                        reads=[(h_sb.name, ft), gs_sb.r()], writes=[(h_sb.name, ft)],
                        tag=f"{pfx}hg_{s}_{ei}_{cl}_{ft}")
                if ei + 1 < NEXP and cl == 0:
                    load_expert(ei + 1, (ei + 1) % 2)
                elif ei == NEXP - 1 and cl == 0 and s < 3:
                    load_expert(0, 0)
                if dbg is not None and s == 0 and ei == 0 and cl == 0:
                    p.dma("gpsimd", dbg["h"], h_sb[:], p.dma_sem("sdh"),
                          reads=[(h_sb.name, ft) for ft in range(4)], tag="dbg_h")
                    p.dma("gpsimd", dbg["wbc"], wbc_sb[:], p.dma_sem("sdwb"),
                          reads=[wbc_sb.r()], tag="dbg_wbc")
                for dot in range(8):
                    bz = bank[3 + dot % 2]
                    for ft in range(4):
                        p.do("tensor", lambda e, dot=dot, ft=ft, buf=buf, bz=bz: e.matmul(
                            bz[:, 0:TC],
                            wd_sb[:, buf * 4 * 1024 + ft * 1024 + dot * 128:buf * 4 * 1024 + ft * 1024 + (dot + 1) * 128],
                            h_sb[:, ft * TC:(ft + 1) * TC],
                            start=(ft == 0), stop=(ft == 3)),
                            reads=[(wd_sb.name, buf), (h_sb.name, ft)],
                            writes=[bz.r()], tag=f"{pfx}z_{s}_{ei}_{cl}_{dot}_{ft}")
                    p.do("vector", lambda e, dot=dot, cl=cl, bz=bz: e.tensor_add(
                        acc[:, dot * 1024 + cl * TC:dot * 1024 + (cl + 1) * TC],
                        acc[:, dot * 1024 + cl * TC:dot * 1024 + (cl + 1) * TC],
                        bz[:, 0:TC]),
                        reads=[(acc.name, (cl, dot)), bz.r()],
                        writes=[(acc.name, (cl, dot))],
                        tag=f"{pfx}za_{s}_{ei}_{cl}_{dot}")
        for cl in range(2):
            c = s * 2 + cl
            xsl = xin[:, c * TC:(c + 1) * TC].rearrange("(j p) t -> p j t", p=128)
            p.dma("sync", x_sb[:].rearrange("p (j t) -> p j t", j=8), xsl, sem_x2,
                  writes=[x_sb.r()], tag=f"{pfx}x2_{c}")
            for j in range(8):
                p.do("vector", lambda e, j=j, cl=cl: e.tensor_add(
                    x_sb[:, j * TC:(j + 1) * TC], x_sb[:, j * TC:(j + 1) * TC],
                    acc[:, j * 1024 + cl * TC:j * 1024 + (cl + 1) * TC]),
                    reads=[(x_sb.name, None), (acc.name, (cl, j))],
                    writes=[(x_sb.name, ("o", j))], tag=f"{pfx}oa_{c}_{j}")
            osl = xout[:, c * TC:(c + 1) * TC].rearrange("(j p) t -> p j t", p=128)
            p.dma("sync", osl, x_sb[:].rearrange("p (j t) -> p j t", j=8), sem_out,
                  reads=[x_sb.r()] + [(x_sb.name, ("o", j)) for j in range(8)],
                  tag=f"{pfx}out{c}")
    return sem_out


L0F_DRAM_SPECS = [
    ("rw17", [1024, 17], F32), ("Wg", [16, 1024, 512], F32),
    ("Wu", [16, 1024, 512], F32), ("Wd", [16, 512, 1024], F32),
    ("Sg", [1024, 2048], F32), ("Su", [1024, 2048], F32),
    ("Sd", [2048, 1024], F32),
]


def build_l0f_standalone(debug=False):
    nc = bass.Bass("TRN2", target_bir_lowering=False)
    stack = ExitStack()
    with stack:
        ctx = Ctx(nc, stack)
        xin = nc.dram_tensor("xT", [D, T], F32, kind="ExternalInput").ap()
        xout = nc.dram_tensor("yT", [D, T], F32, kind="ExternalOutput").ap()
        w = {}
        for name, shape, dt in L0F_DRAM_SPECS:
            w[name] = nc.dram_tensor(name, shape, dt, kind="ExternalInput").ap()
        dbg = None if not debug else {
            "wT": nc.dram_tensor("dbg_wT", [17, T], F32, kind="ExternalOutput").ap(),
            "xn": nc.dram_tensor("dbg_xn", [128, T], F32, kind="ExternalOutput").ap(),
            "h": nc.dram_tensor("dbg_h", [128, 4 * 512], F32, kind="ExternalOutput").ap(),
            "wbc": nc.dram_tensor("dbg_wbc", [128, 512], F32, kind="ExternalOutput").ap(),
        }
        ctx.setup_consts()
        sem_out = emit_l0f(ctx, xin, xout, w, "mf_", dbg=dbg)
        last = sem_out.inc_ops[-1]
        ctx.p._add_wait(ctx.p.do("sync", lambda e: e.nop(), tag="fin"), last)
        ctx.p.finalize()
    return nc





def emit_l3a(ctx, xin, xout, w, stage, pfx):
    p = ctx.p
    TC = 512
    NCH = T // TC
    bank = ctx.banks
    INV2PI = 1.0 / (2 * np.pi)

    xn_sb = ctx.sb(pfx + "xn", [128, 8 * T], BF16)
    x_sb = ctx.sb(pfx + "x", [128, 8 * TC], F32)
    wk = ctx.sb(pfx + "wk", [128, 8 * 128], BF16)
    wv = ctx.sb(pfx + "wv", [128, 8 * 128], BF16)
    wo = ctx.sb(pfx + "wo", [128, 8 * D], BF16)
    scr1 = ctx.sb(pfx + "scr1", [128, T], F32)
    v_sb = ctx.sb(pfx + "v", [128, T], BF16)
    ck = ctx.sb(pfx + "ck", [128, T], BF16)
    sk = ctx.sb(pfx + "sk", [128, T], BF16)
    cr = ctx.sb(pfx + "cr", [128, T], BF16)
    sr = ctx.sb(pfx + "sr", [128, T], BF16)
    prod = ctx.sb(pfx + "prod", [128, T], BF16)
    mem = ctx.sb(pfx + "mem", [128, T], BF16)
    isq = ctx.sb(pfx + "isq", [128, TC], F32)
    qs = ctx.sb(pfx + "qs", [128, TC], F32)
    ki = ctx.sb(pfx + "ki", [128, TC], I32)
    kf = ctx.sb(pfx + "kf", [128, TC], F32)
    frow = ctx.sb(pfx + "frow", [1, 1024], F32)
    trow = ctx.sb(pfx + "trow", [1, TC], F32)
    gpair = ctx.sb(pfx + "gpair", [2, 2], F32)
    gcols = ctx.sb(pfx + "gcols", [128, 2], F32)
    blk2 = ctx.sb(pfx + "blk2", [2, 128], F32)
    c2pi = ctx.constcol(2 * np.pi)
    czero = ctx.constcol(0.0)
    sqt = ctx.sb(pfx + "sq", [128, 4 * TC], BF16)
    a1 = ctx.sb(pfx + "a1", [128, TC], BF16)
    b1r = ctx.sb(pfx + "b1r", [128, TC], BF16)
    lnrows = ctx.sb(pfx + "lnrows", [1, 4 * TC], F32)

    sem_x = p.dma_sem(pfx + "sx")
    sem_x2 = p.dma_sem(pfx + "sx2")
    sem_out = p.dma_sem(pfx + "so")
    sem_st = p.dma_sem(pfx + "sst")
    sem_cb = p.dma_sem(pfx + "scb")
    sem_g = p.dma_sem(pfx + "sg")
    ln_banks = {"s1": bank[4], "s2": bank[5], "bc": bank[6]}

    p.dma("gpsimd", wo[:].rearrange("p (j e) -> p j e", j=8),
          w["Wo"].rearrange("(j p) e -> p j e", p=128),
          p.dma_sem(pfx + "sWo"), writes=[wo.r()], tag=pfx + "Wo")
    sem_wk = p.dma_sem(pfx + "sWk")
    sem_wv = p.dma_sem(pfx + "sWv")
    p.dma("sync", frow[:], w["freqs"], p.dma_sem(pfx + "sf"),
          writes=[frow.r()], tag=pfx + "freqs")
    p.do("gpsimd", lambda e: e.memset(blk2[:], 1.0), writes=[blk2.r()], tag=pfx + "bk0")
    p.do("gpsimd", lambda e: e.affine_select(
        out=blk2[:], in_=blk2[:], compare_op=ALU.is_equal, fill=0.0,
        base=0, pattern=[[-1, 2], [0, 64]], channel_multiplier=1),
        reads=[blk2.r()], writes=[blk2.r()], tag=pfx + "bk1")
    def emit_trow(c, tg):
        p.do("gpsimd", lambda e, c=c: e.iota(trow[:], pattern=[[1, TC]], base=c * TC,
                                             channel_multiplier=0,
                                             allow_small_or_imprecise_dtypes=True),
             writes=[trow.r()], tag=tg)
    def emit_isq(c, tg):
        emit_trow(c, tg + "t")
        p.do("scalar", lambda e: e.activation(
            lnrows[0:1, 0:TC], trow[0:1, 0:TC], AF.Sqrt,
            bias=ctx.constcol(1.0)[0:1, 0:1], scale=1.0),
            reads=[trow.r(), ctx.constcol(1.0).r()], writes=[(lnrows.name, 0)],
            tag=tg + "s")
        p.do("vector", lambda e: e.reciprocal(lnrows[0:1, TC:2 * TC], lnrows[0:1, 0:TC]),
             reads=[(lnrows.name, 0)], writes=[(lnrows.name, 1)], tag=tg + "r")
        p.do("tensor", lambda e: e.matmul(
            bank[3][:, 0:TC], ctx.ones_row[:], lnrows[0:1, TC:2 * TC],
            start=True, stop=True),
            reads=[(lnrows.name, 1), ctx.ones_row.r()], writes=[bank[3].r()],
            tag=tg + "b")
        p.do("scalar", lambda e: e.copy(isq[:], bank[3][:, 0:TC]),
             reads=[bank[3].r()], writes=[isq.r()], tag=tg + "e")

    for c in range(NCH):
        xsl = xin[:, c * TC:(c + 1) * TC].rearrange("(j p) t -> p j t", p=128)
        p.dma("sync", x_sb[:].rearrange("p (j t) -> p j t", j=8), xsl, sem_x,
              writes=[x_sb.r()], tag=f"{pfx}x{c}")
        emit_ln(ctx, x_sb, "c", 8, TC, a1, b1r, sqt, lnrows, ln_banks, f"{pfx}ln_{c}")
        for j in range(8):
            p.do("vector", lambda e, j=j: e.tensor_mul(
                x_sb[:, j * TC:(j + 1) * TC], x_sb[:, j * TC:(j + 1) * TC], a1[:]),
                reads=[(x_sb.name, ("c", j)), a1.r()],
                writes=[(x_sb.name, ("c", j))], tag=f"{pfx}y1_{c}_{j}")
            p.do("vector", lambda e, j=j, c=c: e.tensor_tensor(
                xn_sb[:, j * T + c * TC:j * T + (c + 1) * TC],
                x_sb[:, j * TC:(j + 1) * TC], b1r[:], ALU.add),
                reads=[(x_sb.name, ("c", j)), b1r.r()],
                writes=[(xn_sb.name, ("xn", c, j))], tag=f"{pfx}y2_{c}_{j}")

    def reduce_sin(src_ap, src_reads, dst, dst_key, c, shift, scale_in, tagp):
        p.do("vector", lambda e: e.tensor_scalar(
            qs[:], src_ap, scale_in * INV2PI, shift * INV2PI, ALU.mult, ALU.add),
            reads=src_reads, writes=[qs.r()], tag=tagp + "q")
        p.do("vector", lambda e: e.tensor_copy(ki[:], qs[:]),
             reads=[qs.r()], writes=[ki.r()], tag=tagp + "ki")
        p.do("vector", lambda e: e.tensor_copy(kf[:], ki[:]),
             reads=[ki.r()], writes=[kf.r()], tag=tagp + "kf")
        p.do("vector", lambda e: e.tensor_sub(qs[:], qs[:], kf[:]),
             reads=[qs.r(), kf.r()], writes=[qs.r()], tag=tagp + "d")
        p.do("scalar", lambda e: e.activation(
            dst[:, c * TC:(c + 1) * TC], qs[:], AF.Sin,
            bias=czero[:, 0:1], scale=c2pi[:, 0:1]),
            reads=[qs.r(), czero.r(), c2pi.r()], writes=[(dst.name, dst_key)],
            tag=tagp + "sin")

    for hi in range(8):
        p.dma("sync", gpair[:], w["gate"][hi], sem_g, writes=[gpair.r()],
              tag=f"{pfx}gp{hi}")
        p.do("tensor", lambda e: e.matmul(
            bank[3][:, 0:2], blk2[:], gpair[:], start=True, stop=True),
            reads=[blk2.r(), gpair.r()], writes=[bank[3].r()], tag=f"{pfx}gx{hi}")
        p.do("scalar", lambda e: e.copy(gcols[:], bank[3][:, 0:2]),
             reads=[bank[3].r()], writes=[gcols.r()], tag=f"{pfx}ge{hi}")
        p.dma("gpsimd", wk[:].rearrange("p (j e) -> p j e", j=8),
              w["Wk"][:, hi * 128:(hi + 1) * 128].rearrange("(j p) e -> p j e", p=128),
              sem_wk, writes=[wk.r()], tag=f"{pfx}wk{hi}")
        p.dma("gpsimd", wv[:].rearrange("p (j e) -> p j e", j=8),
              w["Wv"][:, hi * 128:(hi + 1) * 128].rearrange("(j p) e -> p j e", p=128),
              sem_wv, writes=[wv.r()], tag=f"{pfx}wv{hi}")
        for c in range(NCH):
            for which, wsb, dst, dkey in (("kr", wk, scr1, ("kr", c)),
                                          ("v", wv, v_sb, ("v", c))):
                bm = bank[0] if which == "kr" else bank[1]
                for dt in range(8):
                    p.do("tensor", lambda e, hi=hi, dt=dt, c=c, wsb=wsb, bm=bm: e.matmul(
                        bm[:, 0:TC], wsb[:, dt * 128:(dt + 1) * 128],
                        xn_sb[:, dt * T + c * TC:dt * T + (c + 1) * TC],
                        start=(dt == 0), stop=(dt == 7)),
                        reads=[wsb.r(), (xn_sb.name, ("xn", c, dt))],
                        writes=[bm.r()], tag=f"{pfx}{which}_{hi}_{c}_{dt}")
                wl = [(dst.name, dkey)]
                if which == "kr" and c == 0:
                    wl.append((dst.name, None))
                p.do("scalar", lambda e, c=c, dst=dst, bm=bm: e.copy(
                    dst[:, c * TC:(c + 1) * TC], bm[:, 0:TC]),
                    reads=[bm.r()], writes=wl,
                    tag=f"{pfx}{which}e_{hi}_{c}")
        for c in range(NCH):
            emit_trow(c, f"{pfx}ti_{hi}_{c}")
            p.do("tensor", lambda e, hi=hi, c=c: e.matmul(
                bank[2][:, 0:TC], frow[0:1, hi * 128:(hi + 1) * 128],
                trow[0:1, 0:TC], start=True, stop=True),
                reads=[frow.r(), trow.r()], writes=[bank[2].r()],
                tag=f"{pfx}ang_{hi}_{c}")
            reduce_sin(bank[2][:, 0:TC], [bank[2].r()], sr, ("sr", c), c,
                       0.0, 1.0, f"{pfx}sr_{hi}_{c}_")
            reduce_sin(bank[2][:, 0:TC], [bank[2].r()], cr, ("cr", c), c,
                       np.pi / 2, 1.0, f"{pfx}cr_{hi}_{c}_")
            reduce_sin(scr1[:, c * TC:(c + 1) * TC], [(scr1.name, ("kr", c))],
                       sk, ("sk", c), c, 0.0, 10.0, f"{pfx}sk_{hi}_{c}_")
            reduce_sin(scr1[:, c * TC:(c + 1) * TC], [(scr1.name, ("kr", c))],
                       ck, ("ck", c), c, np.pi / 2, 10.0, f"{pfx}ck_{hi}_{c}_")
        rck = [(ck.name, ("ck", c)) for c in range(NCH)]
        rsk = [(sk.name, ("sk", c)) for c in range(NCH)]
        rcr = [(cr.name, ("cr", c)) for c in range(NCH)]
        rsr = [(sr.name, ("sr", c)) for c in range(NCH)]
        rv = [(v_sb.name, ("v", c)) for c in range(NCH)]
        ones_bc = ctx.ones_col[:].broadcast_to([128, T])
        streams = [(cr, rcr, cr, rcr, False, 0), (sr, rsr, sr, rsr, False, 0),
                   (ck, rck, ck, rck, True, 1), (sk, rsk, sk, rsk, True, 1)]
        first = True
        for (bnd, rbnd, prb, rprb, shifted, gi) in streams:
            if shifted:
                p.do("vector", lambda e, bnd=bnd: e.tensor_tensor(
                    prod[:, 1:T], v_sb[:, 1:T], bnd[:, 0:T - 1], ALU.mult),
                    reads=rv + rbnd, writes=[prod.r()], tag=f"{pfx}pr_{hi}_{gi}")
                p.do("vector", lambda e: e.memset(prod[:, 0:1], 0.0),
                     reads=[prod.r()], writes=[prod.r()], tag=f"{pfx}pz_{hi}_{gi}")
            else:
                p.do("vector", lambda e, bnd=bnd: e.tensor_tensor(
                    prod[:], v_sb[:], bnd[:], ALU.mult),
                    reads=rv + rbnd, writes=[prod.r()], tag=f"{pfx}pr_{hi}_{gi}")
            p.do("vector", lambda e: e.tensor_tensor_scan(
                mem[:], ones_bc, prod[:], 0.0, ALU.mult, ALU.add),
                reads=[prod.r(), ctx.ones_col.r()], writes=[mem.r()],
                tag=f"{pfx}scan_{hi}_{gi}")
            p.do("vector", lambda e, prb=prb: e.tensor_tensor(
                mem[:], mem[:], prb[:], ALU.mult),
                reads=[mem.r()] + rprb, writes=[mem.r()], tag=f"{pfx}pm_{hi}_{gi}")
            if first:
                p.do("vector", lambda e, gi=gi: e.tensor_scalar_mul(
                    scr1[:], mem[:], gcols[:, gi:gi + 1]),
                    reads=[mem.r(), gcols.r()], writes=[(scr1.name, None)],
                    tag=f"{pfx}cb_{hi}_{gi}")
                first = False
            else:
                p.do("vector", lambda e, gi=gi: e.scalar_tensor_tensor(
                    scr1[:], mem[:], gcols[:, gi:gi + 1], scr1[:],
                    ALU.mult, ALU.add),
                    reads=[mem.r(), gcols.r(), (scr1.name, "cb")],
                    writes=[(scr1.name, "cb")], tag=f"{pfx}cb_{hi}_{gi}")
        for c in range(NCH):
            emit_isq(c, f"{pfx}isq_{hi}_{c}")
            p.do("vector", lambda e, c=c: e.tensor_tensor(
                prod[:, c * TC:(c + 1) * TC], scr1[:, c * TC:(c + 1) * TC],
                isq[:], ALU.mult),
                reads=[(scr1.name, "cb"), isq.r()],
                writes=[prod.r()], tag=f"{pfx}cbi_{hi}_{c}")
        p.dma("sync", stage[hi * 128:(hi + 1) * 128, :], prod[:], sem_st,
              reads=[prod.r()], writes=[("stage", hi)], tag=f"{pfx}st_{hi}")

    cb_sb = xn_sb
    for c in range(NCH):
        xsl = xin[:, c * TC:(c + 1) * TC].rearrange("(j p) t -> p j t", p=128)
        p.dma("sync", x_sb[:].rearrange("p (j t) -> p j t", j=8), xsl, sem_x2,
              writes=[x_sb.r()], tag=f"{pfx}x2_{c}")
        csl = stage[:, c * TC:(c + 1) * TC].rearrange("(j p) t -> p j t", p=128)
        p.dma("sync", cb_sb[:, 0:8 * TC].rearrange("p (j t) -> p j t", j=8), csl,
              sem_cb, reads=[("stage", j) for j in range(8)],
              writes=[(cb_sb.name, None)], tag=f"{pfx}cb2_{c}")
        for dot in range(8):
            bm = bank[dot % 2]
            for ci in range(8):
                p.do("tensor", lambda e, dot=dot, ci=ci, bm=bm: e.matmul(
                    bm[:, 0:TC], wo[:, ci * D + dot * 128:ci * D + (dot + 1) * 128],
                    cb_sb[:, ci * TC:(ci + 1) * TC],
                    start=(ci == 0), stop=(ci == 7)),
                    reads=[wo.r(), (cb_sb.name, None)],
                    writes=[bm.r()], tag=f"{pfx}wo_{c}_{dot}_{ci}")
            p.do("vector", lambda e, dot=dot, bm=bm: e.tensor_add(
                x_sb[:, dot * TC:(dot + 1) * TC], x_sb[:, dot * TC:(dot + 1) * TC],
                bm[:, 0:TC]),
                reads=[(x_sb.name, None), bm.r()],
                writes=[(x_sb.name, ("o", dot))], tag=f"{pfx}oa_{c}_{dot}")
        osl = xout[:, c * TC:(c + 1) * TC].rearrange("(j p) t -> p j t", p=128)
        p.dma("sync", osl, x_sb[:].rearrange("p (j t) -> p j t", j=8), sem_out,
              reads=[x_sb.r()] + [(x_sb.name, ("o", j)) for j in range(8)],
              tag=f"{pfx}out{c}")
    return sem_out


L3A_DRAM_SPECS = [("Wk", [1024, 1024], F32), ("Wv", [1024, 1024], F32),
                  ("Wo", [1024, 1024], F32), ("freqs", [1, 1024], F32),
                  ("gate", [8, 2, 2], F32)]


def build_l3a_standalone():
    nc = bass.Bass("TRN2", target_bir_lowering=False)
    stack = ExitStack()
    with stack:
        ctx = Ctx(nc, stack)
        xin = nc.dram_tensor("xT", [D, T], F32, kind="ExternalInput").ap()
        xout = nc.dram_tensor("yT", [D, T], F32, kind="ExternalOutput").ap()
        stage = nc.dram_tensor("l3stage", [D, T], BF16, kind="Internal").ap()
        w = {}
        for name, shape, dt in L3A_DRAM_SPECS:
            w[name] = nc.dram_tensor(name, shape, dt, kind="ExternalInput").ap()
        ctx.setup_consts()
        sem_out = emit_l3a(ctx, xin, xout, w, stage, "ha_")
        last = sem_out.inc_ops[-1]
        ctx.p._add_wait(ctx.p.do("sync", lambda e: e.nop(), tag="fin"), last)
        ctx.p.finalize()
    return nc





def emit_l1f(ctx, xin, xout, w, ys, xns, pfx):
    p = ctx.p
    TC = 512
    NCH = T // TC
    TS = 1024
    bank = ctx.banks
    ohot_bf = ctx.get_ohot(BF16)

    xsh = ctx.sb(pfx + "xsh", [128, 8 * TC], F32)
    h_sb = ctx.sb(pfx + "h", [128, 32 * TS], BF16)
    xn_sb = TBuf(h_sb.ap[:, 0:8 * TC], h_sb.name + "_xn")
    w1c = ctx.sb(pfx + "w1c", [128, 2 * 8 * 512], BF16)
    w2h = ctx.sb(pfx + "w2h", [128, 16 * D], BF16)
    acc = ctx.sb(pfx + "acc", [128, 8 * TS], F32)
    wg_sb = ctx.sb(pfx + "wg", [128, 8 * 8], BF16)
    bg_sb = ctx.sb(pfx + "bgsb", [1, 8], BF16)
    b1c = ctx.sb(pfx + "b1c", [128, 32], F32)
    b2row = ctx.sb(pfx + "b2row", [1, D], BF16)
    b1rows = ctx.sb(pfx + "b1rows", [32, 128], F32)
    onr512 = ctx.sb(pfx + "onr512", [1, TC], BF16)
    w8T = ctx.sb(pfx + "w8T", [8, T], BF16)
    _hf32 = h_sb.ap.bitcast(F32)
    rout = TBuf(_hf32[:, 2048:2304], h_sb.name + "_rout")
    m1i = TBuf(_hf32[:, 2304:2560], h_sb.name + "_m1i")
    cmb = TBuf(_hf32[:, 2560:2816], h_sb.name + "_cmb")
    cls = ctx.sb(pfx + "cls", [128, 128], F32)
    wbc_sb = ctx.sb(pfx + "wbc", [128, TC], F32)
    t1_sb = ctx.sb(pfx + "t1", [128, TC], F32)
    sqt = ctx.sb(pfx + "sq", [128, 2 * TC], F32)
    lnsq = ctx.sb(pfx + "lnsq", [128, 4 * TC], BF16)
    a1 = ctx.sb(pfx + "a1", [128, TC], BF16)
    b1r = ctx.sb(pfx + "b1r", [128, TC], BF16)
    a2 = ctx.sb(pfx + "a2", [128, TC], BF16)
    b2r = ctx.sb(pfx + "b2r", [128, TC], BF16)
    lnrows = ctx.sb(pfx + "lnrows", [1, 4 * TC], F32)
    gc = load_cols(ctx, pfx + "gc", w["gamma"], 128, 8, None, bank[6])
    bc_ = load_cols(ctx, pfx + "bc", w["beta"], 128, 8, None, bank[6])
    xn_super = xsh.ap.bitcast(BF16)

    sem_x = p.dma_sem(pfx + "sx")
    sem_y = p.dma_sem(pfx + "sy")
    sem_xn = p.dma_sem(pfx + "sxn")
    sem_xs = p.dma_sem(pfx + "sxs")
    sem_x2 = p.dma_sem(pfx + "sx2")
    sem_y2 = p.dma_sem(pfx + "sy2")
    sem_out = p.dma_sem(pfx + "so")
    sem_w1 = [p.dma_sem(pfx + f"sw1_{i}") for i in range(2)]
    sem_w2 = [p.dma_sem(pfx + f"sw2_{i}") for i in range(2)]
    sem_eb = p.dma_sem(pfx + "seb")
    sem_eb2 = p.dma_sem(pfx + "seb2")
    ln_banks = {"s1": bank[4], "s2": bank[5], "bc": bank[6]}

    p.dma("gpsimd", wg_sb[:].rearrange("p (j e) -> p j e", j=8),
          w["WgT"].rearrange("(j p) e -> p j e", p=128),
          p.dma_sem(pfx + "swg"), writes=[wg_sb.r()], tag=pfx + "wg")
    p.dma("gpsimd", bg_sb[:], w["bg"], p.dma_sem(pfx + "sbg"),
          writes=[bg_sb.r()], tag=pfx + "bg")
    p.do("gpsimd", lambda e: e.memset(onr512[:], 1.0), writes=[onr512.r()],
         tag=pfx + "onr")

    for c in range(NCH):
        xsl = xin[:, c * TC:(c + 1) * TC].rearrange("(j p) t -> p j t", p=128)
        p.dma("sync", xsh[:].rearrange("p (j t) -> p j t", j=8), xsl, sem_x,
              writes=[xsh.r()], tag=f"{pfx}x{c}")
        emit_ln(ctx, xsh, "c", 8, TC, a1, b1r, lnsq, lnrows, ln_banks, f"{pfx}ln1_{c}")
        for j in range(8):
            p.do("vector", lambda e, j=j: e.tensor_mul(
                xsh[:, j * TC:(j + 1) * TC], xsh[:, j * TC:(j + 1) * TC], a1[:]),
                reads=[(xsh.name, ("c", j)), a1.r()],
                writes=[(xsh.name, ("c", j))], tag=f"{pfx}y1_{c}_{j}")
            p.do("vector", lambda e, j=j: e.tensor_add(
                xsh[:, j * TC:(j + 1) * TC], xsh[:, j * TC:(j + 1) * TC], b1r[:]),
                reads=[(xsh.name, ("c", j)), b1r.r()],
                writes=[(xsh.name, ("c", j)), (xsh.name, ("y", j))], tag=f"{pfx}y2_{c}_{j}")
        ysl = ys[:, c * TC:(c + 1) * TC].rearrange("(j p) t -> p j t", p=128)
        p.dma("sync", ysl, xsh[:].rearrange("p (j t) -> p j t", j=8), sem_y,
              reads=[xsh.r()] + [(xsh.name, ("y", j)) for j in range(8)],
              tag=f"{pfx}ys{c}")
        emit_ln(ctx, xsh, "y", 8, TC, a2, b2r, lnsq, lnrows, ln_banks,
                f"{pfx}ln2_{c}")
        for j in range(8):
            jj = j % 2
            p.do("vector", lambda e, j=j, jj=jj: e.tensor_mul(
                sqt[:, jj * TC:(jj + 1) * TC], xsh[:, j * TC:(j + 1) * TC], a2[:]),
                reads=[(xsh.name, ("y", j)), a2.r()],
                writes=[(sqt.name, jj), sqt.r()], tag=f"{pfx}t2a_{c}_{j}")
            p.do("vector", lambda e, j=j, jj=jj: e.tensor_add(
                sqt[:, jj * TC:(jj + 1) * TC], sqt[:, jj * TC:(jj + 1) * TC], b2r[:]),
                reads=[(sqt.name, jj), b2r.r()],
                writes=[(sqt.name, jj)], tag=f"{pfx}t2b_{c}_{j}")
            p.do("vector", lambda e, j=j, jj=jj: e.tensor_scalar(
                xn_sb[:, j * TC:(j + 1) * TC], sqt[:, jj * TC:(jj + 1) * TC],
                gc[:, j:j + 1], bc_[:, j:j + 1], ALU.mult, ALU.add),
                reads=[(sqt.name, jj), gc.r(), bc_.r()],
                writes=[(xn_sb.name, ("c", j))], tag=f"{pfx}xnp_{c}_{j}")
        xnsl = xns[:, c * TC:(c + 1) * TC].rearrange("(j p) t -> p j t", p=128)
        p.dma("sync", xnsl, xn_sb[:].rearrange("p (j t) -> p j t", j=8), sem_xn,
              reads=[xn_sb.r()] + [(xn_sb.name, ("c", j)) for j in range(8)],
              tag=f"{pfx}xns{c}")
        for ts in range(4):
            g = c * 4 + ts
            bR = bank[2]
            for dt in range(8):
                p.do("tensor", lambda e, ts=ts, dt=dt, bR=bR: e.matmul(
                    bR[:, 0:8], xn_sb[:, dt * TC + ts * 128:dt * TC + (ts + 1) * 128],
                    wg_sb[:, dt * 8:(dt + 1) * 8],
                    start=(dt == 0), stop=False),
                    reads=[(xn_sb.name, ("c", dt)), wg_sb.r()],
                    writes=[bR.r()], tag=f"{pfx}rl_{c}_{ts}_{dt}")
            p.do("tensor", lambda e, bR=bR: e.matmul(
                bR[:, 0:8], ctx.ones_row_bf[:], bg_sb[:],
                start=False, stop=True),
                reads=[bg_sb.r(), ctx.ones_row_bf.r()],
                writes=[bR.r()], tag=f"{pfx}rb_{c}_{ts}")
            p.do("scalar", lambda e, g=g, bR=bR: e.copy(
                rout[:, g * 8:(g + 1) * 8], bR[:, 0:8]),
                reads=[bR.r()], writes=[(rout.name, g)], tag=f"{pfx}re_{c}_{ts}")

    NE = 8
    ALLR = [(rout.name, g) for g in range(32)]
    r3 = rout[:].rearrange("p (g k) -> p g k", k=NE)
    c3 = lambda sl: cls[:, sl].rearrange("p (g o) -> p g o", o=1).broadcast_to([128, 32, NE])
    q3 = cmb[:].rearrange("p (g k) -> p g k", k=NE)
    m3 = m1i[:].rearrange("p (g k) -> p g k", k=NE)
    p.do("vector", lambda e: e.reduce_max(
        cls[:, 0:32], r3, axis=AX.X, negate=True),
        reads=ALLR, writes=[(cls.name, "nm")], tag=pfx + "nm")
    p.do("vector", lambda e: e.tensor_tensor(
        r3, r3, c3(slice(0, 32)), ALU.add),
        reads=ALLR + [(cls.name, "nm")], writes=[(rout.name, "e")], tag=pfx + "dsub")
    p.do("scalar", lambda e: e.activation(rout[:], rout[:], AF.Exp),
         reads=[(rout.name, "e")], writes=[(rout.name, "e")], tag=pfx + "exp")
    p.do("vector", lambda e: e.reduce_sum(cls[:, 96:128], r3, axis=AX.X),
         reads=[(rout.name, "e")], writes=[(cls.name, "S")], tag=pfx + "S")
    p.do("vector", lambda e: e.tensor_single_scalar(m1i[:], rout[:], 1.0, ALU.is_lt),
         reads=[(rout.name, "e")], writes=[m1i.r()], tag=pfx + "m1i")
    p.do("vector", lambda e: e.tensor_mul(cmb[:], rout[:], m1i[:]),
         reads=[(rout.name, "e"), m1i.r()], writes=[cmb.r()], tag=pfx + "p2")
    p.do("vector", lambda e: e.reduce_max(cls[:, 32:64], q3, axis=AX.X),
         reads=[cmb.r()], writes=[(cls.name, "m2")], tag=pfx + "m2")
    p.do("vector", lambda e: e.tensor_tensor(
        q3, q3, c3(slice(32, 64)), ALU.subtract),
        reads=[cmb.r(), (cls.name, "m2")], writes=[cmb.r()], tag=pfx + "d2")
    p.do("vector", lambda e: e.tensor_single_scalar(cmb[:], cmb[:], 0.0, ALU.is_ge),
         reads=[cmb.r()], writes=[cmb.r()], tag=pfx + "mk2")
    p.do("vector", lambda e: e.scalar_tensor_tensor(
        m1i[:], m1i[:], -1.0, cmb[:], ALU.mult, ALU.add),
        reads=[m1i.r(), cmb.r()], writes=[m1i.r()], tag=pfx + "msk")
    p.do("vector", lambda e: e.tensor_single_scalar(
        cls[:, 32:64], cls[:, 32:64], 1.0, ALU.add),
        reads=[(cls.name, "m2")], writes=[(cls.name, "m2")], tag=pfx + "dn")
    p.do("vector", lambda e: e.tensor_single_scalar(m1i[:], m1i[:], 1.0, ALU.add),
         reads=[m1i.r()], writes=[m1i.r()], tag=pfx + "msk1")
    p.do("vector", lambda e: e.scalar_tensor_tensor(
        cls[:, 32:64], cls[:, 96:128], 1e-8, cls[:, 32:64], ALU.mult, ALU.add),
        reads=[(cls.name, "S"), (cls.name, "m2")], writes=[(cls.name, "m2")],
        tag=pfx + "dne")
    p.do("vector", lambda e: e.tensor_mul(cmb[:], rout[:], m1i[:]),
         reads=[(rout.name, "e"), m1i.r()], writes=[cmb.r()], tag=pfx + "wmul")
    p.do("vector", lambda e: e.reciprocal(cls[:, 64:96], cls[:, 32:64]),
         reads=[(cls.name, "m2")], writes=[(cls.name, "rc")], tag=pfx + "rc")
    p.do("gpsimd", lambda e: e.tensor_tensor(
        q3, q3, c3(slice(64, 96)), ALU.mult),
        reads=[cmb.r(), (cls.name, "rc")], writes=[(cmb.name, "w")], tag=pfx + "wfin")
    for g in range(32):
        p.do("tensor", lambda e, g=g: e.transpose(
            bank[3][0:8, 0:128], cmb[:, g * 8:(g + 1) * 8], ctx.idn[:]),
            reads=[(cmb.name, "w"), ctx.idn.r()],
            writes=[bank[3].r()], tag=f"{pfx}wtt_{g}")
        p.do("scalar", lambda e, g=g: e.copy(
            w8T[0:8, g * 128:(g + 1) * 128], bank[3][0:8, 0:128]),
            reads=[bank[3].r()], writes=[(w8T.name, ("g", g))],
            tag=f"{pfx}wte_{g}")

    p.claim(xn_sb.name, tag=pfx + "xnclaim")
    p.claim(h_sb.name, tag=pfx + "hclaim")
    W8TALL = [(w8T.name, ("g", g)) for g in range(32)]
    for s in range(4):
        xssl = xns[:, s * TS:(s + 1) * TS].rearrange("(j p) t -> p j t", p=128)
        p.dma("sync", xn_super.rearrange("p (j t) -> p j t", j=8), xssl, sem_xs,
              writes=[(xsh.name, None)], tag=f"{pfx}xs{s}")
        p.do("gpsimd", lambda e: e.memset(acc[:], 0.0), writes=[acc.r()],
             tag=f"{pfx}acc0_{s}")
        for ei in range(8):
            p.dma("sync", b1rows[:], w["b1"][ei], sem_eb,
                  writes=[b1rows.r()], tag=f"{pfx}b1r{s}_{ei}")
            p.do("tensor", lambda e: e.transpose(
                bank[3][:, 0:32], b1rows[:], ctx.idn[0:32, 0:32]),
                reads=[b1rows.r(), ctx.idn.r()], writes=[bank[3].r()],
                tag=f"{pfx}b1t{s}_{ei}")
            p.do("scalar", lambda e: e.copy(b1c[:], bank[3][:, 0:32]),
                 reads=[bank[3].r()], writes=[b1c.r()], tag=f"{pfx}b1e{s}_{ei}")
            p.dma("gpsimd", b2row[:], w["b2"][ei], sem_eb2,
                  writes=[b2row.r()], tag=f"{pfx}b2r{s}_{ei}")
            for fb in range(8):
                buf = fb % 2
                p.dma("gpsimd",
                      w1c[:, buf * 4096:(buf + 1) * 4096].rearrange(
                          "p (j f) -> p j f", j=8),
                      w["W1T"][ei][:, fb * 512:(fb + 1) * 512].rearrange(
                          "(j p) f -> p j f", p=128),
                      sem_w1[buf], writes=[(w1c.name, buf)], tag=f"{pfx}w1_{s}_{ei}_{fb}")
                for ftl in range(4):
                    ft = fb * 4 + ftl
                    for th in range(2):
                        bm = bank[(ft * 2 + th) % 2]
                        for dt in range(8):
                            p.do("tensor", lambda e, dt=dt, ftl=ftl, th=th, buf=buf, bm=bm: e.matmul(
                                bm[:, 0:TC],
                                w1c[:, buf * 4096 + dt * 512 + ftl * 128:buf * 4096 + dt * 512 + (ftl + 1) * 128],
                                xn_super[:, dt * TS + th * TC:dt * TS + (th + 1) * TC],
                                start=(dt == 0), stop=(dt == 7)),
                                reads=[(w1c.name, buf), (xsh.name, None)],
                                writes=[bm.r()], tag=f"{pfx}h_{s}_{ei}_{ft}_{th}_{dt}")
                        p.do("scalar", lambda e, ft=ft, th=th, bm=bm: e.activation(
                            h_sb[:, ft * TS + th * TC:ft * TS + (th + 1) * TC],
                            bm[:, 0:TC], AF.Gelu, bias=b1c[:, ft:ft + 1], scale=1.0),
                            reads=[bm.r(), b1c.r()], writes=[(h_sb.name, (ft, th))],
                            tag=f"{pfx}hev_{s}_{ei}_{ft}_{th}")
            for hf in range(4):
                qb = hf % 2
                p.dma("gpsimd",
                      w2h[:, qb * 8 * D:(qb + 1) * 8 * D].rearrange(
                          "p (j n) -> p j n", j=8),
                      w["W2T"][ei][hf * 1024:(hf + 1) * 1024, :].rearrange(
                          "(j p) n -> p j n", p=128),
                      sem_w2[qb], writes=[(w2h.name, qb)], tag=f"{pfx}w2_{s}_{ei}_{hf}")
                for th in range(2):
                    p.do("tensor", lambda e, ei=ei, s=s, th=th: e.matmul(
                        bank[2][:, 0:TC], ohot_bf[0:8, ei * 128:(ei + 1) * 128],
                        w8T[0:8, s * TS + th * TC:s * TS + (th + 1) * TC],
                        start=True, stop=True),
                        reads=W8TALL + [ohot_bf.r()],
                        writes=[bank[2].r()], tag=f"{pfx}wbc_{s}_{ei}_{hf}_{th}")
                    p.do("scalar", lambda e: e.copy(wbc_sb[:], bank[2][:, 0:TC]),
                         reads=[bank[2].r()], writes=[wbc_sb.r()],
                         tag=f"{pfx}wbce_{s}_{ei}_{hf}_{th}")
                    for dot in range(8):
                        bz = bank[3 + dot % 2]
                        for fj in range(8):
                            p.do("tensor", lambda e, dot=dot, fj=fj, hf=hf, th=th, qb=qb, bz=bz: e.matmul(
                                bz[:, 0:TC],
                                w2h[:, qb * 8 * D + fj * D + dot * 128:qb * 8 * D + fj * D + (dot + 1) * 128],
                                h_sb[:, (hf * 8 + fj) * TS + th * TC:(hf * 8 + fj) * TS + (th + 1) * TC],
                                start=(fj == 0), stop=(hf != 0 and fj == 7)),
                                reads=[(w2h.name, qb), (h_sb.name, (hf * 8 + fj, th))],
                                writes=[bz.r()], tag=f"{pfx}z_{s}_{ei}_{hf}_{th}_{dot}_{fj}")
                        if hf == 0:
                            p.do("tensor", lambda e, dot=dot, bz=bz: e.matmul(
                                bz[:, 0:TC], b2row[0:1, dot * 128:(dot + 1) * 128],
                                onr512[:], start=False, stop=True),
                                reads=[b2row.r(), onr512.r()],
                                writes=[bz.r()], tag=f"{pfx}zb_{s}_{ei}_{th}_{dot}")
                        p.do("vector", lambda e, bz=bz: e.tensor_mul(
                            t1_sb[:], bz[:, 0:TC], wbc_sb[:]),
                            reads=[bz.r(), wbc_sb.r()], writes=[t1_sb.r()],
                            tag=f"{pfx}zt_{s}_{ei}_{hf}_{th}_{dot}")
                        p.do("vector", lambda e, dot=dot, th=th: e.tensor_add(
                            acc[:, dot * TS + th * TC:dot * TS + (th + 1) * TC],
                            acc[:, dot * TS + th * TC:dot * TS + (th + 1) * TC],
                            t1_sb[:]),
                            reads=[(acc.name, (dot, th)), t1_sb.r()],
                            writes=[(acc.name, (dot, th))],
                            tag=f"{pfx}za_{s}_{ei}_{hf}_{th}_{dot}")
        for cl in range(2):
            c = s * 2 + cl
            xsl = xin[:, c * TC:(c + 1) * TC].rearrange("(j p) t -> p j t", p=128)
            p.dma("sync", xsh[:].rearrange("p (j t) -> p j t", j=8), xsl, sem_x2,
                  writes=[(xsh.name, None)], tag=f"{pfx}x2_{c}")
            for j in range(8):
                p.dma("sync", t1_sb[:], ys[j * 128:(j + 1) * 128, c * TC:(c + 1) * TC],
                      sem_y2, writes=[t1_sb.r()], tag=f"{pfx}y2_{c}_{j}")
                p.do("vector", lambda e, j=j: e.tensor_add(
                    xsh[:, j * TC:(j + 1) * TC], xsh[:, j * TC:(j + 1) * TC], t1_sb[:]),
                    reads=[(xsh.name, None), t1_sb.r()],
                    writes=[(xsh.name, ("o", j))], tag=f"{pfx}oa_{c}_{j}")
                p.do("vector", lambda e, j=j, cl=cl: e.tensor_add(
                    xsh[:, j * TC:(j + 1) * TC], xsh[:, j * TC:(j + 1) * TC],
                    acc[:, j * TS + cl * TC:j * TS + (cl + 1) * TC]),
                    reads=[(xsh.name, ("o", j)), (acc.name, (j, cl))],
                    writes=[(xsh.name, ("o", j))], tag=f"{pfx}ob_{c}_{j}")
            osl = xout[:, c * TC:(c + 1) * TC].rearrange("(j p) t -> p j t", p=128)
            p.dma("sync", osl, xsh[:].rearrange("p (j t) -> p j t", j=8), sem_out,
                  reads=[xsh.r()] + [(xsh.name, ("o", j)) for j in range(8)],
                  tag=f"{pfx}out{c}")
    return sem_out




L1F_DRAM_SPECS = [
    ("WgT", [1024, 8], F32), ("bg", [1, 8], F32),
    ("W1T", [8, 1024, 4096], F32), ("W2T", [8, 4096, 1024], F32),
    ("b1", [8, 32, 128], F32), ("b2", [8, 1, 1024], F32),
    ("gamma", [8, 128], F32), ("beta", [8, 128], F32),
]


def build_l1f_standalone():
    nc = bass.Bass("TRN2", target_bir_lowering=False)
    stack = ExitStack()
    with stack:
        ctx = Ctx(nc, stack)
        xin = nc.dram_tensor("xT", [D, T], F32, kind="ExternalInput").ap()
        xout = nc.dram_tensor("yT", [D, T], F32, kind="ExternalOutput").ap()
        ys = nc.dram_tensor("l1y", [D, T], F32, kind="Internal").ap()
        xns = nc.dram_tensor("l1xn", [D, T], BF16, kind="Internal").ap()
        w = {}
        for name, shape, dt in L1F_DRAM_SPECS:
            w[name] = nc.dram_tensor(name, shape, dt, kind="ExternalInput").ap()
        ctx.setup_consts()
        sem_out = emit_l1f(ctx, xin, xout, w, ys, xns, "gf_")
        last = sem_out.inc_ops[-1]
        ctx.p._add_wait(ctx.p.do("sync", lambda e: e.nop(), tag="fin"), last)
        ctx.p.finalize()
    return nc





def emit_attn_core(ctx, pfx, big, qt, kt, vhm, bqch, bkch, outw, obcol,
                   xin, xout, masks4, cinv8, x_sb=None):
    p = ctx.p
    TC = 512
    NCH = T // TC
    bank = ctx.banks
    nm = big.name
    at = TBuf(big.ap[:, 0:1536], nm)
    oc = TBuf(big.ap[:, 1536:5632], nm)
    ot = TBuf(big.ap[:, 5632:9728], nm)
    kbuf = [TBuf(big.ap[0:64, 9728 + i * 4096:9728 + (i + 1) * 4096], nm) for i in range(2)]
    vaug = [TBuf(big.ap[:, 17920 + i * 2080:17920 + (i + 1) * 2080], nm) for i in range(2)]
    qbuf = [TBuf(big.ap[0:64, 22080 + i * 512:22080 + (i + 1) * 512], nm) for i in range(2)]
    if x_sb is None:
        x_sb = ctx.sb(pfx + "cx", [128, 8 * TC], F32)
    bcols = ctx.sb(pfx + "bcols", [128, 16], F32)
    b2x8 = ctx.sb(pfx + "b2x8", [2, 2 * 8], F32)
    rlc = ctx.sb(pfx + "rlc", [128, 1], F32)
    t1c = ctx.sb(pfx + "t1c", [128, TC], F32)
    sel2 = ctx.sb(pfx + "sel2", [2, 256], F32)

    sem_q = [p.dma_sem(pfx + f"sq{i}") for i in range(2)]
    sem_k = [p.dma_sem(pfx + f"sk{i}") for i in range(2)]
    sem_v = [p.dma_sem(pfx + f"sv{i}") for i in range(2)]
    sem_x = p.dma_sem(pfx + "sx")
    sem_out = p.dma_sem(pfx + "so")

    p.do("gpsimd", lambda e: e.memset(sel2[:], 1.0), writes=[sel2.r()], tag=pfx + "sel0")
    p.do("gpsimd", lambda e: e.affine_select(
        out=sel2[:], in_=sel2[:], compare_op=ALU.is_equal, fill=0.0,
        base=0, pattern=[[-1, 2], [0, 128]], channel_multiplier=1),
        reads=[sel2.r()], writes=[sel2.r()], tag=pfx + "sel1")

    p.do("vector", lambda e: e.reduce_max(
        b2x8[:, 0:8], bqch[:].rearrange("p (a b) -> p a b", b=8), axis=AX.X),
        reads=[bqch.r()], writes=[(b2x8.name, "q")], tag=pfx + "bq")
    p.do("vector", lambda e: e.reduce_max(
        b2x8[:, 8:16], bkch[:].rearrange("p (a b) -> p a b", b=8), axis=AX.X),
        reads=[bkch.r()], writes=[(b2x8.name, "k")], tag=pfx + "bk")
    p.do("scalar", lambda e: e.activation(b2x8[:, 0:8], b2x8[:, 0:8], AF.Sqrt),
         reads=[(b2x8.name, "q")], writes=[(b2x8.name, "q")], tag=pfx + "sq1")
    p.do("scalar", lambda e: e.activation(b2x8[:, 8:16], b2x8[:, 8:16], AF.Sqrt),
         reads=[(b2x8.name, "k")], writes=[(b2x8.name, "k")], tag=pfx + "sq2")
    p.do("vector", lambda e: e.tensor_mul(b2x8[:, 0:8], b2x8[:, 0:8], b2x8[:, 8:16]),
         reads=[(b2x8.name, "q"), (b2x8.name, "k")], writes=[(b2x8.name, "q")],
         tag=pfx + "bmul")
    p.do("scalar", lambda e: e.mul(b2x8[:, 0:8], b2x8[:, 0:8], -0.125),
         reads=[(b2x8.name, "q")], writes=[(b2x8.name, "q")], tag=pfx + "bneg")
    for h in range(16):
        half, et = h % 2, h // 2
        p.do("tensor", lambda e, half=half, et=et: e.matmul(
            bank[6][:, 0:1], sel2[:, half * 128:(half + 1) * 128],
            b2x8[:, et:et + 1], start=True, stop=True),
            reads=[(b2x8.name, "q"), sel2.r()], writes=[bank[6].r()],
            tag=f"{pfx}bb{h}")
        p.do("scalar", lambda e, h=h: e.copy(bcols[:, h:h + 1], bank[6][:, 0:1]),
             reads=[bank[6].r()], writes=[(bcols.name, h)], tag=f"{pfx}be{h}")

    p.claim(nm, tag=pfx + "bigclaim")
    for i in range(2):
        p.do("gpsimd", lambda e, i=i: e.memset(vaug[i][:], 1.0),
             writes=[(nm, ("va", i))], tag=f"{pfx}vone{i}")

    for qc in range(NCH):
        nkt = 4 * qc + 4
        xsl = xin[:, qc * TC:(qc + 1) * TC].rearrange("(j p) t -> p j t", p=128)
        p.dma("sync", x_sb[:].rearrange("p (j t) -> p j t", j=8), xsl, sem_x,
              writes=[x_sb.r()], tag=f"{pfx}x{qc}")
        for h in range(16):
            hb = h % 2
            p.dma("sync", qbuf[hb][:], qt[h * 64:(h + 1) * 64, qc * TC:(qc + 1) * TC],
                  sem_q[hb], writes=[(nm, ("qb", hb))], tag=f"{pfx}q_{qc}_{h}")
            p.dma("sync", kbuf[hb][0:64, 0:nkt * 128],
                  kt[h * 64:(h + 1) * 64, 0:nkt * 128],
                  sem_k[hb], writes=[(nm, ("kb", hb))], tag=f"{pfx}k_{qc}_{h}")
            p.dma("sync",
                  vaug[hb][:, 0:nkt * 65].rearrange("p (k c) -> p k c", c=65)[:, :, 0:64],
                  vhm[h, 0:nkt * 128, :].rearrange("(k p) c -> p k c", p=128),
                  sem_v[hb], writes=[(nm, ("vdat", hb))], tag=f"{pfx}v_{qc}_{h}")
            for kti in range(nkt):
                bs = bank[kti % 2]
                ar = kti % 3
                p.do("tensor", lambda e, kti=kti, hb=hb, bs=bs: e.matmul(
                    bs[:, 0:TC], kbuf[hb][0:64, kti * 128:(kti + 1) * 128], qbuf[hb][:],
                    start=True, stop=True),
                    reads=[(nm, ("kb", hb)), (nm, ("qb", hb))], writes=[bs.r()],
                    tag=f"{pfx}s_{qc}_{h}_{kti}")
                p.do("scalar", lambda e, ar=ar, h=h, bs=bs: e.activation(
                    at[:, ar * TC:(ar + 1) * TC], bs[:, 0:TC], AF.Exp,
                    bias=bcols[:, h:h + 1], scale=cinv8[:, 0:1]),
                    reads=[bs.r(), (bcols.name, h), cinv8.r()],
                    writes=[(nm, ("at", ar))], tag=f"{pfx}e_{qc}_{h}_{kti}")
                off = kti - 4 * qc
                if off >= 0:
                    p.do("vector", lambda e, ar=ar, off=off: e.tensor_mul(
                        at[:, ar * TC:(ar + 1) * TC],
                        at[:, ar * TC:(ar + 1) * TC],
                        masks4[:, off * 512:(off + 1) * 512]),
                        reads=[(nm, ("at", ar)), masks4.r()],
                        writes=[(nm, ("at", ar))], tag=f"{pfx}m_{qc}_{h}_{kti}")
                for j in range(4):
                    bo = bank[2 + j]
                    p.do("tensor", lambda e, kti=kti, j=j, ar=ar, hb=hb, bo=bo, nkt=nkt: e.matmul(
                        bo[:, 0:65],
                        at[:, ar * TC + j * 128:ar * TC + (j + 1) * 128],
                        vaug[hb][:, kti * 65:(kti + 1) * 65],
                        start=(kti == 0), stop=(kti == nkt - 1)),
                        reads=[(nm, ("at", ar)), (nm, ("va", hb)), (nm, ("vdat", hb))],
                        writes=[(bo.name, None)], tag=f"{pfx}av_{qc}_{h}_{kti}_{j}")
            for j in range(4):
                bo = bank[2 + j]
                p.do("vector", lambda e, bo=bo: e.reciprocal(rlc[:], bo[:, 64:65]),
                     reads=[(bo.name, None)], writes=[rlc.r()],
                     tag=f"{pfx}rl_{qc}_{h}_{j}")
                p.do("scalar", lambda e, j=j, h=h, bo=bo: e.activation(
                    oc[:, j * 1024 + h * 64:j * 1024 + (h + 1) * 64], bo[:, 0:64],
                    AF.Identity, bias=ctx.constcol(0.0)[:, 0:1], scale=rlc[:, 0:1]),
                    reads=[(bo.name, None), rlc.r(), ctx.constcol(0.0).r()],
                    writes=[(nm, ("oc", j, h))], tag=f"{pfx}oe_{qc}_{h}_{j}")
        for j in range(4):
            for eb in range(8):
                p.do("tensor", lambda e, j=j, eb=eb: e.transpose(
                    ctx.bankT[:, 0:128],
                    oc[:, j * 1024 + eb * 128:j * 1024 + (eb + 1) * 128],
                    ctx.idn_bf[:]),
                    reads=[(nm, ("oc", j, hh)) for hh in range(2 * eb, 2 * eb + 2)]
                    + [ctx.idn_bf.r()],
                    writes=[ctx.bankT.r()], tag=f"{pfx}tr_{qc}_{j}_{eb}")
                p.do("vector", lambda e, j=j, eb=eb: e.tensor_copy(
                    ot[:, eb * 512 + j * 128:eb * 512 + (j + 1) * 128],
                    ctx.bankT[:, 0:128]),
                    reads=[ctx.bankT.r()], writes=[(nm, ("ot", j, eb))],
                    tag=f"{pfx}te_{qc}_{j}_{eb}")
        for dot in range(8):
            bm = bank[dot % 2]
            for eb in range(8):
                p.do("tensor", lambda e, dot=dot, eb=eb, bm=bm: e.matmul(
                    bm[:, 0:TC], outw[:, eb * 1024 + dot * 128:eb * 1024 + (dot + 1) * 128],
                    ot[:, eb * 512:(eb + 1) * 512],
                    start=(eb == 0), stop=(eb == 7)),
                    reads=[outw.r()] + [(nm, ("ot", j, eb)) for j in range(4)],
                    writes=[bm.r()], tag=f"{pfx}op_{qc}_{dot}_{eb}")
            if obcol is not None:
                p.do("scalar", lambda e, dot=dot, bm=bm: e.activation(
                    t1c[:], bm[:, 0:TC], AF.Identity,
                    bias=obcol[:, dot:dot + 1], scale=ctx.constcol(1.0)[:, 0:1]),
                    reads=[bm.r(), obcol.r(), ctx.constcol(1.0).r()],
                    writes=[t1c.r()], tag=f"{pfx}ob_{qc}_{dot}")
            else:
                p.do("scalar", lambda e, bm=bm: e.copy(t1c[:], bm[:, 0:TC]),
                     reads=[bm.r()], writes=[t1c.r()], tag=f"{pfx}oc_{qc}_{dot}")
            p.do("vector", lambda e, dot=dot: e.tensor_add(
                x_sb[:, dot * TC:(dot + 1) * TC], x_sb[:, dot * TC:(dot + 1) * TC],
                t1c[:]),
                reads=[(x_sb.name, None), t1c.r()],
                writes=[(x_sb.name, ("o", dot))], tag=f"{pfx}oa_{qc}_{dot}")
        osl = xout[:, qc * TC:(qc + 1) * TC].rearrange("(j p) t -> p j t", p=128)
        p.dma("sync", osl, x_sb[:].rearrange("p (j t) -> p j t", j=8), sem_out,
              reads=[x_sb.r()] + [(x_sb.name, ("o", j)) for j in range(8)],
              tag=f"{pfx}out{qc}")
    return sem_out


def emit_sin_chain(ctx, src_ap, src_reads, dst_ap, dst_writes, qs, ki, kf,
                   shift, scale_in, tag):
    p = ctx.p
    INV2PI = 1.0 / (2 * np.pi)
    c2pi = ctx.constcol(2 * np.pi)
    czero = ctx.constcol(0.0)
    p.do("vector", lambda e: e.tensor_scalar(
        qs[:], src_ap, scale_in * INV2PI, shift * INV2PI, ALU.mult, ALU.add),
        reads=src_reads, writes=[qs.r()], tag=tag + "q")
    p.do("vector", lambda e: e.tensor_copy(ki[:], qs[:]),
         reads=[qs.r()], writes=[ki.r()], tag=tag + "ki")
    p.do("vector", lambda e: e.tensor_copy(kf[:], ki[:]),
         reads=[ki.r()], writes=[kf.r()], tag=tag + "kf")
    p.do("vector", lambda e: e.tensor_sub(qs[:], qs[:], kf[:]),
         reads=[qs.r(), kf.r()], writes=[qs.r()], tag=tag + "d")
    p.do("scalar", lambda e: e.activation(
        dst_ap, qs[:], AF.Sin, bias=czero[:, 0:1], scale=c2pi[:, 0:1]),
        reads=[qs.r(), czero.r(), c2pi.r()], writes=dst_writes, tag=tag + "sin")



def emit_l0a(ctx, xin, xout, w, qts, kts, vhms, pfx):
    p = ctx.p
    TC = 512
    NCH = T // TC
    bank = ctx.banks

    big = ctx.sb(pfx + "big", [128, 32768], BF16)
    nm = big.name
    wA = ctx.sb(pfx + "wA", [128, 8 * D], BF16)
    wB = ctx.sb(pfx + "wB", [128, 8 * D], BF16)
    q1r = ctx.sb(pfx + "q1r", [128, 4 * D], BF16)
    q1rT = ctx.sb(pfx + "q1rT", [128, 8 * TC], BF16)
    q2t = ctx.sb(pfx + "q2t", [128, 8 * TC], BF16)
    ctab = ctx.sb(pfx + "ctab", [128, TC], F32)
    stab = ctx.sb(pfx + "stab", [128, TC], F32)
    mscr = ctx.sb(pfx + "mscr", [128, TC], F32)
    q1n = ctx.sb(pfx + "q1n", [128, 2 * TC], F32)
    qs = ctx.sb(pfx + "qs", [128, TC], F32)
    ki = ctx.sb(pfx + "ki", [128, TC], I32)
    kf = ctx.sb(pfx + "kf", [128, TC], F32)
    sqq = ctx.sb(pfx + "sqq", [128, TC], F32)
    throw = ctx.sb(pfx + "throw", [1, TC], F32)
    trow = ctx.sb(pfx + "trow", [1, 128], F32)
    e2 = ctx.sb(pfx + "e2", [128, 2], F32)
    blk2 = ctx.sb(pfx + "blk2", [2, 128], F32)
    bvibc = ctx.sb(pfx + "bvibc", [128, D], F32)
    bvirow = ctx.sb(pfx + "bvirow", [1, D], F32)
    bqch = ctx.sb(pfx + "bqch", [2, 64], F32)
    bkch = ctx.sb(pfx + "bkch", [2, 64], F32)
    masks4 = ctx.sb(pfx + "masks4", [128, 4 * 512], BF16)
    ax = ctx.sb(pfx + "ax", [128, 8 * TC], F32)
    sqt = ctx.sb(pfx + "sqt", [128, 4 * TC], BF16)
    a1 = ctx.sb(pfx + "a1", [128, TC], BF16)
    b1r = ctx.sb(pfx + "b1r", [128, TC], BF16)
    lnrows = ctx.sb(pfx + "lnrows", [1, 4 * TC], F32)
    bqic = load_cols(ctx, pfx + "bqic", w["bqi"], 128, 8, None, bank[6])
    bkic = load_cols(ctx, pfx + "bkic", w["bki"], 128, 8, None, bank[6])
    obc = load_cols(ctx, pfx + "obc", w["outb"], 128, 8, None, bank[6])
    outw = ctx.sb(pfx + "outw", [128, 8 * D], BF16)
    cinv8 = ctx.constcol(0.125)

    sem_x = p.dma_sem(pfx + "sx")
    sem_w = [p.dma_sem(pfx + f"sw{i}") for i in range(2)]
    sem_qt = p.dma_sem(pfx + "sqt")
    sem_kt = p.dma_sem(pfx + "skt")
    sem_v = [p.dma_sem(pfx + f"svv{i}") for i in range(4)]
    ln_banks = {"s1": bank[4], "s2": bank[5], "bc": bank[6]}

    p.do("gpsimd", lambda e: e.memset(masks4[:], 1.0), writes=[masks4.r()], tag=pfx + "mk0")
    p.do("gpsimd", lambda e: e.affine_select(
        out=masks4[:], in_=masks4[:], compare_op=ALU.is_ge, fill=0.0,
        base=0, pattern=[[-128, 4], [1, 512]], channel_multiplier=-1),
        reads=[masks4.r()], writes=[masks4.r()], tag=pfx + "mk1")
    p.do("gpsimd", lambda e: e.memset(blk2[:], 1.0), writes=[blk2.r()], tag=pfx + "b20")
    p.do("gpsimd", lambda e: e.affine_select(
        out=blk2[:], in_=blk2[:], compare_op=ALU.is_equal, fill=0.0,
        base=0, pattern=[[-1, 2], [0, 64]], channel_multiplier=1),
        reads=[blk2.r()], writes=[blk2.r()], tag=pfx + "b21")
    p.do("tensor", lambda e: e.transpose(bank[6][:, 0:2], blk2[:], ctx.idn[0:2, 0:2]),
         reads=[blk2.r(), ctx.idn.r()], writes=[bank[6].r()], tag=pfx + "e2t")
    p.do("scalar", lambda e: e.copy(e2[:], bank[6][:, 0:2]),
         reads=[bank[6].r()], writes=[e2.r()], tag=pfx + "e2e")
    p.do("gpsimd", lambda e: e.iota(throw[:], pattern=[[1, TC]], base=0,
                                    channel_multiplier=0,
                                    allow_small_or_imprecise_dtypes=True),
         writes=[throw.r()], tag=pfx + "thi")
    LOG1E4 = float(np.log(10000.0))
    p.do("scalar", lambda e: e.activation(
        throw[:], throw[:], AF.Exp, bias=ctx.constcol(2.0 * LOG1E4 / 1024.0)[0:1, 0:1],
        scale=ctx.constcol(-2.0 * LOG1E4 / 1024.0)[0:1, 0:1]),
        reads=[throw.r(), ctx.constcol(2.0 * LOG1E4 / 1024.0).r(),
               ctx.constcol(-2.0 * LOG1E4 / 1024.0).r()],
        writes=[throw.r()], tag=pfx + "the")
    p.dma("sync", bvirow[:], w["bvi"], p.dma_sem(pfx + "sbvi"),
          writes=[bvirow.r()], tag=pfx + "bvi")
    for eh in range(2):
        p.do("tensor", lambda e, eh=eh: e.matmul(
            bank[6][:, 0:512], ctx.ones_row[:], bvirow[0:1, eh * 512:(eh + 1) * 512],
            start=True, stop=True),
            reads=[bvirow.r(), ctx.ones_row.r()], writes=[bank[6].r()],
            tag=f"{pfx}bvb{eh}")
        p.do("scalar", lambda e, eh=eh: e.copy(
            bvibc[:, eh * 512:(eh + 1) * 512], bank[6][:, 0:512]),
            reads=[bank[6].r()], writes=[bvibc.r()], tag=f"{pfx}bve{eh}")

    for c in range(NCH):
        xsl = xin[:, c * TC:(c + 1) * TC].rearrange("(j p) t -> p j t", p=128)
        p.dma("sync", ax[:].rearrange("p (j t) -> p j t", j=8), xsl, sem_x,
              writes=[ax.r()], tag=f"{pfx}x{c}")
        emit_ln(ctx, ax, "c", 8, TC, a1, b1r, sqt, lnrows, ln_banks, f"{pfx}ln_{c}")
        for j in range(8):
            p.do("vector", lambda e, j=j: e.tensor_mul(
                ax[:, j * TC:(j + 1) * TC], ax[:, j * TC:(j + 1) * TC], a1[:]),
                reads=[(ax.name, ("c", j)), a1.r()],
                writes=[(ax.name, ("c", j))], tag=f"{pfx}y1_{c}_{j}")
            p.do("vector", lambda e, j=j, c=c: e.tensor_tensor(
                big[:, j * T + c * TC:j * T + (c + 1) * TC],
                ax[:, j * TC:(j + 1) * TC], b1r[:], ALU.add),
                reads=[(ax.name, ("c", j)), b1r.r()],
                writes=[(nm, ("xn", c, j))], tag=f"{pfx}y2_{c}_{j}")

    for path in ("q", "k", "v"):
        wa_d, wb_d = w[path + "T"], w[path + "iT"]
        p.dma("gpsimd", wA[:].rearrange("p (j e) -> p j e", j=8),
              wa_d.rearrange("(j p) e -> p j e", p=128), sem_w[0],
              writes=[wA.r()], tag=f"{pfx}wA_{path}")
        p.dma("gpsimd", wB[:].rearrange("p (j e) -> p j e", j=8),
              wb_d.rearrange("(j p) e -> p j e", p=128), sem_w[1],
              writes=[wB.r()], tag=f"{pfx}wB_{path}")
        for c in range(NCH):
            for tl in range(4):
                for eh in range(2):
                    bm = bank[eh]
                    for dt in range(8):
                        p.do("tensor", lambda e, dt=dt, eh=eh, c=c, tl=tl, bm=bm: e.matmul(
                            bm[:, 0:TC],
                            big[:, dt * T + c * TC + tl * 128:dt * T + c * TC + (tl + 1) * 128],
                            wA[:, dt * D + eh * 512:dt * D + (eh + 1) * 512],
                            start=(dt == 0), stop=(dt == 7)),
                            reads=[(nm, ("xn", c, dt)), wA.r()],
                            writes=[bm.r()], tag=f"{pfx}p1_{path}_{c}_{tl}_{eh}_{dt}")
                p.do("gpsimd", lambda e, c=c, tl=tl: e.iota(
                    trow[:], pattern=[[1, 128]], base=c * TC + tl * 128,
                    channel_multiplier=0, allow_small_or_imprecise_dtypes=True),
                    writes=[trow.r()], tag=f"{pfx}ti_{path}_{c}_{tl}")
                p.do("tensor", lambda e: e.matmul(
                    bank[2][:, 0:TC], trow[:], throw[:], start=True, stop=True),
                    reads=[trow.r(), throw.r()], writes=[bank[2].r()],
                    tag=f"{pfx}ang_{path}_{c}_{tl}")
                emit_sin_chain(ctx, bank[2][:, 0:TC], [bank[2].r()],
                               stab[:], [stab.r()], qs, ki, kf, 0.0, 1.0,
                               f"{pfx}ss_{path}_{c}_{tl}_")
                emit_sin_chain(ctx, bank[2][:, 0:TC], [bank[2].r()],
                               ctab[:], [ctab.r()], qs, ki, kf, np.pi / 2, 1.0,
                               f"{pfx}cs_{path}_{c}_{tl}_")
                for eh in range(2):
                    p.do("scalar", lambda e, eh=eh: e.copy(
                        q1n[:, eh * TC:(eh + 1) * TC], bank[eh][:, 0:TC]),
                        reads=[bank[eh].r()], writes=[(q1n.name, eh)],
                        tag=f"{pfx}qe{eh}_{path}_{c}_{tl}")
                def ev(eh, s0):
                    return q1n[:, eh * TC:(eh + 1) * TC].rearrange(
                        "p (i two) -> p i two", two=2)[:, :, s0]
                for eh in range(2):
                    xe = ev(eh, 0)
                    xo = ev(eh, 1)
                    ch = ctab[:, eh * 256:(eh + 1) * 256]
                    sh = stab[:, eh * 256:(eh + 1) * 256]
                    base = tl * D + eh * TC
                    ye = q1r[:, base:base + TC].rearrange("p (i two) -> p i two", two=2)[:, :, 0]
                    yo = q1r[:, base:base + TC].rearrange("p (i two) -> p i two", two=2)[:, :, 1]
                    p.do("vector", lambda e, xe=xe, ch=ch: e.tensor_tensor(
                        mscr[:, 0:256], xe, ch, ALU.mult),
                        reads=[(q1n.name, eh), ctab.r()], writes=[(mscr.name, 0)],
                        tag=f"{pfx}r1_{path}_{c}_{tl}_{eh}")
                    p.do("vector", lambda e, xo=xo, sh=sh: e.tensor_tensor(
                        mscr[:, 256:512], xo, sh, ALU.mult),
                        reads=[(q1n.name, eh), stab.r()], writes=[(mscr.name, 1)],
                        tag=f"{pfx}r2_{path}_{c}_{tl}_{eh}")
                    p.do("vector", lambda e, ye=ye: e.tensor_tensor(
                        ye, mscr[:, 0:256], mscr[:, 256:512], ALU.add),
                        reads=[(mscr.name, 0), (mscr.name, 1)],
                        writes=[(q1r.name, (tl, eh, 0))], tag=f"{pfx}r3_{path}_{c}_{tl}_{eh}")
                    p.do("vector", lambda e, xe=xe, sh=sh: e.tensor_tensor(
                        mscr[:, 0:256], xe, sh, ALU.mult),
                        reads=[(q1n.name, eh), stab.r()], writes=[(mscr.name, 0)],
                        tag=f"{pfx}r4_{path}_{c}_{tl}_{eh}")
                    p.do("vector", lambda e, xo=xo, ch=ch: e.tensor_tensor(
                        mscr[:, 256:512], xo, ch, ALU.mult),
                        reads=[(q1n.name, eh), ctab.r()], writes=[(mscr.name, 1)],
                        tag=f"{pfx}r5_{path}_{c}_{tl}_{eh}")
                    p.do("vector", lambda e, yo=yo: e.tensor_tensor(
                        yo, mscr[:, 256:512], mscr[:, 0:256], ALU.subtract),
                        reads=[(mscr.name, 0), (mscr.name, 1)],
                        writes=[(q1r.name, (tl, eh, 1))], tag=f"{pfx}r6_{path}_{c}_{tl}_{eh}")
                for eb in range(8):
                    p.do("tensor", lambda e, tl=tl, eb=eb: e.transpose(
                        ctx.bankT[:, 0:128],
                        q1r[:, tl * D + eb * 128:tl * D + (eb + 1) * 128],
                        ctx.idn_bf[:]),
                        reads=[(q1r.name, (tl, eb // 4, 0)), (q1r.name, (tl, eb // 4, 1)),
                               ctx.idn_bf.r()],
                        writes=[ctx.bankT.r()], tag=f"{pfx}t1_{path}_{c}_{tl}_{eb}")
                    p.do("vector", lambda e, tl=tl, eb=eb: e.tensor_copy(
                        q1rT[:, eb * TC + tl * 128:eb * TC + (tl + 1) * 128],
                        ctx.bankT[:, 0:128]),
                        reads=[ctx.bankT.r()], writes=[(q1rT.name, (eb, tl))],
                        tag=f"{pfx}t2_{path}_{c}_{tl}_{eb}")
            RQ1RT = [(q1rT.name, (eb, tl)) for eb in range(8) for tl in range(4)]
            if path in ("q", "k"):
                bic = bqic if path == "q" else bkic
                bch = bqch if path == "q" else bkch
                for et in range(8):
                    bm = bank[4 + et % 2]
                    for dt2 in range(8):
                        p.do("tensor", lambda e, et=et, dt2=dt2, bm=bm: e.matmul(
                            bm[:, 0:TC], wB[:, dt2 * D + et * 128:dt2 * D + (et + 1) * 128],
                            q1rT[:, dt2 * TC:(dt2 + 1) * TC],
                            start=(dt2 == 0), stop=(dt2 == 7)),
                            reads=[wB.r()] + RQ1RT, writes=[bm.r()],
                            tag=f"{pfx}p2_{path}_{c}_{et}_{dt2}")
                    p.do("scalar", lambda e, et=et, bm=bm: e.activation(
                        q2t[:, et * TC:(et + 1) * TC], bm[:, 0:TC], AF.Identity,
                        bias=bic[:, et:et + 1], scale=ctx.constcol(1.0)[:, 0:1]),
                        reads=[bm.r(), bic.r(), ctx.constcol(1.0).r()],
                        writes=[(q2t.name, et)], tag=f"{pfx}ev_{path}_{c}_{et}")
                    p.do("scalar", lambda e, et=et: e.activation(
                        sqq[:], q2t[:, et * TC:(et + 1) * TC], AF.Square),
                        reads=[(q2t.name, et)], writes=[sqq.r()],
                        tag=f"{pfx}sq_{path}_{c}_{et}")
                    p.do("tensor", lambda e: e.matmul(
                        bank[6][0:2, 0:TC], e2[:], sqq[:], start=True, stop=True),
                        reads=[e2.r(), sqq.r()], writes=[bank[6].r()],
                        tag=f"{pfx}n2_{path}_{c}_{et}")
                    p.do("vector", lambda e, et=et, c=c, bch=bch: e.reduce_max(
                        bch[:, et * 8 + c:et * 8 + c + 1], bank[6][0:2, 0:TC], axis=AX.X),
                        reads=[bank[6].r()], writes=[(bch.name, (et, c))],
                        tag=f"{pfx}nr_{path}_{c}_{et}")
                dst = qts if path == "q" else kts
                semd = sem_qt if path == "q" else sem_kt
                p.dma("sync", dst[:, c * TC:(c + 1) * TC].rearrange("(j p) t -> p j t", p=128),
                      q2t[:].rearrange("p (j t) -> p j t", j=8), semd,
                      reads=[q2t.r()] + [(q2t.name, et) for et in range(8)],
                      tag=f"{pfx}st_{path}_{c}")
            else:
                if c == 0:
                    p.claim(q2t.name, tag=pfx + "vq2claim")
                for tl in range(4):
                    for eh in range(2):
                        bm = bank[4 + eh]
                        for dt2 in range(8):
                            p.do("tensor", lambda e, tl=tl, eh=eh, dt2=dt2, bm=bm: e.matmul(
                                bm[:, 0:TC],
                                q1rT[:, dt2 * TC + tl * 128:dt2 * TC + (tl + 1) * 128],
                                wB[:, dt2 * D + eh * 512:dt2 * D + (eh + 1) * 512],
                                start=(dt2 == 0), stop=(dt2 == 7)),
                                reads=[wB.r()] + RQ1RT, writes=[bm.r()],
                                tag=f"{pfx}v2_{c}_{tl}_{eh}_{dt2}")
                        p.do("vector", lambda e, tl=tl, eh=eh, bm=bm: e.tensor_tensor(
                            q2t[:, tl * D + eh * TC:tl * D + (eh + 1) * TC],
                            bm[:, 0:TC], bvibc[:, eh * TC:(eh + 1) * TC], ALU.add),
                            reads=[bm.r(), bvibc.r()],
                            writes=[(q2t.name, ("v", tl, eh))], tag=f"{pfx}ve_{c}_{tl}_{eh}")
                    p.dma("sync",
                          vhms[:, c * TC + tl * 128:c * TC + (tl + 1) * 128, :].rearrange(
                              "h (o p) d -> p (o h) d", p=128),
                          q2t[:, tl * D:(tl + 1) * D].rearrange("p (h d) -> p h d", d=64),
                          sem_v[tl],
                          reads=[(q2t.name, ("v", tl, 0)), (q2t.name, ("v", tl, 1))],
                          tag=f"{pfx}vs_{c}_{tl}")

    return emit_attn_core(ctx, pfx + "c_", big, qts, kts, vhms, bqch, bkch,
                          outw_load(ctx, pfx, outw, w["outwT"]), obc,
                          xin, xout, masks4, cinv8, x_sb=ax)


def outw_load(ctx, pfx, outw, dram_ap):
    ctx.p.dma("gpsimd", outw[:].rearrange("p (j e) -> p j e", j=8),
              dram_ap.rearrange("(j p) e -> p j e", p=128),
              ctx.p.dma_sem(pfx + "sow"), writes=[outw.r()], tag=pfx + "outw")
    return outw


L0A_DRAM_SPECS = [
    ("qT", [1024, 1024], F32), ("kT", [1024, 1024], F32), ("vT", [1024, 1024], F32),
    ("qiT", [1024, 1024], F32), ("kiT", [1024, 1024], F32), ("viT", [1024, 1024], F32),
    ("bqi", [8, 128], F32), ("bki", [8, 128], F32), ("bvi", [1, 1024], F32),
    ("outwT", [1024, 1024], F32), ("outb", [8, 128], F32),
]


def build_l0a_standalone():
    nc = bass.Bass("TRN2", target_bir_lowering=False)
    stack = ExitStack()
    with stack:
        ctx = Ctx(nc, stack)
        xin = nc.dram_tensor("xT", [D, T], F32, kind="ExternalInput").ap()
        xout = nc.dram_tensor("yT", [D, T], F32, kind="ExternalOutput").ap()
        qts = nc.dram_tensor("l0qt", [D, T], BF16, kind="Internal").ap()
        kts = nc.dram_tensor("l0kt", [D, T], BF16, kind="Internal").ap()
        vhms = nc.dram_tensor("l0vh", [16, T, 64], BF16, kind="Internal").ap()
        w = {}
        for name, shape, dt in L0A_DRAM_SPECS:
            w[name] = nc.dram_tensor(name, shape, dt, kind="ExternalInput").ap()
        ctx.setup_consts()
        sem_out = emit_l0a(ctx, xin, xout, w, qts, kts, vhms, "aa_")
        last = sem_out.inc_ops[-1]
        ctx.p._add_wait(ctx.p.do("sync", lambda e: e.nop(), tag="fin"), last)
        ctx.p.finalize()
    return nc





def emit_l1a(ctx, xin, xout, w, qts, kts, vhms, pfx, stage=99):
    p = ctx.p
    TC = 512
    NCH = T // TC
    bank = ctx.banks

    big = ctx.sb(pfx + "big", [128, 32768], BF16)
    nm = big.name
    wq_sb = ctx.sb(pfx + "wq", [128, 8 * D], BF16)
    wdkv = ctx.sb(pfx + "wdkv", [128, 8 * 256], BF16)
    wuk = ctx.sb(pfx + "wuk", [128, 2 * D], BF16)
    wuv = ctx.sb(pfx + "wuv", [128, 2 * D], BF16)
    lat_sb = ctx.sb(pfx + "lat", [128, 32 * 256], BF16)
    latnT = ctx.sb(pfx + "latnT", [128, 2 * T], BF16)
    q1r = ctx.sb(pfx + "q1r", [128, 4 * D], BF16)
    q2t = ctx.sb(pfx + "q2t", [128, 8 * TC], BF16)
    ctab = ctx.sb(pfx + "ctab", [128, 128], F32)
    stab = ctx.sb(pfx + "stab", [128, 128], F32)
    mscr = ctx.sb(pfx + "mscr", [128, 2 * TC], F32)
    qs = ctx.sb(pfx + "qs", [128, 128], F32)
    ki = ctx.sb(pfx + "ki", [128, 128], I32)
    kf = ctx.sb(pfx + "kf", [128, 128], F32)
    sqq = ctx.sb(pfx + "sqq", [128, TC], F32)
    sqlat = ctx.sb(pfx + "sqlat", [128, 256], F32)
    stat = ctx.sb(pfx + "stat", [128, 6 * 32], F32)
    gb_bc = ctx.sb(pfx + "gbbc", [128, 2 * 256], F32)
    grow = ctx.sb(pfx + "grow", [1, 2 * 256], F32)
    frep = ctx.sb(pfx + "frep", [1, 128], F32)
    mrow = ctx.sb(pfx + "mrow", [1, 128], F32)
    orow = ctx.sb(pfx + "orow", [1, 128], F32)
    trow = ctx.sb(pfx + "trow", [1, 128], F32)
    e2 = ctx.sb(pfx + "e2", [128, 2], F32)
    blk2 = ctx.sb(pfx + "blk2", [2, 128], F32)
    bqch = ctx.sb(pfx + "bqch", [2, 64], F32)
    bkch = ctx.sb(pfx + "bkch", [2, 64], F32)
    masks4 = ctx.sb(pfx + "masks4", [128, 4 * 512], BF16)
    ax = ctx.sb(pfx + "ax", [128, 8 * TC], F32)
    sqt = ctx.sb(pfx + "sqt", [128, 2 * TC], F32)
    lnsq = ctx.sb(pfx + "lnsq", [128, 4 * TC], BF16)
    a1 = ctx.sb(pfx + "a1", [128, TC], BF16)
    b1r = ctx.sb(pfx + "b1r", [128, TC], BF16)
    lnrows = ctx.sb(pfx + "lnrows", [1, 4 * TC], F32)
    outw = TBuf(wq_sb.ap, wq_sb.name)
    cinv8 = ctx.constcol(0.125)

    sem_x = p.dma_sem(pfx + "sx")
    sem_qt = p.dma_sem(pfx + "sqt")
    sem_kt = p.dma_sem(pfx + "skt")
    sem_v = [p.dma_sem(pfx + f"svv{i}") for i in range(4)]
    ln_banks = {"s1": bank[4], "s2": bank[5], "bc": bank[6]}

    for name, t, jn in (("Wq", wq_sb, 8), ("Wdkv", wdkv, 8), ("Wuk", wuk, 2),
                        ("Wuv", wuv, 2)):
        p.dma("gpsimd", t[:].rearrange("p (j e) -> p j e", j=jn),
              w[name].rearrange("(j p) e -> p j e", p=128),
              p.dma_sem(pfx + "s" + name), writes=[t.r()], tag=pfx + name)
    p.dma("sync", grow[:], w["lngb"], p.dma_sem(pfx + "sgb"),
          writes=[grow.r()], tag=pfx + "lngb")
    for i in range(2):
        p.do("tensor", lambda e, i=i: e.matmul(
            bank[6][:, 0:256], ctx.ones_row[:], grow[0:1, i * 256:(i + 1) * 256],
            start=True, stop=True),
            reads=[grow.r(), ctx.ones_row.r()], writes=[bank[6].r()],
            tag=f"{pfx}gb{i}")
        p.do("scalar", lambda e, i=i: e.copy(gb_bc[:, i * 256:(i + 1) * 256],
                                             bank[6][:, 0:256]),
             reads=[bank[6].r()], writes=[gb_bc.r()], tag=f"{pfx}ge{i}")
    p.do("gpsimd", lambda e: e.memset(masks4[:], 1.0), writes=[masks4.r()], tag=pfx + "mk0")
    p.do("gpsimd", lambda e: e.affine_select(
        out=masks4[:], in_=masks4[:], compare_op=ALU.is_ge, fill=0.0,
        base=0, pattern=[[-128, 4], [1, 512]], channel_multiplier=-1),
        reads=[masks4.r()], writes=[masks4.r()], tag=pfx + "mk1")
    p.do("gpsimd", lambda e: e.memset(blk2[:], 1.0), writes=[blk2.r()], tag=pfx + "b20")
    p.do("gpsimd", lambda e: e.affine_select(
        out=blk2[:], in_=blk2[:], compare_op=ALU.is_equal, fill=0.0,
        base=0, pattern=[[-1, 2], [0, 64]], channel_multiplier=1),
        reads=[blk2.r()], writes=[blk2.r()], tag=pfx + "b21")
    p.do("tensor", lambda e: e.transpose(bank[6][:, 0:2], blk2[:], ctx.idn[0:2, 0:2]),
         reads=[blk2.r(), ctx.idn.r()], writes=[bank[6].r()], tag=pfx + "e2t")
    p.do("scalar", lambda e: e.copy(e2[:], bank[6][:, 0:2]),
         reads=[bank[6].r()], writes=[e2.r()], tag=pfx + "e2e")
    LOG1E4 = float(np.log(10000.0))
    p.do("gpsimd", lambda e: e.iota(frep[:], pattern=[[0, 4], [1, 32]], base=0,
                                    channel_multiplier=0,
                                    allow_small_or_imprecise_dtypes=True),
         writes=[frep.r()], tag=pfx + "fi")
    p.do("scalar", lambda e: e.activation(
        frep[:], frep[:], AF.Exp, bias=ctx.constcol(0.0)[0:1, 0:1],
        scale=ctx.constcol(-LOG1E4 / 32.0)[0:1, 0:1]),
        reads=[frep.r(), ctx.constcol(0.0).r(), ctx.constcol(-LOG1E4 / 32.0).r()],
        writes=[frep.r()], tag=pfx + "fe")

    for c in range(NCH):
        xsl = xin[:, c * TC:(c + 1) * TC].rearrange("(j p) t -> p j t", p=128)
        p.dma("sync", ax[:].rearrange("p (j t) -> p j t", j=8), xsl, sem_x,
              writes=[ax.r()], tag=f"{pfx}x{c}")
        emit_ln(ctx, ax, "c", 8, TC, a1, b1r, lnsq, lnrows, ln_banks, f"{pfx}ln_{c}")
        for j in range(8):
            p.do("vector", lambda e, j=j: e.tensor_mul(
                ax[:, j * TC:(j + 1) * TC], ax[:, j * TC:(j + 1) * TC], a1[:]),
                reads=[(ax.name, ("c", j)), a1.r()],
                writes=[(ax.name, ("c", j))], tag=f"{pfx}y1_{c}_{j}")
            p.do("vector", lambda e, j=j, c=c: e.tensor_tensor(
                big[:, j * T + c * TC:j * T + (c + 1) * TC],
                ax[:, j * TC:(j + 1) * TC], b1r[:], ALU.add),
                reads=[(ax.name, ("c", j)), b1r.r()],
                writes=[(nm, ("xn", c, j))], tag=f"{pfx}y2_{c}_{j}")

    if stage < 1:
        return sem_x
    for c in range(NCH):
        for tl in range(4):
            g = c * 4 + tl
            bm = bank[g % 2]
            for dt in range(8):
                p.do("tensor", lambda e, dt=dt, c=c, tl=tl, bm=bm: e.matmul(
                    bm[:, 0:256],
                    big[:, dt * T + c * TC + tl * 128:dt * T + c * TC + (tl + 1) * 128],
                    wdkv[:, dt * 256:(dt + 1) * 256],
                    start=(dt == 0), stop=(dt == 7)),
                    reads=[(nm, ("xn", c, dt)), wdkv.r()],
                    writes=[bm.r()], tag=f"{pfx}lat_{g}_{dt}")
            p.do("scalar", lambda e, g=g, bm=bm: e.copy(
                lat_sb[:, g * 256:(g + 1) * 256], bm[:, 0:256]),
                reads=[bm.r()], writes=[(lat_sb.name, g)], tag=f"{pfx}le_{g}")
            if stage >= 0.8:
                p.do("vector", lambda e, g=g: e.reduce_sum(
                    stat[:, g:g + 1], lat_sb[:, g * 256:(g + 1) * 256], axis=AX.X),
                    reads=[(lat_sb.name, g)], writes=[(stat.name, ("s", g))],
                    tag=f"{pfx}ls_{g}")
            if stage >= 0.9:
                p.do("vector", lambda e, g=g: e.tensor_mul(
                    sqlat[:], lat_sb[:, g * 256:(g + 1) * 256],
                    lat_sb[:, g * 256:(g + 1) * 256]),
                    reads=[(lat_sb.name, g)], writes=[sqlat.r()], tag=f"{pfx}lq_{g}")
                p.do("vector", lambda e, g=g: e.reduce_sum(
                    stat[:, 32 + g:32 + g + 1], sqlat[:], axis=AX.X),
                    reads=[sqlat.r()], writes=[(stat.name, ("m", g))], tag=f"{pfx}lm_{g}")
    if stage < 1.5:
        return sem_x
    RS = [(stat.name, ("s", g)) for g in range(32)]
    RM = [(stat.name, ("m", g)) for g in range(32)]
    p.do("vector", lambda e: e.tensor_scalar_mul(stat[:, 64:96], stat[:, 0:32], 1.0 / 256),
         reads=RS, writes=[(stat.name, "mean")], tag=pfx + "mean")
    p.do("scalar", lambda e: e.mul(stat[:, 96:128], stat[:, 32:64], 1.0 / 256),
         reads=RM, writes=[(stat.name, "msn")], tag=pfx + "msn")
    p.do("gpsimd", lambda e: e.tensor_mul(
        stat[:, 128:160], stat[:, 64:96], stat[:, 64:96]),
        reads=[(stat.name, "mean")], writes=[(stat.name, "negm2")], tag=pfx + "m2c")
    p.do("vector", lambda e: e.tensor_sub(stat[:, 128:160], stat[:, 96:128],
                                          stat[:, 128:160]),
         reads=[(stat.name, "negm2"), (stat.name, "msn")],
         writes=[(stat.name, "var")], tag=pfx + "var")
    p.do("scalar", lambda e: e.activation(
        stat[:, 128:160], stat[:, 128:160], AF.Sqrt,
        bias=ctx.constcol(1e-5)[:, 0:1], scale=1.0),
        reads=[(stat.name, "var"), ctx.constcol(1e-5).r()],
        writes=[(stat.name, "sd")], tag=pfx + "sd")
    p.do("vector", lambda e: e.reciprocal(stat[:, 96:128], stat[:, 128:160]),
         reads=[(stat.name, "sd")], writes=[(stat.name, "rstd")], tag=pfx + "rstd")
    p.do("gpsimd", lambda e: e.tensor_mul(
        stat[:, 64:96], stat[:, 64:96], stat[:, 96:128]),
        reads=[(stat.name, "mean"), (stat.name, "rstd")],
        writes=[(stat.name, "bpos")], tag=pfx + "bpos")
    p.do("scalar", lambda e: e.mul(stat[:, 64:96], stat[:, 64:96], -1.0),
         reads=[(stat.name, "bpos")], writes=[(stat.name, "bcol")], tag=pfx + "bcol")
    if stage < 1.8:
        return sem_x
    for g in range(32):
        p.do("scalar", lambda e, g=g: e.activation(
            sqlat[:], lat_sb[:, g * 256:(g + 1) * 256], AF.Identity,
            bias=stat[:, 64 + g:64 + g + 1], scale=stat[:, 96 + g:96 + g + 1]),
            reads=[(lat_sb.name, g), (stat.name, "bcol"), (stat.name, "rstd")],
            writes=[sqlat.r()], tag=f"{pfx}af_{g}")
        p.do("vector", lambda e: e.tensor_mul(sqlat[:], sqlat[:], gb_bc[:, 0:256]),
             reads=[sqlat.r(), gb_bc.r()], writes=[sqlat.r()], tag=f"{pfx}ag_{g}")
        p.do("vector", lambda e, g=g: e.tensor_tensor(
            lat_sb[:, g * 256:(g + 1) * 256], sqlat[:], gb_bc[:, 256:512], ALU.add),
            reads=[sqlat.r(), gb_bc.r()], writes=[(lat_sb.name, ("n", g))],
            tag=f"{pfx}ab_{g}")
        for lb in range(2):
            p.do("tensor", lambda e, g=g, lb=lb: e.transpose(
                ctx.bankT[:, 0:128],
                lat_sb[:, g * 256 + lb * 128:g * 256 + (lb + 1) * 128],
                ctx.idn_bf[:]),
                reads=[(lat_sb.name, ("n", g)), ctx.idn_bf.r()],
                writes=[ctx.bankT.r()], tag=f"{pfx}lt_{g}_{lb}")
            p.do("vector", lambda e, g=g, lb=lb: e.tensor_copy(
                latnT[:, lb * T + g * 128:lb * T + (g + 1) * 128],
                ctx.bankT[:, 0:128]),
                reads=[ctx.bankT.r()], writes=[(latnT.name, (lb, g))],
                tag=f"{pfx}le2_{g}_{lb}")
    RLAT = [(latnT.name, (lb, g)) for lb in range(2) for g in range(32)]

    def rope_tables(c, tagp):
        p.do("gpsimd", lambda e, c=c: e.iota(
            mrow[:], pattern=[[128, 4], [0, 32]], base=c * TC,
            channel_multiplier=0, allow_small_or_imprecise_dtypes=True),
            writes=[mrow.r()], tag=tagp + "mi")
        p.do("vector", lambda e: e.tensor_mul(orow[:], frep[:], mrow[:]),
             reads=[frep.r(), mrow.r()], writes=[orow.r()], tag=tagp + "or")
        p.do("gpsimd", lambda e: e.iota(
            trow[:], pattern=[[1, 128]], base=0, channel_multiplier=0,
            allow_small_or_imprecise_dtypes=True),
            writes=[trow.r()], tag=tagp + "ti")
        p.do("tensor", lambda e: e.matmul(
            bank[2][:, 0:128], trow[:], frep[:], start=True, stop=False),
            reads=[trow.r(), frep.r()], writes=[bank[2].r()], tag=tagp + "a1")
        p.do("tensor", lambda e: e.matmul(
            bank[2][:, 0:128], ctx.ones_row[:], orow[:], start=False, stop=True),
            reads=[orow.r(), ctx.ones_row.r()], writes=[bank[2].r()], tag=tagp + "a2")
        emit_sin_chain(ctx, bank[2][:, 0:128], [bank[2].r()],
                       stab[:], [stab.r()], qs, ki, kf, 0.0, 1.0, tagp + "s_")
        emit_sin_chain(ctx, bank[2][:, 0:128], [bank[2].r()],
                       ctab[:], [ctab.r()], qs, ki, kf, np.pi / 2, 1.0, tagp + "c_")

    def rope_apply(psrc, preads, tl, tagp):
        x1 = psrc.rearrange("p (h i) -> p h i", i=64)[:, :, 0:32]
        x2 = psrc.rearrange("p (h i) -> p h i", i=64)[:, :, 32:64]
        ch = ctab[:, tl * 32:(tl + 1) * 32].rearrange(
            "p (o i) -> p o i", o=1).broadcast_to([128, 16, 32])
        sh = stab[:, tl * 32:(tl + 1) * 32].rearrange(
            "p (o i) -> p o i", o=1).broadcast_to([128, 16, 32])
        dst = q1r[:, tl * D:(tl + 1) * D].rearrange("p (h i) -> p h i", i=64)
        y1 = dst[:, :, 0:32]
        y2 = dst[:, :, 32:64]
        m1 = mscr[:, 0:TC].rearrange("p (h i) -> p h i", i=32)
        m2 = mscr[:, TC:2 * TC].rearrange("p (h i) -> p h i", i=32)
        p.do("vector", lambda e: e.tensor_tensor(m1, x1, ch, ALU.mult),
             reads=preads + [ctab.r()], writes=[(mscr.name, 0)], tag=tagp + "1")
        p.do("vector", lambda e: e.tensor_tensor(m2, x2, sh, ALU.mult),
             reads=preads + [stab.r()], writes=[(mscr.name, 1)], tag=tagp + "2")
        p.do("vector", lambda e: e.tensor_tensor(y1, m1, m2, ALU.subtract),
             reads=[(mscr.name, 0), (mscr.name, 1)],
             writes=[(q1r.name, (tl, 0))], tag=tagp + "3")
        p.do("vector", lambda e: e.tensor_tensor(m1, x1, sh, ALU.mult),
             reads=preads + [stab.r()], writes=[(mscr.name, 0)], tag=tagp + "4")
        p.do("vector", lambda e: e.tensor_tensor(m2, x2, ch, ALU.mult),
             reads=preads + [ctab.r()], writes=[(mscr.name, 1)], tag=tagp + "5")
        p.do("vector", lambda e: e.tensor_tensor(y2, m1, m2, ALU.add),
             reads=[(mscr.name, 0), (mscr.name, 1)],
             writes=[(q1r.name, (tl, 1))], tag=tagp + "6")

    def transpose_tile(tl, c, tagp):
        for eb in range(8):
            p.do("tensor", lambda e, tl=tl, eb=eb: e.transpose(
                ctx.bankT[:, 0:128],
                q1r[:, tl * D + eb * 128:tl * D + (eb + 1) * 128],
                ctx.idn_bf[:]),
                reads=[(q1r.name, (tl, 0)), (q1r.name, (tl, 1)), ctx.idn_bf.r()],
                writes=[ctx.bankT.r()], tag=f"{tagp}t1_{eb}")
            p.do("vector", lambda e, tl=tl, eb=eb: e.tensor_copy(
                q2t[:, eb * TC + tl * 128:eb * TC + (tl + 1) * 128],
                ctx.bankT[:, 0:128]),
                reads=[ctx.bankT.r()], writes=[(q2t.name, ("t", eb, tl))],
                tag=f"{tagp}t2_{eb}")

    def norms_store(c, bch, dst, semd, tagp):
        for et in range(8):
            p.do("scalar", lambda e, et=et: e.activation(
                sqq[:], q2t[:, et * TC:(et + 1) * TC], AF.Square),
                reads=[(q2t.name, ("t", et, tl2)) for tl2 in range(4)],
                writes=[sqq.r()], tag=f"{tagp}sq_{et}")
            p.do("tensor", lambda e: e.matmul(
                bank[6][0:2, 0:TC], e2[:], sqq[:], start=True, stop=True),
                reads=[e2.r(), sqq.r()], writes=[bank[6].r()], tag=f"{tagp}n2_{et}")
            p.do("vector", lambda e, et=et, c=c, bch=bch: e.reduce_max(
                bch[:, et * 8 + c:et * 8 + c + 1], bank[6][0:2, 0:TC], axis=AX.X),
                reads=[bank[6].r()], writes=[(bch.name, (et, c))], tag=f"{tagp}nr_{et}")
        p.dma("sync", dst[:, c * TC:(c + 1) * TC].rearrange("(j p) t -> p j t", p=128),
              q2t[:].rearrange("p (j t) -> p j t", j=8), semd,
              reads=[q2t.r()] + [(q2t.name, ("t", eb, tl2))
                                 for eb in range(8) for tl2 in range(4)],
              tag=f"{tagp}st")

    if stage < 2:
        return p.dma_sems[-1] if False else sem_x
    for c in range(NCH):
        rope_tables(c, f"{pfx}rt_{c}_")
        if c > 0:
            p.claim(q1r.name, tag=f"{pfx}clq_{c}")
        for tl in range(4):
            for eh in range(2):
                bm = bank[eh]
                for dt in range(8):
                    p.do("tensor", lambda e, dt=dt, eh=eh, c=c, tl=tl, bm=bm: e.matmul(
                        bm[:, 0:TC],
                        big[:, dt * T + c * TC + tl * 128:dt * T + c * TC + (tl + 1) * 128],
                        wq_sb[:, dt * D + eh * 512:dt * D + (eh + 1) * 512],
                        start=(dt == 0), stop=(dt == 7)),
                        reads=[(nm, ("xn", c, dt)), wq_sb.r()],
                        writes=[bm.r()], tag=f"{pfx}q_{c}_{tl}_{eh}_{dt}")
            p.do("scalar", lambda e, tl=tl: e.copy(sqt[:, 0:TC], bank[0][:, 0:TC]),
                 reads=[bank[0].r()], writes=[(sqt.name, 0)], tag=f"{pfx}qe0_{c}_{tl}")
            p.do("scalar", lambda e, tl=tl: e.copy(sqt[:, TC:2 * TC], bank[1][:, 0:TC]),
                 reads=[bank[1].r()], writes=[(sqt.name, 1)], tag=f"{pfx}qe1_{c}_{tl}")
            rope_apply(sqt[:], [(sqt.name, 0), (sqt.name, 1)], tl, f"{pfx}qr_{c}_{tl}_")
            transpose_tile(tl, c, f"{pfx}qt_{c}_{tl}_")
        norms_store(c, bqch, qts, sem_qt, f"{pfx}qn_{c}_")
        if stage < 3:
            continue
        for tl in range(4):
            for eh in range(2):
                bm = bank[eh]
                for lb in range(2):
                    p.do("tensor", lambda e, lb=lb, eh=eh, c=c, tl=tl, bm=bm: e.matmul(
                        bm[:, 0:TC],
                        latnT[:, lb * T + c * TC + tl * 128:lb * T + c * TC + (tl + 1) * 128],
                        wuk[:, lb * D + eh * 512:lb * D + (eh + 1) * 512],
                        start=(lb == 0), stop=(lb == 1)),
                        reads=RLAT + [wuk.r()],
                        writes=[bm.r()], tag=f"{pfx}k_{c}_{tl}_{eh}_{lb}")
            p.do("scalar", lambda e, tl=tl: e.copy(sqt[:, 0:TC], bank[0][:, 0:TC]),
                 reads=[bank[0].r()], writes=[(sqt.name, 0)], tag=f"{pfx}ke0_{c}_{tl}")
            p.do("scalar", lambda e, tl=tl: e.copy(sqt[:, TC:2 * TC], bank[1][:, 0:TC]),
                 reads=[bank[1].r()], writes=[(sqt.name, 1)], tag=f"{pfx}ke1_{c}_{tl}")
            rope_apply(sqt[:], [(sqt.name, 0), (sqt.name, 1)], tl, f"{pfx}kr_{c}_{tl}_")
            transpose_tile(tl, c, f"{pfx}kt_{c}_{tl}_")
        norms_store(c, bkch, kts, sem_kt, f"{pfx}kn_{c}_")
        if stage < 4:
            continue
        p.claim(q1r.name, tag=f"{pfx}clv_{c}")
        for tl in range(4):
            for eh in range(2):
                bm = bank[4 + eh]
                for lb in range(2):
                    p.do("tensor", lambda e, lb=lb, eh=eh, c=c, tl=tl, bm=bm: e.matmul(
                        bm[:, 0:TC],
                        latnT[:, lb * T + c * TC + tl * 128:lb * T + c * TC + (tl + 1) * 128],
                        wuv[:, lb * D + eh * 512:lb * D + (eh + 1) * 512],
                        start=(lb == 0), stop=(lb == 1)),
                        reads=RLAT + [wuv.r()],
                        writes=[bm.r()], tag=f"{pfx}v_{c}_{tl}_{eh}_{lb}")
                p.do("scalar", lambda e, tl=tl, eh=eh, bm=bm: e.copy(
                    q1r[:, tl * D + eh * TC:tl * D + (eh + 1) * TC], bm[:, 0:TC]),
                    reads=[bm.r()], writes=[(q1r.name, ("v", tl, eh))],
                    tag=f"{pfx}vev_{c}_{tl}_{eh}")
            p.dma("sync",
                  vhms[:, c * TC + tl * 128:c * TC + (tl + 1) * 128, :].rearrange(
                      "h (o p) d -> p (o h) d", p=128),
                  q1r[:, tl * D:(tl + 1) * D].rearrange("p (h d) -> p h d", d=64),
                  sem_v[tl],
                  reads=[(q1r.name, ("v", tl, 0)), (q1r.name, ("v", tl, 1))],
                  tag=f"{pfx}vs_{c}_{tl}")

    if stage < 5:
        return {1: sem_x, 2: sem_qt, 3: sem_kt, 4: sem_v[3]}[min(stage, 4)]
    return emit_attn_core(ctx, pfx + "c_", big, qts, kts, vhms, bqch, bkch,
                          outw_load(ctx, pfx, outw, w["Wo"]), None,
                          xin, xout, masks4, cinv8, x_sb=ax)


L1A_DRAM_SPECS = [
    ("Wq", [1024, 1024], F32), ("Wdkv", [1024, 256], F32),
    ("Wuk", [256, 1024], F32), ("Wuv", [256, 1024], F32),
    ("Wo", [1024, 1024], F32), ("lngb", [1, 512], F32),
]


def build_l1a_standalone():
    nc = bass.Bass("TRN2", target_bir_lowering=False)
    stack = ExitStack()
    with stack:
        ctx = Ctx(nc, stack)
        xin = nc.dram_tensor("xT", [D, T], F32, kind="ExternalInput").ap()
        xout = nc.dram_tensor("yT", [D, T], F32, kind="ExternalOutput").ap()
        qts = nc.dram_tensor("l1qt", [D, T], BF16, kind="Internal").ap()
        kts = nc.dram_tensor("l1kt", [D, T], BF16, kind="Internal").ap()
        vhms = nc.dram_tensor("l1vh", [16, T, 64], BF16, kind="Internal").ap()
        w = {}
        for name, shape, dt in L1A_DRAM_SPECS:
            w[name] = nc.dram_tensor(name, shape, dt, kind="ExternalInput").ap()
        ctx.setup_consts()
        import os
        sem_out = emit_l1a(ctx, xin, xout, w, qts, kts, vhms, "ba_",
                           stage=float(os.environ.get("L1A_STAGE", "99")))
        last = sem_out.inc_ops[-1]
        ctx.p._add_wait(ctx.p.do("sync", lambda e: e.nop(), tag="fin"), last)
        ctx.p.finalize()
    return nc



def _prep_l0a(inputs):
    wqi, wki, wvi = np.split(inputs["L0_attn_in_proj_w"], 3, axis=0)
    bqi, bki, bvi = np.split(inputs["L0_attn_in_proj_b"], 3)
    return {
        "qT": np.ascontiguousarray(inputs["L0_attn_wq"].T),
        "kT": np.ascontiguousarray(inputs["L0_attn_wk"].T),
        "vT": np.ascontiguousarray(inputs["L0_attn_wv"].T),
        "qiT": np.ascontiguousarray(wqi.T), "kiT": np.ascontiguousarray(wki.T),
        "viT": np.ascontiguousarray(wvi.T),
        "bqi": np.ascontiguousarray(bqi.reshape(8, 128)),
        "bki": np.ascontiguousarray(bki.reshape(8, 128)),
        "bvi": np.ascontiguousarray(bvi.reshape(1, 1024)),
        "outwT": np.ascontiguousarray(inputs["L0_attn_out_w"].T),
        "outb": np.ascontiguousarray(inputs["L0_attn_out_b"].reshape(8, 128)),
    }


def _prep_l0f(inputs):
    return {
        "rw17": np.ascontiguousarray(np.concatenate(
            [inputs["L0_ffn_router_w"].T, inputs["L0_ffn_shared_gate_w"]], axis=1)),
        "Wg": inputs["L0_ffn_Wg"], "Wu": inputs["L0_ffn_Wu"], "Wd": inputs["L0_ffn_Wd"],
        "Sg": inputs["L0_ffn_Sg"], "Su": inputs["L0_ffn_Su"], "Sd": inputs["L0_ffn_Sd"],
    }


def _prep_l1a(inputs):
    return {
        "Wq": inputs["L1_attn_W_q"], "Wdkv": inputs["L1_attn_W_dkv"],
        "Wuk": inputs["L1_attn_W_uk"], "Wuv": inputs["L1_attn_W_uv"],
        "Wo": inputs["L1_attn_W_o"],
        "lngb": np.ascontiguousarray(np.concatenate(
            [inputs["L1_attn_ln_gamma"], inputs["L1_attn_ln_beta"]]).reshape(1, 512)),
    }


def _prep_l1f(inputs):
    return {
        "WgT": np.ascontiguousarray(inputs["L1_ffn_Wg"].T),
        "bg": np.ascontiguousarray(inputs["L1_ffn_bg"].reshape(1, 8)),
        "W1T": np.ascontiguousarray(inputs["L1_ffn_W1"].transpose(0, 2, 1)),
        "W2T": np.ascontiguousarray(inputs["L1_ffn_W2"].transpose(0, 2, 1)),
        "b1": np.ascontiguousarray(inputs["L1_ffn_b1"].reshape(8, 32, 128)),
        "b2": np.ascontiguousarray(inputs["L1_ffn_b2"].reshape(8, 1, 1024)),
        "gamma": np.ascontiguousarray(inputs["L1_ffn_gamma"].reshape(8, 128)),
        "beta": np.ascontiguousarray(inputs["L1_ffn_beta"].reshape(8, 128)),
    }


def _prep_l2a(inputs):
    return {
        "wq": np.ascontiguousarray(inputs["L2_attn_W_q"].T),
        "wk": np.ascontiguousarray(inputs["L2_attn_W_k"].T),
        "wv": np.ascontiguousarray(inputs["L2_attn_W_v"].T),
    }


def _prep_qmlp(L, inputs):
    pf = f"L{L}_ffn_"
    return {
        "fc1_q": np.ascontiguousarray(inputs[pf + "fc1_q"].astype(np.int32)),
        "fc1_scales": inputs[pf + "fc1_scales"],
        "fc1_bias": np.ascontiguousarray(inputs[pf + "fc1_bias"].reshape(32, 128)),
        "fc2_q": np.ascontiguousarray(inputs[pf + "fc2_q"].astype(np.int32)),
        "fc2_scales": inputs[pf + "fc2_scales"],
        "fc2_bias": np.ascontiguousarray(inputs[pf + "fc2_bias"].reshape(8, 128)),
        "gamma": np.ascontiguousarray(inputs[pf + "ln_gamma"].reshape(8, 128)),
        "beta": np.ascontiguousarray(inputs[pf + "ln_beta"].reshape(8, 128)),
    }


def _prep_l3a(inputs):
    return {
        "Wk": inputs["L3_attn_Wk"], "Wv": inputs["L3_attn_Wv"],
        "Wo": inputs["L3_attn_Wo"],
        "freqs": np.ascontiguousarray(inputs["L3_attn_freqs"].reshape(1, 1024)),
        "gate": np.ascontiguousarray(inputs["L3_attn_gate"].reshape(8, 2, 2)),
    }


_PHASES = [
    ("l0a", build_l0a_standalone, _prep_l0a),
    ("l0f", build_l0f_standalone, _prep_l0f),
    ("l1a", build_l1a_standalone, _prep_l1a),
    ("l1f", build_l1f_standalone, _prep_l1f),
    ("l2a", build_l2attn_standalone, _prep_l2a),
    ("l2f", build_qmlp_standalone, lambda i: _prep_qmlp(2, i)),
    ("l3a", build_l3a_standalone, _prep_l3a),
    ("l3f", build_qmlp_standalone, lambda i: _prep_qmlp(3, i)),
]

_NC_CACHE = {}

_FUSED_SPECS = {
    "l0a": L0A_DRAM_SPECS, "l0f": L0F_DRAM_SPECS, "l1a": L1A_DRAM_SPECS,
    "l1f": L1F_DRAM_SPECS, "l2a": L2A_DRAM_SPECS,
    "l2f": QMLP_DRAM_SPECS, "l3a": L3A_DRAM_SPECS, "l3f": QMLP_DRAM_SPECS,
}


def build_fused(nphase=8, pstart=0):
    nc = bass.Bass("TRN2", target_bir_lowering=False)
    stack = ExitStack()
    with stack:
        ctx = Ctx(nc, stack)
        p = ctx.p
        xT = nc.dram_tensor("xT", [D, T], F32, kind="ExternalInput").ap()
        yT = nc.dram_tensor("yT", [D, T], F32, kind="ExternalOutput").ap()
        x0 = nc.dram_tensor("x0", [D, T], F32, kind="Internal").ap()
        x1 = nc.dram_tensor("x1", [D, T], F32, kind="Internal").ap()
        qts = nc.dram_tensor("qts", [D, T], BF16, kind="Internal").ap()
        kts = nc.dram_tensor("kts", [D, T], BF16, kind="Internal").ap()
        vhms = nc.dram_tensor("vhms", [16, T, 64], BF16, kind="Internal").ap()
        l3st = nc.dram_tensor("l3st", [D, T], BF16, kind="Internal").ap()
        l1y = nc.dram_tensor("l1y", [D, T], F32, kind="Internal").ap()
        l1xn = nc.dram_tensor("l1xn", [D, T], BF16, kind="Internal").ap()
        wmaps = {}
        for ph in ("l0a", "l0f", "l1a", "l1f", "l2a", "l2f", "l3a", "l3f"):
            wmaps[ph] = {}
            for name, shape, dt in _FUSED_SPECS[ph]:
                wmaps[ph][name] = nc.dram_tensor(
                    f"{ph}_{name}", shape, dt, kind="ExternalInput").ap()
        ctx.setup_consts()
        LOG1E4 = float(np.log(10000.0))
        for v in (0.0, 1.0, 0.5, 0.125, 2 * np.pi, 1e-5,
                  2.0 * LOG1E4 / 1024.0, -2.0 * LOG1E4 / 1024.0, -LOG1E4 / 32.0):
            ctx.constcol(v)
        seq = [
            ("l0a", lambda i, o: emit_l0a(ctx, i, o, wmaps["l0a"], qts, kts, vhms, "l0a_")),
            ("l0f", lambda i, o: emit_l0f(ctx, i, o, wmaps["l0f"], "l0f_")),
            ("l1a", lambda i, o: emit_l1a(ctx, i, o, wmaps["l1a"], qts, kts, vhms, "l1a_")),
            ("l1f", lambda i, o: emit_l1f(ctx, i, o, wmaps["l1f"], l1y, l1xn, "l1f_")),
            ("l2a", lambda i, o: emit_l2attn(ctx, i, o, wmaps["l2a"], "l2a_")),
            ("l2f", lambda i, o: emit_qmlp(ctx, i, o, wmaps["l2f"], "l2f_")),
            ("l3a", lambda i, o: emit_l3a(ctx, i, o, wmaps["l3a"], l3st, "l3a_")),
            ("l3f", lambda i, o: emit_qmlp(ctx, i, o, wmaps["l3f"], "l3f_")),
        ]
        cur = xT
        import os as _os
        nobar = _os.environ.get("NOBAR", "") == "1"
        seq = seq[pstart:pstart + nphase]
        for i, (phn, emitfn) in enumerate(seq):
            out = yT if i == len(seq) - 1 else (x0 if i % 2 == 0 else x1)
            ctx.push_phase()
            sem_last = emitfn(cur, out)
            ctx.pop_phase()
            if nobar:
                last = sem_last.inc_ops[-1]
                ctx.p._add_wait(ctx.p.do("sync", lambda e: e.nop(), tag="fin"), last)
            else:
                p.barrier(f"bar{i}")
            cur = out
        ctx.p.finalize()
    return nc


def kernel(**inputs):
    x = np.asarray(inputs["L0_attn_x"], dtype=np.float32)
    if "fused" not in _NC_CACHE:
        _NC_CACHE["fused"] = build_fused()
    nc = _NC_CACHE["fused"]
    wall = {}
    for (name, builder, prep), ph in zip(_PHASES, ("l0a", "l0f", "l1a", "l1f",
                                                   "l2a", "l2f", "l3a", "l3f")):
        for k, v in prep(inputs).items():
            wall[f"{ph}_{k}"] = v
    in_maps = [dict(wall, xT=np.ascontiguousarray(x[c].T)) for c in range(NCORE)]
    res = run_bass_kernel_spmd(nc, in_maps, core_ids=list(range(NCORE)))
    return np.stack([np.asarray(res.results[c]["yT"]).T
                     for c in range(NCORE)]).astype(np.float32)
